# version 07
# baseline (speedup 1.0000x reference)
import contextlib
import numpy as np
import ml_dtypes
import concourse.bass as bass
import concourse.mybir as mybir
from concourse.bass_utils import run_bass_kernel_spmd

F32 = mybir.dt.float32
BF16 = mybir.dt.bfloat16
AX = mybir.AxisListType.X
AF = mybir.ActivationFunctionType
ALU = mybir.AluOpType

T = 2048
NC = 8
NT = 512
NH = 16
HD = 64
NP = 8
KT = 16
LN_EPS = 1e-5
RMS_EPS = 1e-6
BIAS_SHIFT = 30.0


class Sem:
    def __init__(self, h, owner):
        self.h, self.owner, self.n = h, owner, 0


class Plan:
    ENGS = ("sync", "gpsimd", "scalar", "vector", "tensor")

    def __init__(self):
        self.ops = {e: [] for e in self.ENGS}
        self.last_wait = {e: {} for e in self.ENGS}

    def step(self, eng, emitfn, waits=(), inc=None, inc_amt=1):
        lw = self.last_wait[eng]
        ws = []
        for s, n in waits:
            if s.owner == eng or n <= 0:
                continue
            if lw.get(id(s), -1) >= n:
                continue
            assert n <= s.n, f"wait on future count {n} > {s.n} ({eng})"
            lw[id(s)] = n
            ws.append((s, n))
        if inc is not None:
            inc.n += inc_amt

        def f(e):
            for s, n in ws:
                e.wait_ge(s.h, n)
            ins = emitfn(e)
            if inc is not None:
                ins.then_inc(inc.h, inc_amt)

        self.ops[eng].append(f)


def build_program(dbg=frozenset()):
    nc = bass.Bass("TRN2", target_bir_lowering=False)

    x_in = nc.dram_tensor("x_in", [128, NC, T], F32, kind="ExternalInput").ap()
    w_qkv = nc.dram_tensor("w_qkv", [128, NC, 3072], F32, kind="ExternalInput").ap()
    w_o = nc.dram_tensor("w_o", [128, NC, 1024], F32, kind="ExternalInput").ap()
    w_r = nc.dram_tensor("w_r", [128, NC, 1024], F32, kind="ExternalInput").ap()
    w_k2 = nc.dram_tensor("w_k2", [128, NC, 1024], F32, kind="ExternalInput").ap()
    w_v2 = nc.dram_tensor("w_v2", [128, NC, 1024], F32, kind="ExternalInput").ap()
    b_qkv = nc.dram_tensor("b_qkv", [128, 24], F32, kind="ExternalInput").ap()
    b_vrow = nc.dram_tensor("b_vrow", [1, 1024], F32, kind="ExternalInput").ap()
    b_o = nc.dram_tensor("b_o", [128, 8], F32, kind="ExternalInput").ap()
    mu_r_c = nc.dram_tensor("mu_r_c", [128, 8], F32, kind="ExternalInput").ap()
    mu_k_c = nc.dram_tensor("mu_k_c", [128, 8], F32, kind="ExternalInput").ap()
    wq_col = nc.dram_tensor("wq_col", [128, 1], F32, kind="ExternalInput").ap()
    wk_col = nc.dram_tensor("wk_col", [128, 1], F32, kind="ExternalInput").ap()
    cst_ln = nc.dram_tensor("cst_ln", [128, 4], BF16, kind="ExternalInput").ap()
    cst_b2 = nc.dram_tensor("cst_b2", [2, 128], BF16, kind="ExternalInput").ap()
    cst_b2b = nc.dram_tensor("cst_b2b", [2, 128], BF16, kind="ExternalInput").ap()
    cst_i4r = nc.dram_tensor("cst_i4r", [4, 128], BF16, kind="ExternalInput").ap()
    cst_i4n = nc.dram_tensor("cst_i4n", [4, 128], BF16, kind="ExternalInput").ap()
    cst_ind = nc.dram_tensor("cst_ind", [128, 4], BF16, kind="ExternalInput").ap()

    y_out = nc.dram_tensor("y", [128, NC, T], F32, kind="ExternalOutput").ap()
    dbg_t = {}
    for name, shape, dt in [
        ("dbg_ln1", [128, NC * T], BF16),
        ("dbg_q", [128, T], BF16),
        ("dbg_sclq", [128, T], BF16),
        ("dbg_pt", [128, T], BF16),
        ("dbg_zraw", [65, T], BF16),
        ("dbg_zbuf", [128, NC * T], BF16),
        ("dbg_ao", [128, NC * T], BF16),
        ("dbg_vp", [128, NH * 65], BF16),
        ("dbg_ln2", [128, NC * 1024], BF16),
        ("dbg_xr", [128, NC * 1024], BF16),
        ("dbg_r", [128, NC * 1024], BF16),
        ("dbg_ka", [128, NC * 1024], BF16),
        ("dbg_bcol", [128, 6], F32),
        ("dbg_qr", [128, T], BF16),
        ("dbg_rows", [128, 2 * T], BF16),
        ("dbg_lhq", [128, 4], BF16),
    ]:
        if name in dbg:
            dbg_t[name] = nc.dram_tensor(name, shape, dt, kind="ExternalOutput").ap()

    P = Plan()
    with contextlib.ExitStack() as st:
        E = st.enter_context
        A = E(nc.sbuf_tensor("A", [128, NC * T], BF16))
        B = E(nc.sbuf_tensor("B", [128, NC * T], BF16))
        C = E(nc.sbuf_tensor("C", [128, KT * NH * 65], BF16))
        D = E(nc.sbuf_tensor("D", [128, NC * T], BF16))
        Ebuf = E(nc.sbuf_tensor("Ebuf", [128, 2 * T], BF16))
        Fb = E(nc.sbuf_tensor("Fb", [128, 4 * T], BF16))
        G = E(nc.sbuf_tensor("G", [128, 4 * T], BF16))
        ROWS = E(nc.sbuf_tensor("ROWS", [128, 2 * T], BF16))
        XS = E(nc.sbuf_tensor("XS", [128, T], F32))
        YST = E(nc.sbuf_tensor("YST", [128, T], F32))
        WS = E(nc.sbuf_tensor("WS", [128, 2 * 1024], F32))
        WB = E(nc.sbuf_tensor("WB", [128, 2 * 1024], BF16))
        bqc = E(nc.sbuf_tensor("bqc", [128, 24], F32))
        boc = E(nc.sbuf_tensor("boc", [128, 8], F32))
        murc = E(nc.sbuf_tensor("murc", [128, 8], F32))
        mukc = E(nc.sbuf_tensor("mukc", [128, 8], F32))
        omrc = E(nc.sbuf_tensor("omrc", [128, 8], F32))
        omkc = E(nc.sbuf_tensor("omkc", [128, 8], F32))
        wqc = E(nc.sbuf_tensor("wqc", [128, 1], F32))
        wkc = E(nc.sbuf_tensor("wkc", [128, 1], F32))
        lnT = E(nc.sbuf_tensor("lnT", [128, 4], BF16))
        lhq = E(nc.sbuf_tensor("lhq", [128, 4], BF16))
        lhk = E(nc.sbuf_tensor("lhk", [128, 4], BF16))
        b2 = E(nc.sbuf_tensor("b2", [2, 128], BF16))
        b2b = E(nc.sbuf_tensor("b2b", [2, 128], BF16))
        i4r = E(nc.sbuf_tensor("i4r", [4, 128], BF16))
        i4n = E(nc.sbuf_tensor("i4n", [4, 128], BF16))
        CCOL = E(nc.sbuf_tensor("CCOL", [128, 8], F32))
        bcole = E(nc.sbuf_tensor("bcole", [128, 2], F32))
        bcolo = E(nc.sbuf_tensor("bcolo", [128, 2], F32))
        bndc = E(nc.sbuf_tensor("bndc", [128, 8], BF16))
        epsc = E(nc.sbuf_tensor("epsc", [128, 2], F32))
        ps1 = E(nc.psum_tensor("ps1", [128, T], F32))
        ps2 = E(nc.psum_tensor("ps2", [128, T], F32))
        sw = Sem(E(nc.semaphore("sw")), "sync")
        si = Sem(E(nc.semaphore("si")), "gpsimd")
        sb = Sem(E(nc.semaphore("sb")), "gpsimd")
        so = Sem(E(nc.semaphore("so")), "sync")
        tS = Sem(E(nc.semaphore("tS")), "tensor")
        vS = Sem(E(nc.semaphore("vS")), "vector")
        sS = Sem(E(nc.semaphore("sS")), "scalar")
        block = E(nc.Block())

        Vv = C[:].rearrange("p (t h e) -> p t h e", t=KT, h=NH, e=65)
        zb = D[:].rearrange("p (c t) -> p c t", c=NC)
        wbv = [WB[:, i * 1024:(i + 1) * 1024].rearrange("p (c o) -> p c o", c=NC)
               for i in range(2)]
        Ach = [A[:, c * T:(c + 1) * T] for c in range(NC)]
        f = lambda i: Fb[:, i * T:(i + 1) * T]
        g = lambda i: G[:, i * T:(i + 1) * T]
        e0 = Ebuf[:, 0:T]
        e1 = Ebuf[:, T:2 * T]
        ptb = lambda p: Ebuf[:, p * T:(p + 1) * T]
        ptE = lambda k: Ebuf[:, (k % 4) * NT:(k % 4 + 1) * NT]
        ptO = lambda k: Ebuf[:, T + (k % 4) * NT:T + (k % 4 + 1) * NT]
        VBT = G[:, 0:1024]

        def dump(name, ap, waits):
            if name in dbg_t:
                tgt = dbg_t[name]
                P.step("sync", lambda e, a=ap, t=tgt: e.dma_start(out=t, in_=a),
                       waits, so, 16)

        ws_cast = [0, 0]
        wb_read = [(tS, 0), (tS, 0)]
        wld = {}

        def load_wblock(key, src, slot):
            P.step("sync", lambda e, src=src, slot=slot: e.dma_start(
                out=WS[:, slot * 1024:(slot + 1) * 1024], in_=src),
                [(si, ws_cast[slot])], sw, 16)
            P.step("gpsimd", lambda e, slot=slot: e.tensor_copy(
                WB[:, slot * 1024:(slot + 1) * 1024], WS[:, slot * 1024:(slot + 1) * 1024]),
                [(sw, sw.n), wb_read[slot]], si, 1)
            wld[key] = si.n
            ws_cast[slot] = si.n

        def consume_wb(slot, sem, count):
            wb_read[slot] = (sem, count)

        for dst, src in [(bqc, b_qkv), (boc, b_o), (murc, mu_r_c), (mukc, mu_k_c),
                         (wqc, wq_col), (wkc, wk_col), (lnT, cst_ln)]:
            P.step("sync", lambda e, d=dst, s=src: e.dma_start(out=d[:], in_=s),
                   (), sw, 16)
        P.step("sync", lambda e: e.dma_start(out=lhq[:], in_=cst_ind), (), sw, 16)
        P.step("sync", lambda e: e.dma_start(out=lhk[:], in_=cst_ind), (), sw, 16)
        P.step("sync", lambda e: e.dma_start(out=b2[:], in_=cst_b2), (), sw, 16)
        P.step("sync", lambda e: e.dma_start(out=b2b[:], in_=cst_b2b), (), sw, 16)
        P.step("sync", lambda e: e.dma_start(out=i4r[:], in_=cst_i4r), (), sw, 16)
        P.step("sync", lambda e: e.dma_start(out=i4n[:], in_=cst_i4n), (), sw, 16)
        cw = sw.n
        P.step("scalar", lambda e: e.activation(lhq[0:64, 2:3], wqc[0:64, :], AF.Square),
               [(sw, cw)], sS)
        P.step("scalar", lambda e: e.activation(lhq[64:128, 3:4], wqc[64:128, :], AF.Square),
               (), sS)
        P.step("scalar", lambda e: e.activation(lhk[0:64, 2:3], wkc[0:64, :], AF.Square),
               (), sS)
        P.step("scalar", lambda e: e.activation(lhk[64:128, 3:4], wkc[64:128, :], AF.Square),
               (), sS)
        sq2 = sS.n
        P.step("vector", lambda e: e.tensor_scalar(omrc[:], murc[:], -1.0, 1.0,
                                                   ALU.mult, ALU.add), [(sw, cw)], vS)
        P.step("vector", lambda e: e.tensor_scalar(omkc[:], mukc[:], -1.0, 1.0,
                                                   ALU.mult, ALU.add), [(sw, cw)], vS)
        P.step("vector", lambda e: e.memset(Vv[:, :, :, 64:65], 1.0), (), vS)
        P.step("vector", lambda e: e.memset(bndc[:], 0.0), (), vS)
        P.step("vector", lambda e: e.memset(epsc[:, 0:1], LN_EPS), (), vS)
        P.step("vector", lambda e: e.memset(epsc[:, 1:2], RMS_EPS), (), vS)
        cdone = vS.n

        def ln_stats_apply(xch, sq_dst, inb, mm_extra=(), mm_extra2=(), ebuf_guard=(),
                           g_guard=()):
            for c in range(NC):
                P.step("vector",
                       lambda e, c=c: e.tensor_mul(sq_dst(c), xch[c], xch[c]),
                       [inb[c]], vS)
            sqd = vS.n
            for c in range(NC):
                for nt in range(4):
                    P.step("tensor",
                           lambda e, c=c, nt=nt: e.matmul(
                               ps2[0:2, nt * NT:(nt + 1) * NT], lnT[:, 0:2],
                               xch[c][:, nt * NT:(nt + 1) * NT],
                               start=(c == 0), stop=False),
                           [inb[c], (sw, cw)] + list(mm_extra), None)
            for c in range(NC):
                for nt in range(4):
                    last = (c == NC - 1) and (nt == 3)
                    P.step("tensor",
                           lambda e, c=c, nt=nt, last=last: e.matmul(
                               ps2[0:2, nt * NT:(nt + 1) * NT], lnT[:, 2:4],
                               sq_dst(c)[:, nt * NT:(nt + 1) * NT],
                               start=False, stop=last),
                           [(vS, sqd)], tS if last else None)
            mmd = tS.n
            P.step("vector", lambda e: e.tensor_copy(ROWS[0:2, 0:T], ps2[0:2, :]),
                   [(tS, mmd)], vS)
            rowd = vS.n
            for nt in range(4):
                P.step("tensor", lambda e, nt=nt: e.matmul(
                    ps1[:, nt * NT:(nt + 1) * NT], b2[:],
                    ROWS[0:2, nt * NT:(nt + 1) * NT], start=True, stop=True),
                    [(vS, rowd)] + list(mm_extra2), tS if nt == 3 else None)
            mtd = tS.n
            for nt in range(4):
                P.step("tensor", lambda e, nt=nt: e.matmul(
                    ps2[:, nt * NT:(nt + 1) * NT], b2b[:],
                    ROWS[0:2, nt * NT:(nt + 1) * NT], start=True, stop=True),
                    [(vS, rowd)], tS if nt == 3 else None)
            qtd = tS.n
            P.step("scalar", lambda e: e.activation(e0, ps1[:], AF.Square),
                   [(tS, mtd)] + list(ebuf_guard), sS)
            sqd2 = sS.n
            P.step("vector", lambda e: e.tensor_sub(e1, ps2[:], e0),
                   [(tS, qtd), (sS, sqd2)] + list(ebuf_guard), vS)
            vd = vS.n
            P.step("scalar", lambda e: e.activation(e0, e1, AF.Ln, bias=epsc[:, 0:1]),
                   [(vS, vd)], sS)
            P.step("scalar", lambda e: e.activation(g(2), e0, AF.Exp, scale=-0.5),
                   list(g_guard), sS)
            sd = sS.n
            P.step("vector", lambda e: e.tensor_mul(g(3), ps1[:], g(2)),
                   [(sS, sd)] + list(g_guard), vS)
            return {"rows": rowd, "tiles": vS.n, "ps_used": tS.n}

        xcast = []
        for c in range(NC):
            stg = XS if c % 2 == 0 else YST
            wv = [(vS, xcast[c - 2])] if c > 1 else [(vS, cdone)]
            P.step("sync", lambda e, c=c, stg=stg: e.dma_start(out=stg[:], in_=x_in[:, c, :]),
                   wv, sw, 16)
            P.step("vector", lambda e, c=c, stg=stg: e.tensor_copy(Ach[c], stg[:]),
                   [(sw, sw.n)], vS)
            xcast.append(vS.n)
        Bsq = lambda c: B[:, c * T:(c + 1) * T]
        m1 = ln_stats_apply(Ach, Bsq, [(vS, xcast[c]) for c in range(NC)])
        for c in range(NC):
            P.step("vector", lambda e, c=c: e.tensor_mul(Bsq(c), Ach[c], g(2)),
                   [(vS, m1["tiles"])], vS)
            P.step("vector", lambda e, c=c: e.tensor_sub(Ach[c], Bsq(c), g(3)), (), vS)
        ln1d = vS.n
        dump("dbg_ln1", A[:], [(vS, ln1d)])

        P.step("sync", lambda e: e.dma_start(out=XS[0:1, 0:1024], in_=b_vrow),
               [(vS, ln1d)], sw, 16)
        P.step("vector", lambda e: e.tensor_copy(ROWS[0:1, 0:1024], XS[0:1, 0:1024]),
               [(sw, sw.n), (sb, sb.n)], vS)
        P.step("gpsimd", lambda e: e.dma_start(
            out=VBT[:], in_=ROWS[0:1, 0:1024].unsqueeze(1).to_broadcast([1, 128, 1024])),
            [(vS, vS.n)], sb, 16)
        vbtd = sb.n
        wvd = []
        for c in range(NC):
            load_wblock(("wv", c), w_qkv[:, c, 2048:3072], c % 2)
            P.step("vector", lambda e, c=c: e.tensor_copy(
                D[:, c * 1024:(c + 1) * 1024],
                WB[:, (c % 2) * 1024:((c % 2) + 1) * 1024]), [(si, wld[("wv", c)])], vS)
            wvd.append(vS.n)
            consume_wb(c % 2, vS, vS.n)
        psV = [ps1[:, 0:1024], ps1[:, 1024:2048], ps2[:, 0:1024], ps2[:, 1024:2048]]
        vevac = [0, 0, 0, 0]
        for tt in range(KT):
            reg = psV[tt % 4]
            extra = [(vS, m1["tiles"])]
            for c in range(NC):
                for nt in range(2):
                    last = (c == NC - 1) and (nt == 1)
                    P.step("tensor", lambda e, tt=tt, c=c, nt=nt, reg=reg, last=last: e.matmul(
                        reg[:, nt * NT:(nt + 1) * NT],
                        A[:, c * T + tt * 128: c * T + (tt + 1) * 128],
                        D[:, c * 1024 + nt * NT: c * 1024 + (nt + 1) * NT],
                        start=(c == 0), stop=last),
                        [(vS, wvd[c]), (vS, vevac[tt % 4]), (vS, ln1d)] + extra,
                        tS if last else None)
            mmd = tS.n
            P.step("vector", lambda e, tt=tt, reg=reg: e.tensor_add(
                Vv[:, tt, :, 0:64],
                reg[:].rearrange("p (h e) -> p h e", h=NH),
                VBT[:].rearrange("p (h e) -> p h e", h=NH)),
                [(tS, mmd), (sb, vbtd)], vS)
            vevac[tt % 4] = vS.n
        dump("dbg_vp", Vv[:, 0, :, :], [(vS, vS.n)])

        qsq = B[:, 2 * T:3 * T]
        ksq = B[:, 3 * T:4 * T]
        zr = lambda p: B[0:65, p * T:(p + 1) * T]
        zrl = lambda p: B[64:65, p * T:(p + 1) * T]
        LF = B[:, 4 * T:8 * T].bitcast(F32)
        sE = lambda k: ps1[:, (k % 2) * NT:(k % 2 + 1) * NT]
        sO = lambda k: ps1[:, 1024 + (k % 2) * NT:1024 + (k % 2 + 1) * NT]
        zE = ps2[0:65, 0:NT]
        zO = ps2[0:65, NT:2 * NT]
        FA = ps2[:, 1024:2048]
        FAh = [ps2[:, 1024:1536], ps2[:, 1536:2048]]

        last_zevac = vS.n
        last_div = {0: vS.n, 1: vS.n}
        pv_slot = {}
        exp_bank = {0: sS.n, 1: sS.n}
        sub_free = [(vS, 0), (vS, 0)]
        fa_free = (vS, 0)
        bcol_n = {}
        recs_n = {}
        appd_n = {}
        rows_n = {}
        chain_out = {}

        def _sub(idx, sem, cnt):
            sub_free[idx] = (sem, cnt)

        def pk_qkv_group(p, side, nt):
            sub = nt % 2
            reg = FAh[sub]
            wkey = ("q", p) if side == 0 else ("k", p)
            fraw = f(0) if side == 0 else f(1)
            sq = qsq if side == 0 else ksq
            bias = bqc[:, p:p + 1] if side == 0 else bqc[:, 8 + p:9 + p]

            def emit():
                for c in range(NC):
                    last = c == NC - 1
                    P.step("tensor", lambda e, c=c, last=last: e.matmul(
                        reg, wbv[side][:, c, :],
                        A[:, c * T + nt * NT: c * T + (nt + 1) * NT],
                        start=(c == 0), stop=last),
                        [(si, wld[wkey]), sub_free[sub]], tS if last else None)
                mmn = tS.n
                if side == 1 and nt == 3:
                    consume_wb(1, tS, mmn)
                if side == 0 and nt == 3:
                    consume_wb(0, tS, mmn)
                P.step("vector", lambda e: e.tensor_scalar_add(
                    fraw[:, nt * NT:(nt + 1) * NT], reg, bias), [(tS, mmn)], vS)
                P.step("vector", lambda e: e.tensor_mul(
                    sq[:, nt * NT:(nt + 1) * NT], fraw[:, nt * NT:(nt + 1) * NT],
                    fraw[:, nt * NT:(nt + 1) * NT]), (), vS)
                _sub(sub, vS, vS.n)
            return emit

        def pk_stats(p, side):
            sq = qsq if side == 0 else ksq
            lh = lhq if side == 0 else lhk
            off = 0 if side == 0 else T

            def emit():
                for nt in range(4):
                    sub = nt % 2
                    P.step("tensor", lambda e, nt=nt, sub=sub: e.matmul(
                        FAh[sub][0:4, :], lh[:],
                        sq[:, nt * NT:(nt + 1) * NT], start=True, stop=True),
                        [sub_free[sub], (vS, vS.n)], tS)
                    mmn = tS.n
                    P.step("vector", lambda e, nt=nt, sub=sub: e.tensor_copy(
                        ROWS[0:4, off + nt * NT: off + (nt + 1) * NT], FAh[sub][0:4, :]),
                        [(tS, mmn)], vS)
                    _sub(sub, vS, vS.n)
                rows_n[(p, side)] = vS.n
            return emit

        def pk_chain_half(p, side, hh):
            off = 0 if side == 0 else T
            sclg = g(0) if side == 0 else g(1)
            nrm = g(2) if side == 0 else g(3)
            wcol = wqc if side == 0 else wkc
            stg = XS if side == 0 else YST

            def emit():
                for i2 in range(2):
                    P.step("tensor", lambda e, i2=i2: e.matmul(
                        FAh[i2], i4r[:],
                        ROWS[0:4, off + hh * 1024 + i2 * NT: off + hh * 1024 + (i2 + 1) * NT],
                        start=True, stop=True),
                        [sub_free[i2], (vS, rows_n[(p, side)])], tS if i2 else None)
                mtn = tS.n
                P.step("scalar", lambda e: e.activation(stg[:, 0:1024], FA, AF.Ln,
                                                        bias=epsc[:, 1:2]), [(tS, mtn)], sS)
                P.step("scalar", lambda e: e.activation(stg[:, 1024:2048], stg[:, 0:1024],
                                                        AF.Exp, scale=-0.5), (), sS)
                lnn = sS.n
                for i2 in range(2):
                    P.step("tensor", lambda e, i2=i2: e.matmul(
                        FAh[i2], i4n[:],
                        ROWS[0:4, off + hh * 1024 + i2 * NT: off + hh * 1024 + (i2 + 1) * NT],
                        start=True, stop=True),
                        [(sS, lnn)], tS if i2 else None)
                ntn = tS.n
                P.step("vector", lambda e: e.tensor_mul(
                    sclg[:, hh * 1024:(hh + 1) * 1024], stg[:, 1024:2048], FA),
                    [(tS, ntn)], vS)
                gw = [(si, si.n)] if side == 1 else []
                P.step("vector", lambda e: e.tensor_mul(
                    nrm[:, hh * 1024:(hh + 1) * 1024],
                    sclg[:, hh * 1024:(hh + 1) * 1024], stg[:, 1024:2048]), gw, vS)
                P.step("vector", lambda e: e.tensor_scalar_mul(
                    sclg[:, hh * 1024:(hh + 1) * 1024], stg[:, 1024:2048], wcol[:, 0:1]),
                    (), vS)
                _sub(0, vS, vS.n)
                _sub(1, vS, vS.n)
                chain_out[(p, side, hh)] = vS.n
            return emit

        def pk_bias(p):
            par = p % 2

            def emit():
                w0 = [(vS, chain_out[(p, 0, 0)]), (vS, chain_out[(p, 0, 1)]),
                      (vS, chain_out[(p, 1, 0)]), (vS, chain_out[(p, 1, 1)])]
                P.step("vector", lambda e: e.reduce_max(CCOL[:, 0:1], g(2), axis=AX),
                       w0, vS)
                P.step("vector", lambda e: e.reduce_max(CCOL[:, 1:2], g(3), axis=AX),
                       (), vS)
                vmax = vS.n
                P.step("scalar", lambda e: e.activation(CCOL[:, 4:6], CCOL[:, 0:2], AF.Ln),
                       [(vS, vmax)], sS)
                lnn = sS.n
                P.step("vector", lambda e: e.reduce_sum(CCOL[:, 6:7], CCOL[:, 4:6],
                                                        axis=AX), [(sS, lnn)], vS)
                vsum = vS.n
                P.step("scalar", lambda e: e.activation(CCOL[:, 2:3], CCOL[:, 6:7],
                                                        AF.Exp, scale=0.5), [(vS, vsum)], sS)
                expn = sS.n
                pb = bcol_n.get(p - 2, (0, 0))[0]
                P.step("vector", lambda e: e.tensor_scalar(CCOL[:, 3:4], CCOL[:, 2:3],
                                                           -1.0, BIAS_SHIFT,
                                                           ALU.mult, ALU.add),
                       [(sS, expn), (sb, pb)], vS)
                biasd = vS.n
                P.step("sync", lambda e, par=par: e.dma_start(
                    out=bcole[:, par:par + 1],
                    in_=CCOL[0:1, 3:4].unsqueeze(1).to_broadcast([1, 128, 1])),
                    [(vS, biasd)], sb, 16)
                P.step("sync", lambda e, par=par: e.dma_start(
                    out=bcolo[:, par:par + 1],
                    in_=CCOL[64:65, 3:4].unsqueeze(1).to_broadcast([1, 128, 1])),
                    [(vS, biasd)], sb, 16)
                bcol_n[p] = (sb.n, par)
                if p == 0 and "dbg_bcol" in dbg_t:
                    dump("dbg_bcol", CCOL[:, 0:6], [(vS, biasd)])
            return emit

        def pk_divides(jp):
            def emit():
                P.step("vector", lambda e: e.memset(G[64:65, 2 * T:2 * T + 64], 1.0),
                       [(vS, recs_n[jp])], vS)
                onesd = vS.n
                for h2 in range(2):
                    for hh in range(2):
                        for i2 in range(2):
                            P.step("tensor", lambda e, hh=hh, i2=i2, h2=h2: e.matmul(
                                FAh[i2][0:64, :], G[64:65, 2 * T:2 * T + 64],
                                ROWS[64:65, h2 * T + hh * 1024 + i2 * NT:
                                     h2 * T + hh * 1024 + (i2 + 1) * NT],
                                start=True, stop=True),
                                [(vS, onesd), sub_free[i2]], tS if i2 else None)
                        linn = tS.n
                        if h2 == 0:
                            P.step("vector", lambda e, jp=jp, hh=hh: e.tensor_mul(
                                zb[0:64, jp, hh * 1024:(hh + 1) * 1024],
                                zr(0)[0:64, hh * 1024:(hh + 1) * 1024], FA[0:64, :]),
                                [(tS, linn)], vS)
                            last_div[0] = vS.n
                        else:
                            P.step("vector", lambda e, jp=jp, hh=hh: e.tensor_mul(
                                g(3)[0:64, hh * 1024:(hh + 1) * 1024],
                                zr(1)[0:64, hh * 1024:(hh + 1) * 1024], FA[0:64, :]),
                                [(tS, linn), (si, si.n)], vS)
                            last_div[1] = vS.n
                        _sub(0, vS, vS.n)
                        _sub(1, vS, vS.n)
                P.step("gpsimd", lambda e, jp=jp: e.dma_start(
                    out=zb[64:128, jp, :], in_=g(3)[0:64, :]),
                    [(vS, last_div[1])], si, 16)
            return emit

        def emit_attention(j, packets):
            nonlocal last_zevac
            pki = [0]

            def pop(n=1):
                for _ in range(n):
                    if pki[0] < len(packets):
                        packets[pki[0]]()
                        pki[0] += 1

            par = bcol_n[j][1]
            bcE = bcole[:, par:par + 1]
            bcO = bcolo[:, par:par + 1]
            hE, hO = 2 * j, 2 * j + 1
            zfree = {0: (vS, last_zevac), 1: (vS, last_zevac)}
            expn = {}
            for qq in range(4):
                for kt in range(KT):
                    swl = [(sS, exp_bank[kt % 2]), (vS, appd_n[j]),
                           (sb, bcol_n[j][0])]
                    P.step("tensor", lambda e, kt=kt, qq=qq: e.matmul(
                        sE(kt), f(3)[0:64, kt * 128:(kt + 1) * 128],
                        f(2)[0:64, qq * NT:(qq + 1) * NT],
                        start=True, stop=True, tile_position=(0, 0)), swl, None)
                    P.step("tensor", lambda e, kt=kt, qq=qq: e.matmul(
                        sO(kt), f(3)[64:128, kt * 128:(kt + 1) * 128],
                        f(2)[64:128, qq * NT:(qq + 1) * NT],
                        start=True, stop=True, tile_position=(64, 0)), (), tS)
                    smm = tS.n
                    ew = [(tS, smm)]
                    if (0, kt % 4) in pv_slot:
                        ew.append((tS, pv_slot[(0, kt % 4)]))
                    P.step("scalar", lambda e, kt=kt, bcE=bcE: e.activation(
                        ptE(kt), sE(kt), AF.Exp, bias=bcE), ew, sS)
                    ew2 = []
                    if (1, kt % 4) in pv_slot:
                        ew2.append((tS, pv_slot[(1, kt % 4)]))
                    P.step("scalar", lambda e, kt=kt, bcO=bcO: e.activation(
                        ptO(kt), sO(kt), AF.Exp, bias=bcO), ew2, sS)
                    exp_bank[kt % 2] = sS.n
                    expn[kt] = sS.n
                    if kt > 1:
                        kp = kt - 2
                        pwE = [(sS, expn[kp])]
                        if kp == 0:
                            pwE.append(zfree[0])
                            pwE.append(zfree[1])
                        P.step("tensor", lambda e, kp=kp, qq=qq, hE=hE: e.matmul(
                            zE, Vv[:, kp, hE, :], ptE(kp),
                            start=(kp == 0), stop=False), pwE, None)
                        P.step("tensor", lambda e, kp=kp, qq=qq, hO=hO: e.matmul(
                            zO, Vv[:, kp, hO, :], ptO(kp),
                            start=(kp == 0), stop=False), (), tS)
                        pv_slot[(0, kp % 4)] = tS.n
                        pv_slot[(1, kp % 4)] = tS.n
                    if kt % 4 == 1:
                        pop()
                for kp in (KT - 2, KT - 1):
                    P.step("tensor", lambda e, kp=kp, qq=qq, hE=hE: e.matmul(
                        zE, Vv[:, kp, hE, :], ptE(kp),
                        start=False, stop=(kp == KT - 1)), [(sS, expn[kp])], None)
                    P.step("tensor", lambda e, kp=kp, qq=qq, hO=hO: e.matmul(
                        zO, Vv[:, kp, hO, :], ptO(kp),
                        start=False, stop=(kp == KT - 1)), (), tS)
                    pv_slot[(0, kp % 4)] = tS.n
                    pv_slot[(1, kp % 4)] = tS.n
                pvall = tS.n
                P.step("vector", lambda e, qq=qq: e.tensor_copy(
                    zr(0)[:, qq * NT:(qq + 1) * NT], zE),
                    [(tS, pvall), (vS, last_div[0])], vS)
                zfree[0] = (vS, vS.n)
                P.step("vector", lambda e, qq=qq: e.tensor_copy(
                    zr(1)[:, qq * NT:(qq + 1) * NT], zO),
                    [(vS, last_div[1])], vS)
                zfree[1] = (vS, vS.n)
                last_zevac = vS.n

                P.step("vector", lambda e, qq=qq: e.tensor_copy(
                    LF[64:65, qq * NT:(qq + 1) * NT], ps2[64:65, 0:NT]), (), vS)
                P.step("vector", lambda e, qq=qq: e.tensor_copy(
                    LF[64:65, 2048 + qq * NT: 2048 + (qq + 1) * NT],
                    ps2[64:65, NT:2 * NT]), (), vS)
                zfree[0] = (vS, vS.n)
                zfree[1] = (vS, vS.n)
            if j == 0:
                dump("dbg_zraw", zr(0), [(vS, last_zevac)])

            pop(len(packets))

        load_wblock(("q", 0), w_qkv[:, :, 0:128], 0)
        load_wblock(("k", 0), w_qkv[:, :, 1024:1152], 1)
        sub_free = [(vS, vevac[3]), (vS, vevac[3])]
        for side in range(2):
            for nt in range(4):
                pk_qkv_group(0, side, nt)()
            pk_stats(0, side)()
        load_wblock(("q", 1), w_qkv[:, :, 128:256], 0)
        load_wblock(("k", 1), w_qkv[:, :, 1024 + 128:1024 + 256], 1)
        for side in range(2):
            for hh in range(2):
                pk_chain_half(0, side, hh)()
        pk_bias(0)()
        if "dbg_qr" in dbg_t:
            dump("dbg_qr", f(0), [(vS, vS.n)])
            dump("dbg_lhq", lhq[:], [(vS, cdone)])
            dump("dbg_rows", ROWS[:], [(vS, rows_n[(0, 1)])])
        P.step("vector", lambda e: e.tensor_mul(f(2), f(0), g(0)),
               [(vS, chain_out[(0, 0, 1)]), (vS, chain_out[(0, 1, 1)])], vS)
        P.step("vector", lambda e: e.tensor_mul(f(3), f(1), g(1)), (), vS)
        appd_n[0] = vS.n
        if "dbg_q" in dbg_t:
            dump("dbg_q", f(2), [(vS, appd_n[0])])
            dump("dbg_sclq", g(0), [(vS, appd_n[0])])

        for j in range(NP):
            packets = []
            if j > 0:
                packets.append(pk_divides(j - 1))
            if j + 1 < NP:
                p = j + 1
                for side in range(2):
                    for nt in range(4):
                        packets.append(pk_qkv_group(p, side, nt))
                    packets.append(pk_stats(p, side))
                for side in range(2):
                    for hh in range(2):
                        packets.append(pk_chain_half(p, side, hh))
                packets.append(pk_bias(p))
            emit_attention(j, packets)

            def emit_recips(j=j):
                for s8 in range(8):
                    def _rc(e, s8=s8):
                        with nc.allow_low_precision(reason="softmax denom to bf16"):
                            return e.reciprocal(
                                ROWS[64:65, s8 * NT:(s8 + 1) * NT],
                                LF[64:65, s8 * NT:(s8 + 1) * NT])
                    P.step("vector", _rc, (), vS)
                recs_n[j] = vS.n
            if j + 1 < NP:
                if j + 2 < NP:
                    load_wblock(("q", j + 2), w_qkv[:, :, (j + 2) * 128:(j + 3) * 128], 0)
                    load_wblock(("k", j + 2),
                                w_qkv[:, :, 1024 + (j + 2) * 128:1024 + (j + 3) * 128], 1)
                else:
                    load_wblock(("o", 0), w_o[:, :, 0:128], 0)
                    load_wblock(("o", 1), w_o[:, :, 128:256], 1)
                lastS = tS.n
                P.step("vector", lambda e: e.tensor_mul(f(2), f(0), g(0)),
                       [(vS, chain_out[(j + 1, 0, 1)]), (vS, chain_out[(j + 1, 1, 1)]),
                        (tS, lastS)], vS)
                P.step("vector", lambda e: e.tensor_mul(f(3), f(1), g(1)), (), vS)
                appd_n[j + 1] = vS.n
                emit_recips()
            else:
                emit_recips()

        pk_divides(NP - 1)()
        zdone_v = vS.n
        zdone_i = si.n
        pv_final = tS.n
        dump("dbg_zbuf", D[:], [(vS, zdone_v), (si, zdone_i)])

        psW = [ps1, ps2]
        woev = [sS.n, sS.n]
        for o in range(NC):
            reg = psW[o % 2]
            basew = [(si, wld[("o", o)]), (vS, zdone_v), (si, zdone_i),
                     (sS, woev[o % 2])]
            if o < 2:
                basew += [(sS, exp_bank[0]), (sS, exp_bank[1])]
            for c in range(NC):
                for nt in range(4):
                    last = (c == NC - 1) and (nt == 3)
                    P.step("tensor", lambda e, o=o, c=c, nt=nt, reg=reg, last=last: e.matmul(
                        reg[:, nt * NT:(nt + 1) * NT], wbv[o % 2][:, c, :],
                        zb[:, c, nt * NT:(nt + 1) * NT], start=(c == 0), stop=last),
                        basew, tS if last else None)
            mmd = tS.n
            consume_wb(o % 2, tS, mmd)
            if o + 2 < NC:
                load_wblock(("o", o + 2), w_o[:, :, (o + 2) * 128:(o + 3) * 128], o % 2)
            elif o + 2 == NC:
                load_wblock(("f", 0, 0, 0), w_r[:, :, 0:128], 0)
            elif o + 2 == NC + 1:
                load_wblock(("f", 0, 0, 1), w_r[:, :, 128:256], 1)
            P.step("scalar", lambda e, o=o, reg=reg: e.activation(
                B[:, o * T:(o + 1) * T], reg[:], AF.Identity, bias=boc[:, o:o + 1]),
                [(tS, mmd)], sS)
            woev[o % 2] = sS.n
        aod = sS.n
        dump("dbg_ao", B[:], [(sS, aod)])

        x1c = []
        for c in range(NC):
            wv = [(vS, x1c[c - 1])] if c > 0 else [(sS, aod)]
            P.step("sync", lambda e, c=c: e.dma_start(out=XS[:], in_=x_in[:, c, :]),
                   wv, sw, 16)
            P.step("vector", lambda e, c=c: e.tensor_add(Ach[c], XS[:],
                                                         B[:, c * T:(c + 1) * T]),
                   [(sw, sw.n), (sS, aod)], vS)
            x1c.append(vS.n)
        Dsq = lambda c: D[:, c * T:(c + 1) * T]
        m2 = ln_stats_apply(Ach, Dsq, [(vS, x1c[c]) for c in range(NC)],
                            mm_extra=[(sS, aod)],
                            mm_extra2=[(sS, aod)],
                            ebuf_guard=[(tS, pv_final)],
                            g_guard=[(vS, zdone_v), (si, zdone_i)])
        ln2tiles = m2["tiles"]

        ln2v = D[:, 0:8192].rearrange("p (c t) -> p c t", c=NC, t=1024)
        xrv = D[:, 8192:16384].rearrange("p (c t) -> p c t", c=NC, t=1024)
        xkv = C[:, 0:8192].rearrange("p (c t) -> p c t", c=NC, t=1024)
        rv = C[:, 8192:16384].rearrange("p (c t) -> p c t", c=NC, t=1024)
        kav = Fb[:, 0:8192].rearrange("p (c t) -> p c t", c=NC, t=1024)
        psF = [ps1[:, 0:1024], ps1[:, 1024:2048], ps2[:, 0:1024], ps2[:, 1024:2048]]
        ffev = [(vS.n, sS.n)] * 4
        ydma_slot = [0, 0]
        xsfree = vS.n
        revac = {}
        prev_mix = vS.n
        for hh in range(2):
            t0 = hh * 1024
            apl = []
            for c in range(NC):
                aw = [(vS, ln2tiles), (vS, ffev[0][0]), (vS, ffev[1][0]),
                      (vS, ffev[2][0]), (vS, ffev[3][0]), (vS, prev_mix)]
                P.step("vector", lambda e, c=c, t0=t0: e.tensor_mul(
                    VBT[:], Ach[c][:, t0:t0 + 1024],
                    G[:, 2 * T + t0: 2 * T + t0 + 1024]), aw, vS)
                P.step("vector", lambda e, c=c, t0=t0: e.tensor_sub(
                    ln2v[:, c, :], VBT[:], G[:, 3 * T + t0: 3 * T + t0 + 1024]),
                    (), vS)
                apl.append(vS.n)
            if hh == 0:
                dump("dbg_ln2", D[:, 0:8192], [(vS, vS.n)])
            mixd = []
            for c in range(NC):
                for (xv, mu, om) in [(xrv, murc, omrc), (xkv, mukc, omkc)]:
                    P.step("vector", lambda e, c=c, mu=mu: e.tensor_scalar_mul(
                        VBT[:], ln2v[:, c, :], mu[:, c:c + 1]), [(vS, apl[c])], vS)
                    P.step("vector", lambda e, c=c, xv=xv, om=om: e.scalar_tensor_tensor(
                        xv[:, c, 1:1024], ln2v[:, c, 0:1023], om[:, c:c + 1],
                        VBT[:, 1:1024], ALU.mult, ALU.add), (), vS)
                    P.step("vector", lambda e, c=c, xv=xv, om=om: e.scalar_tensor_tensor(
                        xv[:, c, 0:1], bndc[:, c:c + 1], om[:, c:c + 1],
                        VBT[:, 0:1], ALU.mult, ALU.add), (), vS)
                mixd.append(vS.n)
            prev_mix = vS.n
            if hh == 0:
                P.step("vector", lambda e: e.tensor_copy(
                    bndc[:], ln2v[:, :, 1023:1024].rearrange("p c t -> p (c t)")),
                    (), vS)
                prev_mix = vS.n
                dump("dbg_xr", D[:, 8192:16384], [(vS, mixd[-1])])
            matsrc = {0: w_r, 1: w_k2, 2: w_v2}

            def next_block(hh, mat, o):
                if o + 1 < NC:
                    return (hh, mat, o + 1)
                if mat + 1 <= 2:
                    return (hh, mat + 1, 0)
                if hh == 0:
                    return (1, 0, 0)
                return None

            def prefetch_after(hh, mat, o):
                nb = next_block(hh, mat, o)
                if nb is None:
                    return
                nb2 = next_block(*nb)
                if nb2 is None:
                    return
                h2_, m2_, o2_ = nb2
                if ("f", h2_, m2_, o2_) in wld:
                    return
                load_wblock(("f", h2_, m2_, o2_),
                            matsrc[m2_][:, :, o2_ * 128:(o2_ + 1) * 128], o % 2)

            if hh == 0:
                pass
            for mat, (wsrc, xv, kind) in enumerate([(w_r, xrv, "r"), (w_k2, xkv, "k")]):
                for o in range(NC):
                    slot = o % 2
                    reg = psF[2 * mat + (o % 2)]
                    ri = 2 * mat + (o % 2)
                    basew = [(si, wld[("f", hh, mat, o)]), (vS, ffev[ri][0]),
                             (sS, ffev[ri][1]), (vS, mixd[-1])]
                    if hh == 0 and mat == 0 and o < 2:
                        basew += [(sS, exp_bank[0]), (sS, exp_bank[1]), (sS, aod)]
                    for c in range(NC):
                        for nt in range(2):
                            last = (c == NC - 1) and (nt == 1)
                            P.step("tensor", lambda e, c=c, nt=nt, reg=reg, slot=slot,
                                   xv=xv, last=last: e.matmul(
                                       reg[:, nt * NT:(nt + 1) * NT], wbv[slot][:, c, :],
                                       xv[:, c, nt * NT:(nt + 1) * NT],
                                       start=(c == 0), stop=last),
                                   basew, tS if last else None)
                    mmd = tS.n
                    consume_wb(slot, tS, mmd)
                    prefetch_after(hh, mat, o)
                    if kind == "r":
                        P.step("scalar", lambda e, o=o, reg=reg: e.activation(
                            rv[:, o, :], reg[:], AF.Sigmoid), [(tS, mmd)], sS)
                        revac[(hh, o)] = sS.n
                        ffev[ri] = (vS.n, sS.n)
                    else:
                        P.step("vector", lambda e, o=o, reg=reg: e.tensor_scalar_max(
                            VBT[:], reg[:], 0.0), [(tS, mmd)], vS)
                        P.step("vector", lambda e, o=o: e.tensor_mul(
                            kav[:, o, :], VBT[:], VBT[:]), (), vS)
                        ffev[ri] = (vS.n, sS.n)
            kall = vS.n
            if hh == 0:
                dump("dbg_r", C[:, 8192:16384], [(sS, sS.n)])
                dump("dbg_ka", Fb[:, 0:8192], [(vS, kall)])
            for o in range(NC):
                slot = o % 2
                reg = psF[2 + (o % 2)]
                ri = 2 + (o % 2)
                basew = [(si, wld[("f", hh, 2, o)]), (vS, ffev[ri][0]),
                         (sS, ffev[ri][1]), (vS, kall)]
                for c in range(NC):
                    for nt in range(2):
                        last = (c == NC - 1) and (nt == 1)
                        P.step("tensor", lambda e, c=c, nt=nt, reg=reg, slot=slot,
                               last=last: e.matmul(
                                   reg[:, nt * NT:(nt + 1) * NT], wbv[slot][:, c, :],
                                   kav[:, c, nt * NT:(nt + 1) * NT],
                                   start=(c == 0), stop=last),
                               basew, tS if last else None)
                mmd = tS.n
                consume_wb(slot, tS, mmd)
                prefetch_after(hh, 2, o)
                P.step("sync", lambda e, o=o, t0=t0: e.dma_start(
                    out=XS[:, 0:1024], in_=x_in[:, o, t0:t0 + 1024]),
                    [(vS, xsfree)], sw, 16)
                xld = sw.n
                par = o % 2
                yreg = YST[:, par * 1024:(par + 1) * 1024]
                P.step("vector", lambda e, o=o, hh=hh, reg=reg, yreg=yreg:
                       e.scalar_tensor_tensor(yreg, reg[:], 1.0, rv[:, o, :],
                                              ALU.mult, ALU.mult),
                       [(tS, mmd), (so, ydma_slot[par]), (sS, revac[(hh, o)])], vS)
                P.step("vector", lambda e, o=o, t0=t0, yreg=yreg: e.tensor_add(
                    XS[:, 1024:2048], yreg, B[:, o * T + t0: o * T + t0 + 1024]),
                    (), vS)
                P.step("vector", lambda e, yreg=yreg: e.tensor_add(
                    yreg, XS[:, 1024:2048], XS[:, 0:1024]), [(sw, xld)], vS)
                xsfree = vS.n
                find = vS.n
                P.step("sync", lambda e, o=o, t0=t0, yreg=yreg: e.dma_start(
                    out=y_out[:, o, t0:t0 + 1024], in_=yreg), [(vS, find)], so, 16)
                ydma_slot[par] = so.n
                ffev[ri] = (vS.n, sS.n)
        P.step("sync", lambda e, n=so.n: e.wait_ge(so.h, n), (), None)

        @block.sync
        def _(e):
            for fn in P.ops["sync"]:
                fn(e)

        @block.gpsimd
        def _(e):
            for fn in P.ops["gpsimd"]:
                fn(e)

        @block.scalar
        def _(e):
            for fn in P.ops["scalar"]:
                fn(e)

        @block.vector
        def _(e):
            for fn in P.ops["vector"]:
                fn(e)

        @block.tensor
        def _(e):
            for fn in P.ops["tensor"]:
                fn(e)

    return nc


def _prep_inputs(L0_attn_x, L0_attn_Wqkv, L0_attn_bqkv, L0_attn_Wo, L0_attn_bo,
                 L0_attn_wq, L0_attn_wk, L0_ffn_W_r, L0_ffn_W_k, L0_ffn_W_v,
                 L0_ffn_mu_r, L0_ffn_mu_k):
    f32 = np.float32
    bf16 = ml_dtypes.bfloat16
    x = np.asarray(L0_attn_x, f32)

    def fm(w):
        wt = np.ascontiguousarray(np.asarray(w, f32).T)
        return np.ascontiguousarray(wt.reshape(8, 128, -1).transpose(1, 0, 2))

    wqkv = fm(L0_attn_Wqkv)
    wo = fm(L0_attn_Wo)
    wr = fm(L0_ffn_W_r)
    wk2 = fm(L0_ffn_W_k)
    wv2 = fm(L0_ffn_W_v)
    bqkv = np.asarray(L0_attn_bqkv, f32)
    b_qkv = np.ascontiguousarray(bqkv.reshape(24, 128).T)
    b_vrow = np.ascontiguousarray(bqkv[2048:3072].reshape(1, 1024))
    b_o = np.ascontiguousarray(np.asarray(L0_attn_bo, f32).reshape(8, 128).T)
    mu_r = np.ascontiguousarray(np.asarray(L0_ffn_mu_r, f32).reshape(8, 128).T)
    mu_k = np.ascontiguousarray(np.asarray(L0_ffn_mu_k, f32).reshape(8, 128).T)
    wq_col = np.ascontiguousarray(np.tile(np.asarray(L0_attn_wq, f32), 2).reshape(128, 1))
    wk_col = np.ascontiguousarray(np.tile(np.asarray(L0_attn_wk, f32), 2).reshape(128, 1))
    cst_ln = np.zeros((128, 4), bf16)
    cst_ln[:, 0] = bf16(1.0 / 1024.0)
    cst_ln[:, 3] = bf16(1.0 / 1024.0)
    cst_ind = np.zeros((128, 4), bf16)
    cst_ind[0:64, 0] = bf16(1.0 / 64.0)
    cst_ind[64:128, 1] = bf16(1.0 / 64.0)
    cst_b2 = np.zeros((2, 128), bf16)
    cst_b2[0, :] = bf16(1.0)
    cst_b2b = np.zeros((2, 128), bf16)
    cst_b2b[1, :] = bf16(1.0)
    cst_i4r = np.zeros((4, 128), bf16)
    cst_i4r[0, 0:64] = bf16(1.0)
    cst_i4r[1, 64:128] = bf16(1.0)
    cst_i4n = np.zeros((4, 128), bf16)
    cst_i4n[2, 0:64] = bf16(1.0)
    cst_i4n[3, 64:128] = bf16(1.0)

    shared = dict(w_qkv=wqkv, w_o=wo, w_r=wr, w_k2=wk2, w_v2=wv2, b_qkv=b_qkv,
                  b_vrow=b_vrow, b_o=b_o, mu_r_c=mu_r, mu_k_c=mu_k,
                  wq_col=wq_col, wk_col=wk_col, cst_ln=cst_ln, cst_ind=cst_ind,
                  cst_b2=cst_b2, cst_b2b=cst_b2b, cst_i4r=cst_i4r, cst_i4n=cst_i4n)
    in_maps = []
    for b in range(8):
        xb = np.ascontiguousarray(x[b].T.reshape(8, 128, T).transpose(1, 0, 2))
        m = dict(shared)
        m["x_in"] = xb
        in_maps.append(m)
    return in_maps


def kernel(**inputs):
    in_maps = _prep_inputs(**inputs)
    nc = build_program()
    res = run_bass_kernel_spmd(nc, in_maps, core_ids=list(range(8)))
    outs = []
    for b in range(8):
        yb = res.results[b]["y"]
        outs.append(yb.transpose(1, 0, 2).reshape(1024, T).T)
    return np.ascontiguousarray(np.stack(outs, axis=0), dtype=np.float32)


# version 08
# speedup vs baseline: 1.0114x; 1.0114x over previous
import contextlib
import numpy as np
import ml_dtypes
import concourse.bass as bass
import concourse.mybir as mybir
from concourse.bass_utils import run_bass_kernel_spmd

F32 = mybir.dt.float32
BF16 = mybir.dt.bfloat16
AX = mybir.AxisListType.X
AF = mybir.ActivationFunctionType
ALU = mybir.AluOpType

T = 2048
NC = 8
NT = 512
NH = 16
HD = 64
NP = 8
KT = 16
LN_EPS = 1e-5
RMS_EPS = 1e-6
BIAS_SHIFT = 30.0


class Sem:
    def __init__(self, h, owner):
        self.h, self.owner, self.n = h, owner, 0


class Plan:
    ENGS = ("sync", "gpsimd", "scalar", "vector", "tensor")

    def __init__(self):
        self.ops = {e: [] for e in self.ENGS}
        self.last_wait = {e: {} for e in self.ENGS}

    def step(self, eng, emitfn, waits=(), inc=None, inc_amt=1):
        lw = self.last_wait[eng]
        ws = []
        for s, n in waits:
            if s.owner == eng or n <= 0:
                continue
            if lw.get(id(s), -1) >= n:
                continue
            assert n <= s.n, f"wait on future count {n} > {s.n} ({eng})"
            lw[id(s)] = n
            ws.append((s, n))
        if inc is not None:
            inc.n += inc_amt

        def f(e):
            for s, n in ws:
                e.wait_ge(s.h, n)
            ins = emitfn(e)
            if inc is not None:
                ins.then_inc(inc.h, inc_amt)

        self.ops[eng].append(f)


def build_program(dbg=frozenset()):
    nc = bass.Bass("TRN2", target_bir_lowering=False)

    x_in = nc.dram_tensor("x_in", [128, NC, T], F32, kind="ExternalInput").ap()
    w_qkv = nc.dram_tensor("w_qkv", [128, NC, 3072], F32, kind="ExternalInput").ap()
    w_o = nc.dram_tensor("w_o", [128, NC, 1024], F32, kind="ExternalInput").ap()
    w_r = nc.dram_tensor("w_r", [128, NC, 1024], F32, kind="ExternalInput").ap()
    w_k2 = nc.dram_tensor("w_k2", [128, NC, 1024], F32, kind="ExternalInput").ap()
    w_v2 = nc.dram_tensor("w_v2", [128, NC, 1024], F32, kind="ExternalInput").ap()
    b_qkv = nc.dram_tensor("b_qkv", [128, 24], F32, kind="ExternalInput").ap()
    b_vrow = nc.dram_tensor("b_vrow", [1, 1024], F32, kind="ExternalInput").ap()
    b_o = nc.dram_tensor("b_o", [128, 8], F32, kind="ExternalInput").ap()
    mu_r_c = nc.dram_tensor("mu_r_c", [128, 8], F32, kind="ExternalInput").ap()
    mu_k_c = nc.dram_tensor("mu_k_c", [128, 8], F32, kind="ExternalInput").ap()
    wq_col = nc.dram_tensor("wq_col", [128, 1], F32, kind="ExternalInput").ap()
    wk_col = nc.dram_tensor("wk_col", [128, 1], F32, kind="ExternalInput").ap()
    cst_ln = nc.dram_tensor("cst_ln", [128, 4], BF16, kind="ExternalInput").ap()
    cst_b2 = nc.dram_tensor("cst_b2", [2, 128], BF16, kind="ExternalInput").ap()
    cst_b2b = nc.dram_tensor("cst_b2b", [2, 128], BF16, kind="ExternalInput").ap()
    cst_i4r = nc.dram_tensor("cst_i4r", [4, 128], BF16, kind="ExternalInput").ap()
    cst_i4n = nc.dram_tensor("cst_i4n", [4, 128], BF16, kind="ExternalInput").ap()
    cst_ind = nc.dram_tensor("cst_ind", [128, 4], BF16, kind="ExternalInput").ap()

    y_out = nc.dram_tensor("y", [128, NC, T], F32, kind="ExternalOutput").ap()
    dbg_t = {}
    for name, shape, dt in [
        ("dbg_ln1", [128, NC * T], BF16),
        ("dbg_q", [128, T], BF16),
        ("dbg_sclq", [128, T], BF16),
        ("dbg_pt", [128, T], BF16),
        ("dbg_zraw", [65, T], BF16),
        ("dbg_zbuf", [128, NC * T], BF16),
        ("dbg_ao", [128, NC * T], BF16),
        ("dbg_vp", [128, NH * 65], BF16),
        ("dbg_ln2", [128, NC * 1024], BF16),
        ("dbg_xr", [128, NC * 1024], BF16),
        ("dbg_r", [128, NC * 1024], BF16),
        ("dbg_ka", [128, NC * 1024], BF16),
        ("dbg_bcol", [128, 6], F32),
        ("dbg_qr", [128, T], BF16),
        ("dbg_rows", [128, 2 * T], BF16),
        ("dbg_lhq", [128, 4], BF16),
    ]:
        if name in dbg:
            dbg_t[name] = nc.dram_tensor(name, shape, dt, kind="ExternalOutput").ap()

    P = Plan()
    with contextlib.ExitStack() as st:
        E = st.enter_context
        A = E(nc.sbuf_tensor("A", [128, NC * T], BF16))
        B = E(nc.sbuf_tensor("B", [128, NC * T], BF16))
        C = E(nc.sbuf_tensor("C", [128, KT * NH * 65], BF16))
        D = E(nc.sbuf_tensor("D", [128, NC * T], BF16))
        Ebuf = E(nc.sbuf_tensor("Ebuf", [128, 2 * T], BF16))
        Fb = E(nc.sbuf_tensor("Fb", [128, 4 * T], BF16))
        G = E(nc.sbuf_tensor("G", [128, 4 * T], BF16))
        ROWS = E(nc.sbuf_tensor("ROWS", [128, 2 * T], BF16))
        XS = E(nc.sbuf_tensor("XS", [128, T], F32))
        YST = E(nc.sbuf_tensor("YST", [128, T], F32))
        WS = E(nc.sbuf_tensor("WS", [128, 2 * 1024], F32))
        WB = E(nc.sbuf_tensor("WB", [128, 2 * 1024], BF16))
        bqc = E(nc.sbuf_tensor("bqc", [128, 24], F32))
        boc = E(nc.sbuf_tensor("boc", [128, 8], F32))
        murc = E(nc.sbuf_tensor("murc", [128, 8], F32))
        mukc = E(nc.sbuf_tensor("mukc", [128, 8], F32))
        omrc = E(nc.sbuf_tensor("omrc", [128, 8], F32))
        omkc = E(nc.sbuf_tensor("omkc", [128, 8], F32))
        wqc = E(nc.sbuf_tensor("wqc", [128, 1], F32))
        wkc = E(nc.sbuf_tensor("wkc", [128, 1], F32))
        lnT = E(nc.sbuf_tensor("lnT", [128, 4], BF16))
        lhq = E(nc.sbuf_tensor("lhq", [128, 4], BF16))
        lhk = E(nc.sbuf_tensor("lhk", [128, 4], BF16))
        b2 = E(nc.sbuf_tensor("b2", [2, 128], BF16))
        b2b = E(nc.sbuf_tensor("b2b", [2, 128], BF16))
        i4r = E(nc.sbuf_tensor("i4r", [4, 128], BF16))
        i4n = E(nc.sbuf_tensor("i4n", [4, 128], BF16))
        CCOL = E(nc.sbuf_tensor("CCOL", [128, 8], F32))
        bcole = E(nc.sbuf_tensor("bcole", [128, 2], F32))
        bcolo = E(nc.sbuf_tensor("bcolo", [128, 2], F32))
        bndc = E(nc.sbuf_tensor("bndc", [128, 8], BF16))
        epsc = E(nc.sbuf_tensor("epsc", [128, 2], F32))
        ps1 = E(nc.psum_tensor("ps1", [128, T], F32))
        ps2 = E(nc.psum_tensor("ps2", [128, T], F32))
        sw = Sem(E(nc.semaphore("sw")), "sync")
        si = Sem(E(nc.semaphore("si")), "gpsimd")
        sb = Sem(E(nc.semaphore("sb")), "gpsimd")
        so = Sem(E(nc.semaphore("so")), "sync")
        tS = Sem(E(nc.semaphore("tS")), "tensor")
        vS = Sem(E(nc.semaphore("vS")), "vector")
        sS = Sem(E(nc.semaphore("sS")), "scalar")
        block = E(nc.Block())

        Vv = C[:].rearrange("p (t h e) -> p t h e", t=KT, h=NH, e=65)
        zb = D[:].rearrange("p (c t) -> p c t", c=NC)
        wbv = [WB[:, i * 1024:(i + 1) * 1024].rearrange("p (c o) -> p c o", c=NC)
               for i in range(2)]
        Ach = [A[:, c * T:(c + 1) * T] for c in range(NC)]
        f = lambda i: Fb[:, i * T:(i + 1) * T]
        g = lambda i: G[:, i * T:(i + 1) * T]
        e0 = Ebuf[:, 0:T]
        e1 = Ebuf[:, T:2 * T]
        ptb = lambda p: Ebuf[:, p * T:(p + 1) * T]
        ptE = lambda k: Ebuf[:, (k % 4) * NT:(k % 4 + 1) * NT]
        ptO = lambda k: Ebuf[:, T + (k % 4) * NT:T + (k % 4 + 1) * NT]
        VBT = G[:, 0:1024]

        def dump(name, ap, waits):
            if name in dbg_t:
                tgt = dbg_t[name]
                P.step("sync", lambda e, a=ap, t=tgt: e.dma_start(out=t, in_=a),
                       waits, so, 16)

        ws_cast = [0, 0]
        wb_read = [(tS, 0), (tS, 0)]
        wld = {}

        def load_wblock(key, src, slot):
            P.step("sync", lambda e, src=src, slot=slot: e.dma_start(
                out=WS[:, slot * 1024:(slot + 1) * 1024], in_=src),
                [(si, ws_cast[slot])], sw, 16)
            P.step("gpsimd", lambda e, slot=slot: e.tensor_copy(
                WB[:, slot * 1024:(slot + 1) * 1024], WS[:, slot * 1024:(slot + 1) * 1024]),
                [(sw, sw.n), wb_read[slot]], si, 1)
            wld[key] = si.n
            ws_cast[slot] = si.n

        def consume_wb(slot, sem, count):
            wb_read[slot] = (sem, count)

        for dst, src in [(bqc, b_qkv), (boc, b_o), (murc, mu_r_c), (mukc, mu_k_c),
                         (wqc, wq_col), (wkc, wk_col), (lnT, cst_ln)]:
            P.step("sync", lambda e, d=dst, s=src: e.dma_start(out=d[:], in_=s),
                   (), sw, 16)
        P.step("sync", lambda e: e.dma_start(out=lhq[:], in_=cst_ind), (), sw, 16)
        P.step("sync", lambda e: e.dma_start(out=lhk[:], in_=cst_ind), (), sw, 16)
        P.step("sync", lambda e: e.dma_start(out=b2[:], in_=cst_b2), (), sw, 16)
        P.step("sync", lambda e: e.dma_start(out=b2b[:], in_=cst_b2b), (), sw, 16)
        P.step("sync", lambda e: e.dma_start(out=i4r[:], in_=cst_i4r), (), sw, 16)
        P.step("sync", lambda e: e.dma_start(out=i4n[:], in_=cst_i4n), (), sw, 16)
        cw = sw.n
        P.step("scalar", lambda e: e.activation(lhq[0:64, 2:3], wqc[0:64, :], AF.Square),
               [(sw, cw)], sS)
        P.step("scalar", lambda e: e.activation(lhq[64:128, 3:4], wqc[64:128, :], AF.Square),
               (), sS)
        P.step("scalar", lambda e: e.activation(lhk[0:64, 2:3], wkc[0:64, :], AF.Square),
               (), sS)
        P.step("scalar", lambda e: e.activation(lhk[64:128, 3:4], wkc[64:128, :], AF.Square),
               (), sS)
        sq2 = sS.n
        P.step("vector", lambda e: e.tensor_scalar(omrc[:], murc[:], -1.0, 1.0,
                                                   ALU.mult, ALU.add), [(sw, cw)], vS)
        P.step("vector", lambda e: e.tensor_scalar(omkc[:], mukc[:], -1.0, 1.0,
                                                   ALU.mult, ALU.add), [(sw, cw)], vS)
        P.step("vector", lambda e: e.memset(Vv[:, :, :, 64:65], 1.0), (), vS)
        P.step("vector", lambda e: e.memset(bndc[:], 0.0), (), vS)
        P.step("vector", lambda e: e.memset(epsc[:, 0:1], LN_EPS), (), vS)
        P.step("vector", lambda e: e.memset(epsc[:, 1:2], RMS_EPS), (), vS)
        cdone = vS.n

        def ln_stats_apply(xch, sq_dst, inb, mm_extra=(), mm_extra2=(), ebuf_guard=(),
                           g_guard=()):
            for c in range(NC):
                P.step("vector",
                       lambda e, c=c: e.tensor_mul(sq_dst(c), xch[c], xch[c]),
                       [inb[c]], vS)
            sqd = vS.n
            for c in range(NC):
                for nt in range(4):
                    P.step("tensor",
                           lambda e, c=c, nt=nt: e.matmul(
                               ps2[0:2, nt * NT:(nt + 1) * NT], lnT[:, 0:2],
                               xch[c][:, nt * NT:(nt + 1) * NT],
                               start=(c == 0), stop=False),
                           [inb[c], (sw, cw)] + list(mm_extra), None)
            for c in range(NC):
                for nt in range(4):
                    last = (c == NC - 1) and (nt == 3)
                    P.step("tensor",
                           lambda e, c=c, nt=nt, last=last: e.matmul(
                               ps2[0:2, nt * NT:(nt + 1) * NT], lnT[:, 2:4],
                               sq_dst(c)[:, nt * NT:(nt + 1) * NT],
                               start=False, stop=last),
                           [(vS, sqd)], tS if last else None)
            mmd = tS.n
            P.step("vector", lambda e: e.tensor_copy(ROWS[0:2, 0:T], ps2[0:2, :]),
                   [(tS, mmd)], vS)
            rowd = vS.n
            for nt in range(4):
                P.step("tensor", lambda e, nt=nt: e.matmul(
                    ps1[:, nt * NT:(nt + 1) * NT], b2[:],
                    ROWS[0:2, nt * NT:(nt + 1) * NT], start=True, stop=True),
                    [(vS, rowd)] + list(mm_extra2), tS if nt == 3 else None)
            mtd = tS.n
            for nt in range(4):
                P.step("tensor", lambda e, nt=nt: e.matmul(
                    ps2[:, nt * NT:(nt + 1) * NT], b2b[:],
                    ROWS[0:2, nt * NT:(nt + 1) * NT], start=True, stop=True),
                    [(vS, rowd)], tS if nt == 3 else None)
            qtd = tS.n
            P.step("scalar", lambda e: e.activation(e0, ps1[:], AF.Square),
                   [(tS, mtd)] + list(ebuf_guard), sS)
            sqd2 = sS.n
            P.step("vector", lambda e: e.tensor_sub(e1, ps2[:], e0),
                   [(tS, qtd), (sS, sqd2)] + list(ebuf_guard), vS)
            vd = vS.n
            P.step("scalar", lambda e: e.activation(e0, e1, AF.Ln, bias=epsc[:, 0:1]),
                   [(vS, vd)], sS)
            P.step("scalar", lambda e: e.activation(g(2), e0, AF.Exp, scale=-0.5),
                   list(g_guard), sS)
            sd = sS.n
            P.step("vector", lambda e: e.tensor_mul(g(3), ps1[:], g(2)),
                   [(sS, sd)] + list(g_guard), vS)
            return {"rows": rowd, "tiles": vS.n, "ps_used": tS.n}

        xcast = []
        for c in range(NC):
            stg = XS if c % 2 == 0 else YST
            wv = [(vS, xcast[c - 2])] if c > 1 else [(vS, cdone)]
            P.step("sync", lambda e, c=c, stg=stg: e.dma_start(out=stg[:], in_=x_in[:, c, :]),
                   wv, sw, 16)
            P.step("vector", lambda e, c=c, stg=stg: e.tensor_copy(Ach[c], stg[:]),
                   [(sw, sw.n)], vS)
            xcast.append(vS.n)
        Bsq = lambda c: B[:, c * T:(c + 1) * T]
        m1 = ln_stats_apply(Ach, Bsq, [(vS, xcast[c]) for c in range(NC)])
        for c in range(NC):
            P.step("vector", lambda e, c=c: e.tensor_mul(Bsq(c), Ach[c], g(2)),
                   [(vS, m1["tiles"])], vS)
            P.step("vector", lambda e, c=c: e.tensor_sub(Ach[c], Bsq(c), g(3)), (), vS)
        ln1d = vS.n
        dump("dbg_ln1", A[:], [(vS, ln1d)])

        P.step("sync", lambda e: e.dma_start(out=XS[0:1, 0:1024], in_=b_vrow),
               [(vS, ln1d)], sw, 16)
        P.step("vector", lambda e: e.tensor_copy(ROWS[0:1, 0:1024], XS[0:1, 0:1024]),
               [(sw, sw.n), (sb, sb.n)], vS)
        P.step("gpsimd", lambda e: e.dma_start(
            out=VBT[:], in_=ROWS[0:1, 0:1024].unsqueeze(1).to_broadcast([1, 128, 1024])),
            [(vS, vS.n)], sb, 16)
        vbtd = sb.n
        wvd = []
        for c in range(NC):
            load_wblock(("wv", c), w_qkv[:, c, 2048:3072], c % 2)
            P.step("vector", lambda e, c=c: e.tensor_copy(
                D[:, c * 1024:(c + 1) * 1024],
                WB[:, (c % 2) * 1024:((c % 2) + 1) * 1024]), [(si, wld[("wv", c)])], vS)
            wvd.append(vS.n)
            consume_wb(c % 2, vS, vS.n)
        psV = [ps1[:, 0:1024], ps1[:, 1024:2048], ps2[:, 0:1024], ps2[:, 1024:2048]]
        vevac = [0, 0, 0, 0]
        for tt in range(KT):
            reg = psV[tt % 4]
            extra = [(vS, m1["tiles"])]
            for c in range(NC):
                for nt in range(2):
                    last = (c == NC - 1) and (nt == 1)
                    P.step("tensor", lambda e, tt=tt, c=c, nt=nt, reg=reg, last=last: e.matmul(
                        reg[:, nt * NT:(nt + 1) * NT],
                        A[:, c * T + tt * 128: c * T + (tt + 1) * 128],
                        D[:, c * 1024 + nt * NT: c * 1024 + (nt + 1) * NT],
                        start=(c == 0), stop=last),
                        [(vS, wvd[c]), (vS, vevac[tt % 4]), (vS, ln1d)] + extra,
                        tS if last else None)
            mmd = tS.n
            P.step("vector", lambda e, tt=tt, reg=reg: e.tensor_add(
                Vv[:, tt, :, 0:64],
                reg[:].rearrange("p (h e) -> p h e", h=NH),
                VBT[:].rearrange("p (h e) -> p h e", h=NH)),
                [(tS, mmd), (sb, vbtd)], vS)
            vevac[tt % 4] = vS.n
        dump("dbg_vp", Vv[:, 0, :, :], [(vS, vS.n)])

        qsq = B[:, 2 * T:3 * T]
        ksq = B[:, 3 * T:4 * T]
        zr = lambda p: B[0:65, p * T:(p + 1) * T]
        zrl = lambda p: B[64:65, p * T:(p + 1) * T]
        LF = B[:, 4 * T:8 * T].bitcast(F32)
        sE = lambda k: ps1[:, (k % 2) * NT:(k % 2 + 1) * NT]
        sO = lambda k: ps1[:, 1024 + (k % 2) * NT:1024 + (k % 2 + 1) * NT]
        zE = ps2[0:65, 0:NT]
        zO = ps2[0:65, NT:2 * NT]
        FA = ps2[:, 1024:2048]
        FAh = [ps2[:, 1024:1536], ps2[:, 1536:2048]]

        last_zevac = vS.n
        last_div = {0: vS.n, 1: vS.n}
        pv_slot = {}
        exp_bank = {0: sS.n, 1: sS.n}
        sub_free = [(vS, 0), (vS, 0)]
        fa_free = (vS, 0)
        bcol_n = {}
        recs_n = {}
        lin_n = [0]
        appd_n = {}
        rows_n = {}
        chain_out = {}

        def _sub(idx, sem, cnt):
            sub_free[idx] = (sem, cnt)

        def pk_qkv_group(p, side, nt):
            sub = nt % 2
            reg = FAh[sub]
            wkey = ("q", p) if side == 0 else ("k", p)
            fraw = f(0) if side == 0 else f(1)
            sq = qsq if side == 0 else ksq
            bias = bqc[:, p:p + 1] if side == 0 else bqc[:, 8 + p:9 + p]

            def emit():
                for c in range(NC):
                    last = c == NC - 1
                    P.step("tensor", lambda e, c=c, last=last: e.matmul(
                        reg, wbv[side][:, c, :],
                        A[:, c * T + nt * NT: c * T + (nt + 1) * NT],
                        start=(c == 0), stop=last),
                        [(si, wld[wkey]), sub_free[sub]], tS if last else None)
                mmn = tS.n
                if side == 1 and nt == 3:
                    consume_wb(1, tS, mmn)
                if side == 0 and nt == 3:
                    consume_wb(0, tS, mmn)
                P.step("vector", lambda e: e.tensor_scalar_add(
                    fraw[:, nt * NT:(nt + 1) * NT], reg, bias), [(tS, mmn)], vS)
                P.step("vector", lambda e: e.tensor_mul(
                    sq[:, nt * NT:(nt + 1) * NT], fraw[:, nt * NT:(nt + 1) * NT],
                    fraw[:, nt * NT:(nt + 1) * NT]), (), vS)
                _sub(sub, vS, vS.n)
            return emit

        def pk_stats(p, side):
            sq = qsq if side == 0 else ksq
            lh = lhq if side == 0 else lhk
            off = 0 if side == 0 else T

            def emit():
                for nt in range(4):
                    sub = nt % 2
                    P.step("tensor", lambda e, nt=nt, sub=sub: e.matmul(
                        FAh[sub][0:4, :], lh[:],
                        sq[:, nt * NT:(nt + 1) * NT], start=True, stop=True),
                        [sub_free[sub], (vS, vS.n)], tS)
                    mmn = tS.n
                    P.step("vector", lambda e, nt=nt, sub=sub: e.tensor_copy(
                        ROWS[0:4, off + nt * NT: off + (nt + 1) * NT], FAh[sub][0:4, :]),
                        [(tS, mmn)], vS)
                    _sub(sub, vS, vS.n)
                rows_n[(p, side)] = vS.n
            return emit

        def pk_chain_half(p, side, hh):
            off = 0 if side == 0 else T
            sclg = g(0) if side == 0 else g(1)
            nrm = g(2) if side == 0 else g(3)
            wcol = wqc if side == 0 else wkc
            stg = XS if side == 0 else YST

            def emit():
                for i2 in range(2):
                    P.step("tensor", lambda e, i2=i2: e.matmul(
                        FAh[i2], i4r[:],
                        ROWS[0:4, off + hh * 1024 + i2 * NT: off + hh * 1024 + (i2 + 1) * NT],
                        start=True, stop=True),
                        [sub_free[i2], (vS, rows_n[(p, side)])], tS if i2 else None)
                mtn = tS.n
                P.step("scalar", lambda e: e.activation(stg[:, 0:1024], FA, AF.Ln,
                                                        bias=epsc[:, 1:2]), [(tS, mtn)], sS)
                P.step("scalar", lambda e: e.activation(stg[:, 1024:2048], stg[:, 0:1024],
                                                        AF.Exp, scale=-0.5), (), sS)
                lnn = sS.n
                for i2 in range(2):
                    P.step("tensor", lambda e, i2=i2: e.matmul(
                        FAh[i2], i4n[:],
                        ROWS[0:4, off + hh * 1024 + i2 * NT: off + hh * 1024 + (i2 + 1) * NT],
                        start=True, stop=True),
                        [(sS, lnn)], tS if i2 else None)
                ntn = tS.n
                P.step("vector", lambda e: e.tensor_mul(
                    sclg[:, hh * 1024:(hh + 1) * 1024], stg[:, 1024:2048], FA),
                    [(tS, ntn)], vS)
                gw = [(si, si.n)] if side == 1 else []
                P.step("vector", lambda e: e.tensor_mul(
                    nrm[:, hh * 1024:(hh + 1) * 1024],
                    sclg[:, hh * 1024:(hh + 1) * 1024], stg[:, 1024:2048]), gw, vS)
                P.step("vector", lambda e: e.tensor_scalar_mul(
                    sclg[:, hh * 1024:(hh + 1) * 1024], stg[:, 1024:2048], wcol[:, 0:1]),
                    (), vS)
                _sub(0, vS, vS.n)
                _sub(1, vS, vS.n)
                chain_out[(p, side, hh)] = vS.n
            return emit

        def pk_bias(p):
            par = p % 2

            def emit():
                w0 = [(vS, chain_out[(p, 0, 0)]), (vS, chain_out[(p, 0, 1)]),
                      (vS, chain_out[(p, 1, 0)]), (vS, chain_out[(p, 1, 1)])]
                P.step("vector", lambda e: e.reduce_max(CCOL[:, 0:1], g(2), axis=AX),
                       w0, vS)
                P.step("vector", lambda e: e.reduce_max(CCOL[:, 1:2], g(3), axis=AX),
                       (), vS)
                vmax = vS.n
                P.step("scalar", lambda e: e.activation(CCOL[:, 4:6], CCOL[:, 0:2], AF.Ln),
                       [(vS, vmax)], sS)
                lnn = sS.n
                P.step("vector", lambda e: e.reduce_sum(CCOL[:, 6:7], CCOL[:, 4:6],
                                                        axis=AX), [(sS, lnn)], vS)
                vsum = vS.n
                P.step("scalar", lambda e: e.activation(CCOL[:, 2:3], CCOL[:, 6:7],
                                                        AF.Exp, scale=0.5), [(vS, vsum)], sS)
                expn = sS.n
                pb = bcol_n.get(p - 2, (0, 0))[0]
                P.step("vector", lambda e: e.tensor_scalar(CCOL[:, 3:4], CCOL[:, 2:3],
                                                           -1.0, BIAS_SHIFT,
                                                           ALU.mult, ALU.add),
                       [(sS, expn), (sb, pb)], vS)
                biasd = vS.n
                P.step("sync", lambda e, par=par: e.dma_start(
                    out=bcole[:, par:par + 1],
                    in_=CCOL[0:1, 3:4].unsqueeze(1).to_broadcast([1, 128, 1])),
                    [(vS, biasd)], sb, 16)
                P.step("sync", lambda e, par=par: e.dma_start(
                    out=bcolo[:, par:par + 1],
                    in_=CCOL[64:65, 3:4].unsqueeze(1).to_broadcast([1, 128, 1])),
                    [(vS, biasd)], sb, 16)
                bcol_n[p] = (sb.n, par)
                if p == 0 and "dbg_bcol" in dbg_t:
                    dump("dbg_bcol", CCOL[:, 0:6], [(vS, biasd)])
            return emit

        def pk_divides(jp):
            def emit():
                P.step("vector", lambda e: e.memset(G[64:65, 2 * T:2 * T + 64], 1.0),
                       [(vS, recs_n[jp])], vS)
                onesd = vS.n
                for h2 in range(2):
                    for hh in range(2):
                        for i2 in range(2):
                            P.step("tensor", lambda e, hh=hh, i2=i2, h2=h2: e.matmul(
                                FAh[i2][0:64, :], G[64:65, 2 * T:2 * T + 64],
                                ROWS[64:65, h2 * T + hh * 1024 + i2 * NT:
                                     h2 * T + hh * 1024 + (i2 + 1) * NT],
                                start=True, stop=True),
                                [(vS, onesd), sub_free[i2]], tS if i2 else None)
                        linn = tS.n
                        lin_n[0] = linn
                        if h2 == 0:
                            P.step("vector", lambda e, jp=jp, hh=hh: e.tensor_mul(
                                zb[0:64, jp, hh * 1024:(hh + 1) * 1024],
                                zr(0)[0:64, hh * 1024:(hh + 1) * 1024], FA[0:64, :]),
                                [(tS, linn)], vS)
                            last_div[0] = vS.n
                        else:
                            P.step("vector", lambda e, jp=jp, hh=hh: e.tensor_mul(
                                g(3)[0:64, hh * 1024:(hh + 1) * 1024],
                                zr(1)[0:64, hh * 1024:(hh + 1) * 1024], FA[0:64, :]),
                                [(tS, linn), (si, si.n)], vS)
                            last_div[1] = vS.n
                        _sub(0, vS, vS.n)
                        _sub(1, vS, vS.n)
                P.step("gpsimd", lambda e, jp=jp: e.dma_start(
                    out=zb[64:128, jp, :], in_=g(3)[0:64, :]),
                    [(vS, last_div[1])], si, 16)
            return emit

        def emit_attention(j, packets):
            nonlocal last_zevac
            pki = [0]

            def pop(n=1):
                for _ in range(n):
                    if pki[0] < len(packets):
                        packets[pki[0]]()
                        pki[0] += 1

            par = bcol_n[j][1]
            bcE = bcole[:, par:par + 1]
            bcO = bcolo[:, par:par + 1]
            hE, hO = 2 * j, 2 * j + 1
            zfree = {0: (vS, last_zevac), 1: (vS, last_zevac)}
            expn = {}
            for qq in range(4):
                for kt in range(KT):
                    swl = [(sS, exp_bank[kt % 2]), (vS, appd_n[j]),
                           (sb, bcol_n[j][0])]
                    P.step("tensor", lambda e, kt=kt, qq=qq: e.matmul(
                        sE(kt), f(3)[0:64, kt * 128:(kt + 1) * 128],
                        f(2)[0:64, qq * NT:(qq + 1) * NT],
                        start=True, stop=True, tile_position=(0, 0)), swl, None)
                    P.step("tensor", lambda e, kt=kt, qq=qq: e.matmul(
                        sO(kt), f(3)[64:128, kt * 128:(kt + 1) * 128],
                        f(2)[64:128, qq * NT:(qq + 1) * NT],
                        start=True, stop=True, tile_position=(64, 0)), (), tS)
                    smm = tS.n
                    ew = [(tS, smm)]
                    if (0, kt % 4) in pv_slot:
                        ew.append((tS, pv_slot[(0, kt % 4)]))
                    P.step("scalar", lambda e, kt=kt, bcE=bcE: e.activation(
                        ptE(kt), sE(kt), AF.Exp, bias=bcE), ew, sS)
                    ew2 = []
                    if (1, kt % 4) in pv_slot:
                        ew2.append((tS, pv_slot[(1, kt % 4)]))
                    P.step("scalar", lambda e, kt=kt, bcO=bcO: e.activation(
                        ptO(kt), sO(kt), AF.Exp, bias=bcO), ew2, sS)
                    exp_bank[kt % 2] = sS.n
                    expn[kt] = sS.n
                    if kt > 1:
                        kp = kt - 2
                        pwE = [(sS, expn[kp])]
                        if kp == 0:
                            pwE.append(zfree[0])
                            pwE.append(zfree[1])
                        P.step("tensor", lambda e, kp=kp, qq=qq, hE=hE: e.matmul(
                            zE, Vv[:, kp, hE, :], ptE(kp),
                            start=(kp == 0), stop=False), pwE, None)
                        P.step("tensor", lambda e, kp=kp, qq=qq, hO=hO: e.matmul(
                            zO, Vv[:, kp, hO, :], ptO(kp),
                            start=(kp == 0), stop=False), (), tS)
                        pv_slot[(0, kp % 4)] = tS.n
                        pv_slot[(1, kp % 4)] = tS.n
                    if kt % 4 == 1:
                        pop()
                for kp in (KT - 2, KT - 1):
                    P.step("tensor", lambda e, kp=kp, qq=qq, hE=hE: e.matmul(
                        zE, Vv[:, kp, hE, :], ptE(kp),
                        start=False, stop=(kp == KT - 1)), [(sS, expn[kp])], None)
                    P.step("tensor", lambda e, kp=kp, qq=qq, hO=hO: e.matmul(
                        zO, Vv[:, kp, hO, :], ptO(kp),
                        start=False, stop=(kp == KT - 1)), (), tS)
                    pv_slot[(0, kp % 4)] = tS.n
                    pv_slot[(1, kp % 4)] = tS.n
                pvall = tS.n
                P.step("vector", lambda e, qq=qq: e.tensor_copy(
                    zr(0)[:, qq * NT:(qq + 1) * NT], zE),
                    [(tS, pvall), (vS, last_div[0])], vS)
                zfree[0] = (vS, vS.n)
                P.step("vector", lambda e, qq=qq: e.tensor_copy(
                    zr(1)[:, qq * NT:(qq + 1) * NT], zO),
                    [(vS, last_div[1])], vS)
                zfree[1] = (vS, vS.n)
                last_zevac = vS.n

                P.step("vector", lambda e, qq=qq: e.tensor_copy(
                    LF[64:65, qq * NT:(qq + 1) * NT], ps2[64:65, 0:NT]), (), vS)
                P.step("vector", lambda e, qq=qq: e.tensor_copy(
                    LF[64:65, 2048 + qq * NT: 2048 + (qq + 1) * NT],
                    ps2[64:65, NT:2 * NT]), (), vS)
                zfree[0] = (vS, vS.n)
                zfree[1] = (vS, vS.n)
            if j == 0:
                dump("dbg_zraw", zr(0), [(vS, last_zevac)])

            pop(len(packets))

        load_wblock(("q", 0), w_qkv[:, :, 0:128], 0)
        load_wblock(("k", 0), w_qkv[:, :, 1024:1152], 1)
        sub_free = [(vS, vevac[3]), (vS, vevac[3])]
        for side in range(2):
            for nt in range(4):
                pk_qkv_group(0, side, nt)()
            pk_stats(0, side)()
        load_wblock(("q", 1), w_qkv[:, :, 128:256], 0)
        load_wblock(("k", 1), w_qkv[:, :, 1024 + 128:1024 + 256], 1)
        for side in range(2):
            for hh in range(2):
                pk_chain_half(0, side, hh)()
        pk_bias(0)()
        if "dbg_qr" in dbg_t:
            dump("dbg_qr", f(0), [(vS, vS.n)])
            dump("dbg_lhq", lhq[:], [(vS, cdone)])
            dump("dbg_rows", ROWS[:], [(vS, rows_n[(0, 1)])])
        P.step("vector", lambda e: e.tensor_mul(f(2), f(0), g(0)),
               [(vS, chain_out[(0, 0, 1)]), (vS, chain_out[(0, 1, 1)])], vS)
        P.step("vector", lambda e: e.tensor_mul(f(3), f(1), g(1)), (), vS)
        appd_n[0] = vS.n
        if "dbg_q" in dbg_t:
            dump("dbg_q", f(2), [(vS, appd_n[0])])
            dump("dbg_sclq", g(0), [(vS, appd_n[0])])

        for j in range(NP):
            packets = []
            if j > 0:
                packets.append(pk_divides(j - 1))
            if j + 1 < NP:
                p = j + 1
                for side in range(2):
                    for nt in range(4):
                        packets.append(pk_qkv_group(p, side, nt))
                    packets.append(pk_stats(p, side))
                for side in range(2):
                    for hh in range(2):
                        packets.append(pk_chain_half(p, side, hh))
                packets.append(pk_bias(p))
            emit_attention(j, packets)

            def emit_recips(j=j):
                for s8 in range(8):
                    def _rc(e, s8=s8):
                        with nc.allow_low_precision(reason="softmax denom to bf16"):
                            return e.reciprocal(
                                ROWS[64:65, s8 * NT:(s8 + 1) * NT],
                                LF[64:65, s8 * NT:(s8 + 1) * NT])
                    P.step("vector", _rc, [(tS, lin_n[0])] if s8 == 0 else (), vS)
                recs_n[j] = vS.n
            if j + 1 < NP:
                if j + 2 < NP:
                    load_wblock(("q", j + 2), w_qkv[:, :, (j + 2) * 128:(j + 3) * 128], 0)
                    load_wblock(("k", j + 2),
                                w_qkv[:, :, 1024 + (j + 2) * 128:1024 + (j + 3) * 128], 1)
                else:
                    load_wblock(("o", 0), w_o[:, :, 0:128], 0)
                    load_wblock(("o", 1), w_o[:, :, 128:256], 1)
                lastS = tS.n
                P.step("vector", lambda e: e.tensor_mul(f(2), f(0), g(0)),
                       [(vS, chain_out[(j + 1, 0, 1)]), (vS, chain_out[(j + 1, 1, 1)]),
                        (tS, lastS)], vS)
                P.step("vector", lambda e: e.tensor_mul(f(3), f(1), g(1)), (), vS)
                appd_n[j + 1] = vS.n
                emit_recips()
            else:
                emit_recips()

        pk_divides(NP - 1)()
        zdone_v = vS.n
        zdone_i = si.n
        pv_final = tS.n
        dump("dbg_zbuf", D[:], [(vS, zdone_v), (si, zdone_i)])

        psW = [ps1, ps2]
        woev = [sS.n, sS.n]
        for o in range(NC):
            reg = psW[o % 2]
            basew = [(si, wld[("o", o)]), (vS, zdone_v), (si, zdone_i),
                     (sS, woev[o % 2])]
            if o < 2:
                basew += [(sS, exp_bank[0]), (sS, exp_bank[1])]
            for c in range(NC):
                for nt in range(4):
                    last = (c == NC - 1) and (nt == 3)
                    P.step("tensor", lambda e, o=o, c=c, nt=nt, reg=reg, last=last: e.matmul(
                        reg[:, nt * NT:(nt + 1) * NT], wbv[o % 2][:, c, :],
                        zb[:, c, nt * NT:(nt + 1) * NT], start=(c == 0), stop=last),
                        basew, tS if last else None)
            mmd = tS.n
            consume_wb(o % 2, tS, mmd)
            if o + 2 < NC:
                load_wblock(("o", o + 2), w_o[:, :, (o + 2) * 128:(o + 3) * 128], o % 2)
            elif o + 2 == NC:
                load_wblock(("f", 0, 0, 0), w_r[:, :, 0:128], 0)
            elif o + 2 == NC + 1:
                load_wblock(("f", 0, 0, 1), w_r[:, :, 128:256], 1)
            P.step("scalar", lambda e, o=o, reg=reg: e.activation(
                B[:, o * T:(o + 1) * T], reg[:], AF.Identity, bias=boc[:, o:o + 1]),
                [(tS, mmd)], sS)
            woev[o % 2] = sS.n
        aod = sS.n
        dump("dbg_ao", B[:], [(sS, aod)])

        x1c = []
        for c in range(NC):
            wv = [(vS, x1c[c - 1])] if c > 0 else [(sS, aod)]
            P.step("sync", lambda e, c=c: e.dma_start(out=XS[:], in_=x_in[:, c, :]),
                   wv, sw, 16)
            P.step("vector", lambda e, c=c: e.tensor_add(Ach[c], XS[:],
                                                         B[:, c * T:(c + 1) * T]),
                   [(sw, sw.n), (sS, aod)], vS)
            x1c.append(vS.n)
        Dsq = lambda c: D[:, c * T:(c + 1) * T]
        m2 = ln_stats_apply(Ach, Dsq, [(vS, x1c[c]) for c in range(NC)],
                            mm_extra=[(sS, aod)],
                            mm_extra2=[(sS, aod)],
                            ebuf_guard=[(tS, pv_final)],
                            g_guard=[(vS, zdone_v), (si, zdone_i)])
        ln2tiles = m2["tiles"]

        ln2v = D[:, 0:8192].rearrange("p (c t) -> p c t", c=NC, t=1024)
        xrv = D[:, 8192:16384].rearrange("p (c t) -> p c t", c=NC, t=1024)
        xkv = C[:, 0:8192].rearrange("p (c t) -> p c t", c=NC, t=1024)
        rv = C[:, 8192:16384].rearrange("p (c t) -> p c t", c=NC, t=1024)
        kav = Fb[:, 0:8192].rearrange("p (c t) -> p c t", c=NC, t=1024)
        psF = [ps1[:, 0:1024], ps1[:, 1024:2048], ps2[:, 0:1024], ps2[:, 1024:2048]]
        ffev = [(vS.n, sS.n)] * 4
        ydma_slot = [0, 0]
        xsfree = vS.n
        revac = {}
        prev_mix = vS.n
        for hh in range(2):
            t0 = hh * 1024
            apl = []
            for c in range(NC):
                aw = [(vS, ln2tiles), (vS, ffev[0][0]), (vS, ffev[1][0]),
                      (vS, ffev[2][0]), (vS, ffev[3][0]), (vS, prev_mix)]
                P.step("vector", lambda e, c=c, t0=t0: e.tensor_mul(
                    VBT[:], Ach[c][:, t0:t0 + 1024],
                    G[:, 2 * T + t0: 2 * T + t0 + 1024]), aw, vS)
                P.step("vector", lambda e, c=c, t0=t0: e.tensor_sub(
                    ln2v[:, c, :], VBT[:], G[:, 3 * T + t0: 3 * T + t0 + 1024]),
                    (), vS)
                apl.append(vS.n)
            if hh == 0:
                dump("dbg_ln2", D[:, 0:8192], [(vS, vS.n)])
            mixd = []
            for c in range(NC):
                for (xv, mu, om) in [(xrv, murc, omrc), (xkv, mukc, omkc)]:
                    P.step("vector", lambda e, c=c, mu=mu: e.tensor_scalar_mul(
                        VBT[:], ln2v[:, c, :], mu[:, c:c + 1]), [(vS, apl[c])], vS)
                    P.step("vector", lambda e, c=c, xv=xv, om=om: e.scalar_tensor_tensor(
                        xv[:, c, 1:1024], ln2v[:, c, 0:1023], om[:, c:c + 1],
                        VBT[:, 1:1024], ALU.mult, ALU.add), (), vS)
                    P.step("vector", lambda e, c=c, xv=xv, om=om: e.scalar_tensor_tensor(
                        xv[:, c, 0:1], bndc[:, c:c + 1], om[:, c:c + 1],
                        VBT[:, 0:1], ALU.mult, ALU.add), (), vS)
                mixd.append(vS.n)
            prev_mix = vS.n
            if hh == 0:
                P.step("vector", lambda e: e.tensor_copy(
                    bndc[:], ln2v[:, :, 1023:1024].rearrange("p c t -> p (c t)")),
                    (), vS)
                prev_mix = vS.n
                dump("dbg_xr", D[:, 8192:16384], [(vS, mixd[-1])])
            matsrc = {0: w_r, 1: w_k2, 2: w_v2}

            def next_block(hh, mat, o):
                if o + 1 < NC:
                    return (hh, mat, o + 1)
                if mat + 1 <= 2:
                    return (hh, mat + 1, 0)
                if hh == 0:
                    return (1, 0, 0)
                return None

            def prefetch_after(hh, mat, o):
                nb = next_block(hh, mat, o)
                if nb is None:
                    return
                nb2 = next_block(*nb)
                if nb2 is None:
                    return
                h2_, m2_, o2_ = nb2
                if ("f", h2_, m2_, o2_) in wld:
                    return
                load_wblock(("f", h2_, m2_, o2_),
                            matsrc[m2_][:, :, o2_ * 128:(o2_ + 1) * 128], o % 2)

            if hh == 0:
                pass
            for mat, (wsrc, xv, kind) in enumerate([(w_r, xrv, "r"), (w_k2, xkv, "k")]):
                for o in range(NC):
                    slot = o % 2
                    reg = psF[2 * mat + (o % 2)]
                    ri = 2 * mat + (o % 2)
                    basew = [(si, wld[("f", hh, mat, o)]), (vS, ffev[ri][0]),
                             (sS, ffev[ri][1]), (vS, mixd[-1])]
                    if hh == 0 and mat == 0 and o < 2:
                        basew += [(sS, exp_bank[0]), (sS, exp_bank[1]), (sS, aod)]
                    for c in range(NC):
                        for nt in range(2):
                            last = (c == NC - 1) and (nt == 1)
                            P.step("tensor", lambda e, c=c, nt=nt, reg=reg, slot=slot,
                                   xv=xv, last=last: e.matmul(
                                       reg[:, nt * NT:(nt + 1) * NT], wbv[slot][:, c, :],
                                       xv[:, c, nt * NT:(nt + 1) * NT],
                                       start=(c == 0), stop=last),
                                   basew, tS if last else None)
                    mmd = tS.n
                    consume_wb(slot, tS, mmd)
                    prefetch_after(hh, mat, o)
                    if kind == "r":
                        P.step("scalar", lambda e, o=o, reg=reg: e.activation(
                            rv[:, o, :], reg[:], AF.Sigmoid), [(tS, mmd)], sS)
                        revac[(hh, o)] = sS.n
                        ffev[ri] = (vS.n, sS.n)
                    else:
                        P.step("vector", lambda e, o=o, reg=reg: e.tensor_scalar_max(
                            VBT[:], reg[:], 0.0), [(tS, mmd)], vS)
                        P.step("vector", lambda e, o=o: e.tensor_mul(
                            kav[:, o, :], VBT[:], VBT[:]), (), vS)
                        ffev[ri] = (vS.n, sS.n)
            kall = vS.n
            if hh == 0:
                dump("dbg_r", C[:, 8192:16384], [(sS, sS.n)])
                dump("dbg_ka", Fb[:, 0:8192], [(vS, kall)])
            for o in range(NC):
                slot = o % 2
                reg = psF[2 + (o % 2)]
                ri = 2 + (o % 2)
                basew = [(si, wld[("f", hh, 2, o)]), (vS, ffev[ri][0]),
                         (sS, ffev[ri][1]), (vS, kall)]
                for c in range(NC):
                    for nt in range(2):
                        last = (c == NC - 1) and (nt == 1)
                        P.step("tensor", lambda e, c=c, nt=nt, reg=reg, slot=slot,
                               last=last: e.matmul(
                                   reg[:, nt * NT:(nt + 1) * NT], wbv[slot][:, c, :],
                                   kav[:, c, nt * NT:(nt + 1) * NT],
                                   start=(c == 0), stop=last),
                               basew, tS if last else None)
                mmd = tS.n
                consume_wb(slot, tS, mmd)
                prefetch_after(hh, 2, o)
                P.step("sync", lambda e, o=o, t0=t0: e.dma_start(
                    out=XS[:, 0:1024], in_=x_in[:, o, t0:t0 + 1024]),
                    [(vS, xsfree)], sw, 16)
                xld = sw.n
                par = o % 2
                yreg = YST[:, par * 1024:(par + 1) * 1024]
                P.step("vector", lambda e, o=o, hh=hh, reg=reg, yreg=yreg:
                       e.scalar_tensor_tensor(yreg, reg[:], 1.0, rv[:, o, :],
                                              ALU.mult, ALU.mult),
                       [(tS, mmd), (so, ydma_slot[par]), (sS, revac[(hh, o)])], vS)
                P.step("vector", lambda e, o=o, t0=t0, yreg=yreg: e.tensor_add(
                    XS[:, 1024:2048], yreg, B[:, o * T + t0: o * T + t0 + 1024]),
                    (), vS)
                P.step("vector", lambda e, yreg=yreg: e.tensor_add(
                    yreg, XS[:, 1024:2048], XS[:, 0:1024]), [(sw, xld)], vS)
                xsfree = vS.n
                find = vS.n
                P.step("sync", lambda e, o=o, t0=t0, yreg=yreg: e.dma_start(
                    out=y_out[:, o, t0:t0 + 1024], in_=yreg), [(vS, find)], so, 16)
                ydma_slot[par] = so.n
                ffev[ri] = (vS.n, sS.n)
        P.step("sync", lambda e, n=so.n: e.wait_ge(so.h, n), (), None)

        @block.sync
        def _(e):
            for fn in P.ops["sync"]:
                fn(e)

        @block.gpsimd
        def _(e):
            for fn in P.ops["gpsimd"]:
                fn(e)

        @block.scalar
        def _(e):
            for fn in P.ops["scalar"]:
                fn(e)

        @block.vector
        def _(e):
            for fn in P.ops["vector"]:
                fn(e)

        @block.tensor
        def _(e):
            for fn in P.ops["tensor"]:
                fn(e)

    return nc


def _prep_inputs(L0_attn_x, L0_attn_Wqkv, L0_attn_bqkv, L0_attn_Wo, L0_attn_bo,
                 L0_attn_wq, L0_attn_wk, L0_ffn_W_r, L0_ffn_W_k, L0_ffn_W_v,
                 L0_ffn_mu_r, L0_ffn_mu_k):
    f32 = np.float32
    bf16 = ml_dtypes.bfloat16
    x = np.asarray(L0_attn_x, f32)

    def fm(w):
        wt = np.ascontiguousarray(np.asarray(w, f32).T)
        return np.ascontiguousarray(wt.reshape(8, 128, -1).transpose(1, 0, 2))

    wqkv = fm(L0_attn_Wqkv)
    wo = fm(L0_attn_Wo)
    wr = fm(L0_ffn_W_r)
    wk2 = fm(L0_ffn_W_k)
    wv2 = fm(L0_ffn_W_v)
    bqkv = np.asarray(L0_attn_bqkv, f32)
    b_qkv = np.ascontiguousarray(bqkv.reshape(24, 128).T)
    b_vrow = np.ascontiguousarray(bqkv[2048:3072].reshape(1, 1024))
    b_o = np.ascontiguousarray(np.asarray(L0_attn_bo, f32).reshape(8, 128).T)
    mu_r = np.ascontiguousarray(np.asarray(L0_ffn_mu_r, f32).reshape(8, 128).T)
    mu_k = np.ascontiguousarray(np.asarray(L0_ffn_mu_k, f32).reshape(8, 128).T)
    wq_col = np.ascontiguousarray(np.tile(np.asarray(L0_attn_wq, f32), 2).reshape(128, 1))
    wk_col = np.ascontiguousarray(np.tile(np.asarray(L0_attn_wk, f32), 2).reshape(128, 1))
    cst_ln = np.zeros((128, 4), bf16)
    cst_ln[:, 0] = bf16(1.0 / 1024.0)
    cst_ln[:, 3] = bf16(1.0 / 1024.0)
    cst_ind = np.zeros((128, 4), bf16)
    cst_ind[0:64, 0] = bf16(1.0 / 64.0)
    cst_ind[64:128, 1] = bf16(1.0 / 64.0)
    cst_b2 = np.zeros((2, 128), bf16)
    cst_b2[0, :] = bf16(1.0)
    cst_b2b = np.zeros((2, 128), bf16)
    cst_b2b[1, :] = bf16(1.0)
    cst_i4r = np.zeros((4, 128), bf16)
    cst_i4r[0, 0:64] = bf16(1.0)
    cst_i4r[1, 64:128] = bf16(1.0)
    cst_i4n = np.zeros((4, 128), bf16)
    cst_i4n[2, 0:64] = bf16(1.0)
    cst_i4n[3, 64:128] = bf16(1.0)

    shared = dict(w_qkv=wqkv, w_o=wo, w_r=wr, w_k2=wk2, w_v2=wv2, b_qkv=b_qkv,
                  b_vrow=b_vrow, b_o=b_o, mu_r_c=mu_r, mu_k_c=mu_k,
                  wq_col=wq_col, wk_col=wk_col, cst_ln=cst_ln, cst_ind=cst_ind,
                  cst_b2=cst_b2, cst_b2b=cst_b2b, cst_i4r=cst_i4r, cst_i4n=cst_i4n)
    in_maps = []
    for b in range(8):
        xb = np.ascontiguousarray(x[b].T.reshape(8, 128, T).transpose(1, 0, 2))
        m = dict(shared)
        m["x_in"] = xb
        in_maps.append(m)
    return in_maps


def kernel(**inputs):
    in_maps = _prep_inputs(**inputs)
    nc = build_program()
    res = run_bass_kernel_spmd(nc, in_maps, core_ids=list(range(8)))
    outs = []
    for b in range(8):
        yb = res.results[b]["y"]
        outs.append(yb.transpose(1, 0, 2).reshape(1024, T).T)
    return np.ascontiguousarray(np.stack(outs, axis=0), dtype=np.float32)


# version 09
# speedup vs baseline: 1.0862x; 1.0739x over previous
import contextlib
import numpy as np
import ml_dtypes
import concourse.bass as bass
import concourse.mybir as mybir
from concourse.bass_utils import run_bass_kernel_spmd

F32 = mybir.dt.float32
BF16 = mybir.dt.bfloat16
AX = mybir.AxisListType.X
AF = mybir.ActivationFunctionType
ALU = mybir.AluOpType

T = 2048
NC = 8
NT = 512
NH = 16
HD = 64
NP = 8
KT = 16
LN_EPS = 1e-5
RMS_EPS = 1e-6
BIAS_SHIFT = 30.0


class Sem:
    def __init__(self, h, owner):
        self.h, self.owner, self.n = h, owner, 0


class Plan:
    ENGS = ("sync", "gpsimd", "scalar", "vector", "tensor")

    def __init__(self):
        self.ops = {e: [] for e in self.ENGS}
        self.last_wait = {e: {} for e in self.ENGS}

    def step(self, eng, emitfn, waits=(), inc=None, inc_amt=1):
        lw = self.last_wait[eng]
        ws = []
        for s, n in waits:
            if s.owner == eng or n <= 0:
                continue
            if lw.get(id(s), -1) >= n:
                continue
            assert n <= s.n, f"wait on future count {n} > {s.n} ({eng})"
            lw[id(s)] = n
            ws.append((s, n))
        if inc is not None:
            inc.n += inc_amt

        def f(e):
            for s, n in ws:
                e.wait_ge(s.h, n)
            ins = emitfn(e)
            if inc is not None:
                ins.then_inc(inc.h, inc_amt)

        self.ops[eng].append(f)


def build_program(dbg=frozenset()):
    nc = bass.Bass("TRN2", target_bir_lowering=False)

    x_in = nc.dram_tensor("x_in", [128, NC, T], F32, kind="ExternalInput").ap()
    w_qkv = nc.dram_tensor("w_qkv", [128, NC, 3072], F32, kind="ExternalInput").ap()
    w_o = nc.dram_tensor("w_o", [128, NC, 1024], F32, kind="ExternalInput").ap()
    w_r = nc.dram_tensor("w_r", [128, NC, 1024], F32, kind="ExternalInput").ap()
    w_k2 = nc.dram_tensor("w_k2", [128, NC, 1024], F32, kind="ExternalInput").ap()
    w_v2 = nc.dram_tensor("w_v2", [128, NC, 1024], F32, kind="ExternalInput").ap()
    b_qkv = nc.dram_tensor("b_qkv", [128, 24], F32, kind="ExternalInput").ap()
    b_vrow = nc.dram_tensor("b_vrow", [1, 1024], F32, kind="ExternalInput").ap()
    b_o = nc.dram_tensor("b_o", [128, 8], F32, kind="ExternalInput").ap()
    mu_r_c = nc.dram_tensor("mu_r_c", [128, 8], F32, kind="ExternalInput").ap()
    mu_k_c = nc.dram_tensor("mu_k_c", [128, 8], F32, kind="ExternalInput").ap()
    wq_col = nc.dram_tensor("wq_col", [128, 1], F32, kind="ExternalInput").ap()
    wk_col = nc.dram_tensor("wk_col", [128, 1], F32, kind="ExternalInput").ap()
    cst_ln = nc.dram_tensor("cst_ln", [128, 4], BF16, kind="ExternalInput").ap()
    cst_b2 = nc.dram_tensor("cst_b2", [2, 128], BF16, kind="ExternalInput").ap()
    cst_b2b = nc.dram_tensor("cst_b2b", [2, 128], BF16, kind="ExternalInput").ap()
    cst_i4r = nc.dram_tensor("cst_i4r", [4, 128], BF16, kind="ExternalInput").ap()
    cst_i4n = nc.dram_tensor("cst_i4n", [4, 128], BF16, kind="ExternalInput").ap()
    cst_ind = nc.dram_tensor("cst_ind", [128, 4], BF16, kind="ExternalInput").ap()

    y_out = nc.dram_tensor("y", [128, NC, T], F32, kind="ExternalOutput").ap()
    dbg_t = {}
    for name, shape, dt in [
        ("dbg_ln1", [128, NC * T], BF16),
        ("dbg_q", [128, T], BF16),
        ("dbg_sclq", [128, T], BF16),
        ("dbg_pt", [128, T], BF16),
        ("dbg_zraw", [65, T], BF16),
        ("dbg_zbuf", [128, NC * T], BF16),
        ("dbg_ao", [128, NC * T], BF16),
        ("dbg_vp", [128, NH * 65], BF16),
        ("dbg_ln2", [128, NC * 1024], BF16),
        ("dbg_xr", [128, NC * 1024], BF16),
        ("dbg_r", [128, NC * 1024], BF16),
        ("dbg_ka", [128, NC * 1024], BF16),
        ("dbg_bcol", [128, 6], F32),
        ("dbg_qr", [128, T], BF16),
        ("dbg_rows", [128, 2 * T], BF16),
        ("dbg_lhq", [128, 4], BF16),
    ]:
        if name in dbg:
            dbg_t[name] = nc.dram_tensor(name, shape, dt, kind="ExternalOutput").ap()

    P = Plan()
    with contextlib.ExitStack() as st:
        E = st.enter_context
        A = E(nc.sbuf_tensor("A", [128, NC * T], BF16))
        B = E(nc.sbuf_tensor("B", [128, NC * T], BF16))
        C = E(nc.sbuf_tensor("C", [128, KT * NH * 65], BF16))
        D = E(nc.sbuf_tensor("D", [128, NC * T], BF16))
        Ebuf = E(nc.sbuf_tensor("Ebuf", [128, 2 * T], BF16))
        Fb = E(nc.sbuf_tensor("Fb", [128, 4 * T], BF16))
        G = E(nc.sbuf_tensor("G", [128, 4 * T], BF16))
        ROWS = E(nc.sbuf_tensor("ROWS", [128, 2 * T], BF16))
        XS = E(nc.sbuf_tensor("XS", [128, T], F32))
        YST = E(nc.sbuf_tensor("YST", [128, T], F32))
        WS = E(nc.sbuf_tensor("WS", [128, 2 * 1024], F32))
        WB = E(nc.sbuf_tensor("WB", [128, 2 * 1024], BF16))
        bqc = E(nc.sbuf_tensor("bqc", [128, 24], F32))
        boc = E(nc.sbuf_tensor("boc", [128, 8], F32))
        murc = E(nc.sbuf_tensor("murc", [128, 8], F32))
        mukc = E(nc.sbuf_tensor("mukc", [128, 8], F32))
        omrc = E(nc.sbuf_tensor("omrc", [128, 8], F32))
        omkc = E(nc.sbuf_tensor("omkc", [128, 8], F32))
        wqc = E(nc.sbuf_tensor("wqc", [128, 1], F32))
        wkc = E(nc.sbuf_tensor("wkc", [128, 1], F32))
        lnT = E(nc.sbuf_tensor("lnT", [128, 4], BF16))
        lhq = E(nc.sbuf_tensor("lhq", [128, 4], BF16))
        lhk = E(nc.sbuf_tensor("lhk", [128, 4], BF16))
        b2 = E(nc.sbuf_tensor("b2", [2, 128], BF16))
        b2b = E(nc.sbuf_tensor("b2b", [2, 128], BF16))
        i4r = E(nc.sbuf_tensor("i4r", [4, 128], BF16))
        i4n = E(nc.sbuf_tensor("i4n", [4, 128], BF16))
        CCOL = E(nc.sbuf_tensor("CCOL", [128, 8], F32))
        bcole = E(nc.sbuf_tensor("bcole", [128, 2], F32))
        bcolo = E(nc.sbuf_tensor("bcolo", [128, 2], F32))
        bndc = E(nc.sbuf_tensor("bndc", [128, 8], BF16))
        epsc = E(nc.sbuf_tensor("epsc", [128, 2], F32))
        ps1 = E(nc.psum_tensor("ps1", [128, T], F32))
        ps2 = E(nc.psum_tensor("ps2", [128, T], F32))
        sw = Sem(E(nc.semaphore("sw")), "sync")
        si = Sem(E(nc.semaphore("si")), "gpsimd")
        sb = Sem(E(nc.semaphore("sb")), "gpsimd")
        so = Sem(E(nc.semaphore("so")), "sync")
        tS = Sem(E(nc.semaphore("tS")), "tensor")
        vS = Sem(E(nc.semaphore("vS")), "vector")
        sS = Sem(E(nc.semaphore("sS")), "scalar")
        block = E(nc.Block())

        Vv = C[:].rearrange("p (t h e) -> p t h e", t=KT, h=NH, e=65)
        zb = D[:].rearrange("p (c t) -> p c t", c=NC)
        wbv = [WB[:, i * 1024:(i + 1) * 1024].rearrange("p (c o) -> p c o", c=NC)
               for i in range(2)]
        Ach = [A[:, c * T:(c + 1) * T] for c in range(NC)]
        f = lambda i: Fb[:, i * T:(i + 1) * T]
        g = lambda i: G[:, i * T:(i + 1) * T]
        e0 = Ebuf[:, 0:T]
        e1 = Ebuf[:, T:2 * T]
        ptb = lambda p: Ebuf[:, p * T:(p + 1) * T]
        ptE = lambda k: Ebuf[:, (k % 4) * NT:(k % 4 + 1) * NT]
        ptO = lambda k: Ebuf[:, T + (k % 4) * NT:T + (k % 4 + 1) * NT]
        VBT = G[:, 0:1024]

        def dump(name, ap, waits):
            if name in dbg_t:
                tgt = dbg_t[name]
                P.step("sync", lambda e, a=ap, t=tgt: e.dma_start(out=t, in_=a),
                       waits, so, 16)

        ws_cast = [0, 0]
        wb_read = [(tS, 0), (tS, 0)]
        wld = {}

        def load_wblock(key, src, slot):
            P.step("sync", lambda e, src=src, slot=slot: e.dma_start(
                out=WS[:, slot * 1024:(slot + 1) * 1024], in_=src),
                [(si, ws_cast[slot])], sw, 16)
            P.step("gpsimd", lambda e, slot=slot: e.tensor_copy(
                WB[:, slot * 1024:(slot + 1) * 1024], WS[:, slot * 1024:(slot + 1) * 1024]),
                [(sw, sw.n), wb_read[slot]], si, 1)
            wld[key] = si.n
            ws_cast[slot] = si.n

        def consume_wb(slot, sem, count):
            wb_read[slot] = (sem, count)

        for dst, src in [(bqc, b_qkv), (boc, b_o), (murc, mu_r_c), (mukc, mu_k_c),
                         (wqc, wq_col), (wkc, wk_col), (lnT, cst_ln)]:
            P.step("sync", lambda e, d=dst, s=src: e.dma_start(out=d[:], in_=s),
                   (), sw, 16)
        P.step("sync", lambda e: e.dma_start(out=lhq[:], in_=cst_ind), (), sw, 16)
        P.step("sync", lambda e: e.dma_start(out=lhk[:], in_=cst_ind), (), sw, 16)
        P.step("sync", lambda e: e.dma_start(out=b2[:], in_=cst_b2), (), sw, 16)
        P.step("sync", lambda e: e.dma_start(out=b2b[:], in_=cst_b2b), (), sw, 16)
        P.step("sync", lambda e: e.dma_start(out=i4r[:], in_=cst_i4r), (), sw, 16)
        P.step("sync", lambda e: e.dma_start(out=i4n[:], in_=cst_i4n), (), sw, 16)
        cw = sw.n
        P.step("scalar", lambda e: e.activation(lhq[0:64, 2:3], wqc[0:64, :], AF.Square),
               [(sw, cw)], sS)
        P.step("scalar", lambda e: e.activation(lhq[64:128, 3:4], wqc[64:128, :], AF.Square),
               (), sS)
        P.step("scalar", lambda e: e.activation(lhk[0:64, 2:3], wkc[0:64, :], AF.Square),
               (), sS)
        P.step("scalar", lambda e: e.activation(lhk[64:128, 3:4], wkc[64:128, :], AF.Square),
               (), sS)
        sq2 = sS.n
        P.step("vector", lambda e: e.tensor_scalar(omrc[:], murc[:], -1.0, 1.0,
                                                   ALU.mult, ALU.add), [(sw, cw)], vS)
        P.step("vector", lambda e: e.tensor_scalar(omkc[:], mukc[:], -1.0, 1.0,
                                                   ALU.mult, ALU.add), [(sw, cw)], vS)
        P.step("vector", lambda e: e.memset(Vv[:, :, :, 64:65], 1.0), (), vS)
        P.step("vector", lambda e: e.memset(bndc[:], 0.0), (), vS)
        P.step("vector", lambda e: e.memset(epsc[:, 0:1], LN_EPS), (), vS)
        P.step("vector", lambda e: e.memset(epsc[:, 1:2], RMS_EPS), (), vS)
        cdone = vS.n

        def ln_stats_apply(xch, sq_dst, inb, mm_extra=(), mm_extra2=(), ebuf_guard=(),
                           g_guard=()):
            for c in range(NC):
                P.step("vector",
                       lambda e, c=c: e.tensor_mul(sq_dst(c), xch[c], xch[c]),
                       [inb[c]], vS)
            sqd = vS.n
            for c in range(NC):
                for nt in range(4):
                    P.step("tensor",
                           lambda e, c=c, nt=nt: e.matmul(
                               ps2[0:2, nt * NT:(nt + 1) * NT], lnT[:, 0:2],
                               xch[c][:, nt * NT:(nt + 1) * NT],
                               start=(c == 0), stop=False),
                           [inb[c], (sw, cw)] + list(mm_extra), None)
            for c in range(NC):
                for nt in range(4):
                    last = (c == NC - 1) and (nt == 3)
                    P.step("tensor",
                           lambda e, c=c, nt=nt, last=last: e.matmul(
                               ps2[0:2, nt * NT:(nt + 1) * NT], lnT[:, 2:4],
                               sq_dst(c)[:, nt * NT:(nt + 1) * NT],
                               start=False, stop=last),
                           [(vS, sqd)], tS if last else None)
            mmd = tS.n
            P.step("vector", lambda e: e.tensor_copy(ROWS[0:2, 0:T], ps2[0:2, :]),
                   [(tS, mmd)], vS)
            rowd = vS.n
            for nt in range(4):
                P.step("tensor", lambda e, nt=nt: e.matmul(
                    ps1[:, nt * NT:(nt + 1) * NT], b2[:],
                    ROWS[0:2, nt * NT:(nt + 1) * NT], start=True, stop=True),
                    [(vS, rowd)] + list(mm_extra2), tS if nt == 3 else None)
            mtd = tS.n
            for nt in range(4):
                P.step("tensor", lambda e, nt=nt: e.matmul(
                    ps2[:, nt * NT:(nt + 1) * NT], b2b[:],
                    ROWS[0:2, nt * NT:(nt + 1) * NT], start=True, stop=True),
                    [(vS, rowd)], tS if nt == 3 else None)
            qtd = tS.n
            P.step("scalar", lambda e: e.activation(e0, ps1[:], AF.Square),
                   [(tS, mtd)] + list(ebuf_guard), sS)
            sqd2 = sS.n
            P.step("vector", lambda e: e.tensor_sub(e1, ps2[:], e0),
                   [(tS, qtd), (sS, sqd2)] + list(ebuf_guard), vS)
            vd = vS.n
            P.step("scalar", lambda e: e.activation(e0, e1, AF.Ln, bias=epsc[:, 0:1]),
                   [(vS, vd)], sS)
            P.step("scalar", lambda e: e.activation(g(2), e0, AF.Exp, scale=-0.5),
                   list(g_guard), sS)
            sd = sS.n
            P.step("vector", lambda e: e.tensor_mul(g(3), ps1[:], g(2)),
                   [(sS, sd)] + list(g_guard), vS)
            return {"rows": rowd, "tiles": vS.n, "ps_used": tS.n}

        xcast = []
        for c in range(NC):
            stg = XS if c % 2 == 0 else YST
            wv = [(vS, xcast[c - 2])] if c > 1 else [(vS, cdone)]
            P.step("sync", lambda e, c=c, stg=stg: e.dma_start(out=stg[:], in_=x_in[:, c, :]),
                   wv, sw, 16)
            P.step("vector", lambda e, c=c, stg=stg: e.tensor_copy(Ach[c], stg[:]),
                   [(sw, sw.n)], vS)
            xcast.append(vS.n)
        Bsq = lambda c: B[:, c * T:(c + 1) * T]
        m1 = ln_stats_apply(Ach, Bsq, [(vS, xcast[c]) for c in range(NC)])
        for c in range(NC):
            P.step("vector", lambda e, c=c: e.tensor_mul(Bsq(c), Ach[c], g(2)),
                   [(vS, m1["tiles"])], vS)
            P.step("vector", lambda e, c=c: e.tensor_sub(Ach[c], Bsq(c), g(3)), (), vS)
        ln1d = vS.n
        dump("dbg_ln1", A[:], [(vS, ln1d)])

        P.step("sync", lambda e: e.dma_start(out=XS[0:1, 0:1024], in_=b_vrow),
               [(vS, ln1d)], sw, 16)
        P.step("vector", lambda e: e.tensor_copy(ROWS[0:1, 0:1024], XS[0:1, 0:1024]),
               [(sw, sw.n), (sb, sb.n)], vS)
        P.step("gpsimd", lambda e: e.dma_start(
            out=VBT[:], in_=ROWS[0:1, 0:1024].unsqueeze(1).to_broadcast([1, 128, 1024])),
            [(vS, vS.n)], sb, 16)
        vbtd = sb.n
        wvd = []
        for c in range(NC):
            load_wblock(("wv", c), w_qkv[:, c, 2048:3072], c % 2)
            P.step("vector", lambda e, c=c: e.tensor_copy(
                D[:, c * 1024:(c + 1) * 1024],
                WB[:, (c % 2) * 1024:((c % 2) + 1) * 1024]), [(si, wld[("wv", c)])], vS)
            wvd.append(vS.n)
            consume_wb(c % 2, vS, vS.n)
        psV = [ps1[:, 0:1024], ps1[:, 1024:2048], ps2[:, 0:1024], ps2[:, 1024:2048]]
        vevac = [0, 0, 0, 0]
        for tt in range(KT):
            reg = psV[tt % 4]
            extra = [(vS, m1["tiles"])]
            for c in range(NC):
                for nt in range(2):
                    last = (c == NC - 1) and (nt == 1)
                    P.step("tensor", lambda e, tt=tt, c=c, nt=nt, reg=reg, last=last: e.matmul(
                        reg[:, nt * NT:(nt + 1) * NT],
                        A[:, c * T + tt * 128: c * T + (tt + 1) * 128],
                        D[:, c * 1024 + nt * NT: c * 1024 + (nt + 1) * NT],
                        start=(c == 0), stop=last),
                        [(vS, wvd[c]), (vS, vevac[tt % 4]), (vS, ln1d)] + extra,
                        tS if last else None)
            mmd = tS.n
            P.step("vector", lambda e, tt=tt, reg=reg: e.tensor_add(
                Vv[:, tt, :, 0:64],
                reg[:].rearrange("p (h e) -> p h e", h=NH),
                VBT[:].rearrange("p (h e) -> p h e", h=NH)),
                [(tS, mmd), (sb, vbtd)], vS)
            vevac[tt % 4] = vS.n
        dump("dbg_vp", Vv[:, 0, :, :], [(vS, vS.n)])

        qsq = B[:, 2 * T:3 * T]
        ksq = B[:, 3 * T:4 * T]
        zr = lambda p: B[0:65, p * T:(p + 1) * T]
        zrl = lambda p: B[64:65, p * T:(p + 1) * T]
        LF = B[:, 4 * T:8 * T].bitcast(F32)
        sE = lambda k: ps1[:, (k % 2) * NT:(k % 2 + 1) * NT]
        sO = lambda k: ps1[:, 1024 + (k % 2) * NT:1024 + (k % 2 + 1) * NT]
        zE = ps2[0:65, 0:NT]
        zO = ps2[0:65, NT:2 * NT]
        FA = ps2[:, 1024:2048]
        FAh = [ps2[:, 1024:1536], ps2[:, 1536:2048]]

        last_zevac = vS.n
        last_div = {0: vS.n, 1: vS.n}
        pv_slot = {}
        exp_bank = {0: sS.n, 1: sS.n}
        sub_free = [(vS, 0), (vS, 0)]
        fa_free = (vS, 0)
        bcol_n = {}
        recs_n = {}
        lin_n = [0]
        appd_n = {}
        rows_n = {}
        chain_out = {}

        def _sub(idx, sem, cnt):
            sub_free[idx] = (sem, cnt)

        sq_n = {}

        def pk_qkv_group(p, side, nt):
            sub = nt % 2
            reg = FAh[sub]
            wkey = ("q", p) if side == 0 else ("k", p)
            fraw = f(0) if side == 0 else f(1)
            sq = qsq if side == 0 else ksq
            bias = bqc[:, p:p + 1] if side == 0 else bqc[:, 8 + p:9 + p]

            def emit():
                for c in range(NC):
                    last = c == NC - 1
                    P.step("tensor", lambda e, c=c, last=last: e.matmul(
                        reg, wbv[side][:, c, :],
                        A[:, c * T + nt * NT: c * T + (nt + 1) * NT],
                        start=(c == 0), stop=last),
                        [(si, wld[wkey]), sub_free[sub]], tS if last else None)
                mmn = tS.n
                if side == 1 and nt == 3:
                    consume_wb(1, tS, mmn)
                if side == 0 and nt == 3:
                    consume_wb(0, tS, mmn)
                P.step("vector", lambda e: e.tensor_scalar_add(
                    fraw[:, nt * NT:(nt + 1) * NT], reg, bias), [(tS, mmn)], vS)
                P.step("vector", lambda e: e.tensor_mul(
                    sq[:, nt * NT:(nt + 1) * NT], fraw[:, nt * NT:(nt + 1) * NT],
                    fraw[:, nt * NT:(nt + 1) * NT]), (), vS)
                _sub(sub, vS, vS.n)
                sq_n[(side, nt)] = vS.n
            return emit

        def pk_stats(p, side):
            sq = qsq if side == 0 else ksq
            lh = lhq if side == 0 else lhk
            off = 0 if side == 0 else T

            def emit():
                for nt in range(4):
                    sub = nt % 2
                    P.step("tensor", lambda e, nt=nt, sub=sub: e.matmul(
                        FAh[sub][0:4, :], lh[:],
                        sq[:, nt * NT:(nt + 1) * NT], start=True, stop=True),
                        [sub_free[sub], (vS, sq_n[(side, nt)])], tS)
                    mmn = tS.n
                    P.step("vector", lambda e, nt=nt, sub=sub: e.tensor_copy(
                        ROWS[0:4, off + nt * NT: off + (nt + 1) * NT], FAh[sub][0:4, :]),
                        [(tS, mmn)], vS)
                    _sub(sub, vS, vS.n)
                rows_n[(p, side)] = vS.n
            return emit

        def pk_chain_half(p, side, hh):
            off = 0 if side == 0 else T
            sclg = g(0) if side == 0 else g(1)
            nrm = g(2) if side == 0 else g(3)
            wcol = wqc if side == 0 else wkc
            stg = XS if side == 0 else YST

            def emit():
                for i2 in range(2):
                    P.step("tensor", lambda e, i2=i2: e.matmul(
                        FAh[i2], i4r[:],
                        ROWS[0:4, off + hh * 1024 + i2 * NT: off + hh * 1024 + (i2 + 1) * NT],
                        start=True, stop=True),
                        [sub_free[i2], (vS, rows_n[(p, side)])], tS if i2 else None)
                mtn = tS.n
                P.step("scalar", lambda e: e.activation(stg[:, 0:1024], FA, AF.Ln,
                                                        bias=epsc[:, 1:2]), [(tS, mtn)], sS)
                P.step("scalar", lambda e: e.activation(stg[:, 1024:2048], stg[:, 0:1024],
                                                        AF.Exp, scale=-0.5), (), sS)
                lnn = sS.n
                for i2 in range(2):
                    P.step("tensor", lambda e, i2=i2: e.matmul(
                        FAh[i2], i4n[:],
                        ROWS[0:4, off + hh * 1024 + i2 * NT: off + hh * 1024 + (i2 + 1) * NT],
                        start=True, stop=True),
                        [(sS, lnn)], tS if i2 else None)
                ntn = tS.n
                P.step("vector", lambda e: e.tensor_mul(
                    sclg[:, hh * 1024:(hh + 1) * 1024], stg[:, 1024:2048], FA),
                    [(tS, ntn)], vS)
                gw = [(si, si.n)] if side == 1 else []
                P.step("vector", lambda e: e.tensor_mul(
                    nrm[:, hh * 1024:(hh + 1) * 1024],
                    sclg[:, hh * 1024:(hh + 1) * 1024], stg[:, 1024:2048]), gw, vS)
                P.step("vector", lambda e: e.tensor_scalar_mul(
                    sclg[:, hh * 1024:(hh + 1) * 1024], stg[:, 1024:2048], wcol[:, 0:1]),
                    (), vS)
                _sub(0, vS, vS.n)
                _sub(1, vS, vS.n)
                chain_out[(p, side, hh)] = vS.n
            return emit

        def pk_bias(p):
            par = p % 2

            def emit():
                w0 = [(vS, chain_out[(p, 0, 0)]), (vS, chain_out[(p, 0, 1)]),
                      (vS, chain_out[(p, 1, 0)]), (vS, chain_out[(p, 1, 1)])]
                P.step("vector", lambda e: e.reduce_max(CCOL[:, 0:1], g(2), axis=AX),
                       w0, vS)
                P.step("vector", lambda e: e.reduce_max(CCOL[:, 1:2], g(3), axis=AX),
                       (), vS)
                vmax = vS.n
                P.step("scalar", lambda e: e.activation(CCOL[:, 4:6], CCOL[:, 0:2], AF.Ln),
                       [(vS, vmax)], sS)
                lnn = sS.n
                P.step("vector", lambda e: e.reduce_sum(CCOL[:, 6:7], CCOL[:, 4:6],
                                                        axis=AX), [(sS, lnn)], vS)
                vsum = vS.n
                P.step("scalar", lambda e: e.activation(CCOL[:, 2:3], CCOL[:, 6:7],
                                                        AF.Exp, scale=0.5), [(vS, vsum)], sS)
                expn = sS.n
                pb = bcol_n.get(p - 2, (0, 0))[0]
                P.step("vector", lambda e: e.tensor_scalar(CCOL[:, 3:4], CCOL[:, 2:3],
                                                           -1.0, BIAS_SHIFT,
                                                           ALU.mult, ALU.add),
                       [(sS, expn), (sb, pb)], vS)
                biasd = vS.n
                P.step("sync", lambda e, par=par: e.dma_start(
                    out=bcole[:, par:par + 1],
                    in_=CCOL[0:1, 3:4].unsqueeze(1).to_broadcast([1, 128, 1])),
                    [(vS, biasd)], sb, 16)
                P.step("sync", lambda e, par=par: e.dma_start(
                    out=bcolo[:, par:par + 1],
                    in_=CCOL[64:65, 3:4].unsqueeze(1).to_broadcast([1, 128, 1])),
                    [(vS, biasd)], sb, 16)
                bcol_n[p] = (sb.n, par)
                if p == 0 and "dbg_bcol" in dbg_t:
                    dump("dbg_bcol", CCOL[:, 0:6], [(vS, biasd)])
            return emit

        def pk_divides(jp):
            def emit():
                P.step("vector", lambda e: e.memset(G[64:65, 2 * T:2 * T + 64], 1.0),
                       [(vS, recs_n[jp])], vS)
                onesd = vS.n
                for h2 in range(2):
                    for hh in range(2):
                        for i2 in range(2):
                            P.step("tensor", lambda e, hh=hh, i2=i2, h2=h2: e.matmul(
                                FAh[i2][0:64, :], G[64:65, 2 * T:2 * T + 64],
                                ROWS[64:65, h2 * T + hh * 1024 + i2 * NT:
                                     h2 * T + hh * 1024 + (i2 + 1) * NT],
                                start=True, stop=True),
                                [(vS, onesd), sub_free[i2]], tS if i2 else None)
                        linn = tS.n
                        lin_n[0] = linn
                        if h2 == 0:
                            P.step("vector", lambda e, jp=jp, hh=hh: e.tensor_mul(
                                zb[0:64, jp, hh * 1024:(hh + 1) * 1024],
                                zr(0)[0:64, hh * 1024:(hh + 1) * 1024], FA[0:64, :]),
                                [(tS, linn)], vS)
                            last_div[0] = vS.n
                        else:
                            P.step("vector", lambda e, jp=jp, hh=hh: e.tensor_mul(
                                g(3)[0:64, hh * 1024:(hh + 1) * 1024],
                                zr(1)[0:64, hh * 1024:(hh + 1) * 1024], FA[0:64, :]),
                                [(tS, linn), (si, si.n)], vS)
                            last_div[1] = vS.n
                        _sub(0, vS, vS.n)
                        _sub(1, vS, vS.n)
                P.step("gpsimd", lambda e, jp=jp: e.dma_start(
                    out=zb[64:128, jp, :], in_=g(3)[0:64, :]),
                    [(vS, last_div[1])], si, 16)
            return emit

        def emit_attention(j, packets):
            nonlocal last_zevac
            pki = [0]

            def pop(n=1):
                for _ in range(n):
                    if pki[0] < len(packets):
                        packets[pki[0]]()
                        pki[0] += 1

            par = bcol_n[j][1]
            bcE = bcole[:, par:par + 1]
            bcO = bcolo[:, par:par + 1]
            hE, hO = 2 * j, 2 * j + 1
            zfree = {0: (vS, last_zevac), 1: (vS, last_zevac)}
            expn = {}
            for qq in range(4):
                for kt in range(KT):
                    swl = [(sS, exp_bank[kt % 2]), (vS, appd_n[j]),
                           (sb, bcol_n[j][0])]
                    P.step("tensor", lambda e, kt=kt, qq=qq: e.matmul(
                        sE(kt), f(3)[0:64, kt * 128:(kt + 1) * 128],
                        f(2)[0:64, qq * NT:(qq + 1) * NT],
                        start=True, stop=True, tile_position=(0, 0)), swl, None)
                    P.step("tensor", lambda e, kt=kt, qq=qq: e.matmul(
                        sO(kt), f(3)[64:128, kt * 128:(kt + 1) * 128],
                        f(2)[64:128, qq * NT:(qq + 1) * NT],
                        start=True, stop=True, tile_position=(64, 0)), (), tS)
                    smm = tS.n
                    ew = [(tS, smm)]
                    if (0, kt % 4) in pv_slot:
                        ew.append((tS, pv_slot[(0, kt % 4)]))
                    P.step("scalar", lambda e, kt=kt, bcE=bcE: e.activation(
                        ptE(kt), sE(kt), AF.Exp, bias=bcE), ew, sS)
                    ew2 = []
                    if (1, kt % 4) in pv_slot:
                        ew2.append((tS, pv_slot[(1, kt % 4)]))
                    P.step("scalar", lambda e, kt=kt, bcO=bcO: e.activation(
                        ptO(kt), sO(kt), AF.Exp, bias=bcO), ew2, sS)
                    exp_bank[kt % 2] = sS.n
                    expn[kt] = sS.n
                    if kt > 1:
                        kp = kt - 2
                        pwE = [(sS, expn[kp])]
                        if kp == 0:
                            pwE.append(zfree[0])
                            pwE.append(zfree[1])
                        P.step("tensor", lambda e, kp=kp, qq=qq, hE=hE: e.matmul(
                            zE, Vv[:, kp, hE, :], ptE(kp),
                            start=(kp == 0), stop=False), pwE, None)
                        P.step("tensor", lambda e, kp=kp, qq=qq, hO=hO: e.matmul(
                            zO, Vv[:, kp, hO, :], ptO(kp),
                            start=(kp == 0), stop=False), (), tS)
                        pv_slot[(0, kp % 4)] = tS.n
                        pv_slot[(1, kp % 4)] = tS.n
                    if kt >= 5 and kt % 2 == 1:
                        pop()
                for kp in (KT - 2, KT - 1):
                    P.step("tensor", lambda e, kp=kp, qq=qq, hE=hE: e.matmul(
                        zE, Vv[:, kp, hE, :], ptE(kp),
                        start=False, stop=(kp == KT - 1)), [(sS, expn[kp])], None)
                    P.step("tensor", lambda e, kp=kp, qq=qq, hO=hO: e.matmul(
                        zO, Vv[:, kp, hO, :], ptO(kp),
                        start=False, stop=(kp == KT - 1)), (), tS)
                    pv_slot[(0, kp % 4)] = tS.n
                    pv_slot[(1, kp % 4)] = tS.n
                pvall = tS.n
                P.step("vector", lambda e, qq=qq: e.tensor_copy(
                    zr(0)[:, qq * NT:(qq + 1) * NT], zE),
                    [(tS, pvall), (vS, last_div[0])], vS)
                zfree[0] = (vS, vS.n)
                P.step("vector", lambda e, qq=qq: e.tensor_copy(
                    zr(1)[:, qq * NT:(qq + 1) * NT], zO),
                    [(vS, last_div[1])], vS)
                zfree[1] = (vS, vS.n)
                last_zevac = vS.n

                P.step("vector", lambda e, qq=qq: e.tensor_copy(
                    LF[64:65, qq * NT:(qq + 1) * NT], ps2[64:65, 0:NT]), (), vS)
                P.step("vector", lambda e, qq=qq: e.tensor_copy(
                    LF[64:65, 2048 + qq * NT: 2048 + (qq + 1) * NT],
                    ps2[64:65, NT:2 * NT]), (), vS)
                zfree[0] = (vS, vS.n)
                zfree[1] = (vS, vS.n)
                for part, off in ((0, qq * NT), (1, 2048 + qq * NT)):
                    def _rc(e, off=off, part=part, qq=qq):
                        with nc.allow_low_precision(reason="softmax denom to bf16"):
                            return e.reciprocal(
                                ROWS[64:65, part * T + qq * NT: part * T + (qq + 1) * NT],
                                LF[64:65, off:off + NT])
                    P.step("vector", _rc,
                           [(tS, lin_n[0])] if (qq == 0 and part == 0) else (), vS)
            if j == 0:
                dump("dbg_zraw", zr(0), [(vS, last_zevac)])
            recs_n[j] = vS.n
            pop(len(packets))

        load_wblock(("q", 0), w_qkv[:, :, 0:128], 0)
        load_wblock(("k", 0), w_qkv[:, :, 1024:1152], 1)
        sub_free = [(vS, vevac[3]), (vS, vevac[3])]
        for side in range(2):
            for nt in range(4):
                pk_qkv_group(0, side, nt)()
            pk_stats(0, side)()
        load_wblock(("q", 1), w_qkv[:, :, 128:256], 0)
        load_wblock(("k", 1), w_qkv[:, :, 1024 + 128:1024 + 256], 1)
        for side in range(2):
            for hh in range(2):
                pk_chain_half(0, side, hh)()
        pk_bias(0)()
        if "dbg_qr" in dbg_t:
            dump("dbg_qr", f(0), [(vS, vS.n)])
            dump("dbg_lhq", lhq[:], [(vS, cdone)])
            dump("dbg_rows", ROWS[:], [(vS, rows_n[(0, 1)])])
        P.step("vector", lambda e: e.tensor_mul(f(2), f(0), g(0)),
               [(vS, chain_out[(0, 0, 1)]), (vS, chain_out[(0, 1, 1)])], vS)
        P.step("vector", lambda e: e.tensor_mul(f(3), f(1), g(1)), (), vS)
        appd_n[0] = vS.n
        if "dbg_q" in dbg_t:
            dump("dbg_q", f(2), [(vS, appd_n[0])])
            dump("dbg_sclq", g(0), [(vS, appd_n[0])])

        for j in range(NP):
            packets = []
            if j > 0:
                packets.append(pk_divides(j - 1))
            if j + 1 < NP:
                p = j + 1
                for side in range(2):
                    for nt in range(4):
                        packets.append(pk_qkv_group(p, side, nt))
                    packets.append(pk_stats(p, side))
                for side in range(2):
                    for hh in range(2):
                        packets.append(pk_chain_half(p, side, hh))
                packets.append(pk_bias(p))
            emit_attention(j, packets)

            if j + 1 < NP:
                if j + 2 < NP:
                    load_wblock(("q", j + 2), w_qkv[:, :, (j + 2) * 128:(j + 3) * 128], 0)
                    load_wblock(("k", j + 2),
                                w_qkv[:, :, 1024 + (j + 2) * 128:1024 + (j + 3) * 128], 1)
                else:
                    load_wblock(("o", 0), w_o[:, :, 0:128], 0)
                    load_wblock(("o", 1), w_o[:, :, 128:256], 1)
                lastS = tS.n
                P.step("vector", lambda e: e.tensor_mul(f(2), f(0), g(0)),
                       [(vS, chain_out[(j + 1, 0, 1)]), (vS, chain_out[(j + 1, 1, 1)]),
                        (tS, lastS)], vS)
                P.step("vector", lambda e: e.tensor_mul(f(3), f(1), g(1)), (), vS)
                appd_n[j + 1] = vS.n

        pk_divides(NP - 1)()
        zdone_v = vS.n
        zdone_i = si.n
        pv_final = tS.n
        dump("dbg_zbuf", D[:], [(vS, zdone_v), (si, zdone_i)])

        psW = [ps1, ps2]
        woev = [sS.n, sS.n]
        for o in range(NC):
            reg = psW[o % 2]
            basew = [(si, wld[("o", o)]), (vS, zdone_v), (si, zdone_i),
                     (sS, woev[o % 2])]
            if o < 2:
                basew += [(sS, exp_bank[0]), (sS, exp_bank[1])]
            for c in range(NC):
                for nt in range(4):
                    last = (c == NC - 1) and (nt == 3)
                    P.step("tensor", lambda e, o=o, c=c, nt=nt, reg=reg, last=last: e.matmul(
                        reg[:, nt * NT:(nt + 1) * NT], wbv[o % 2][:, c, :],
                        zb[:, c, nt * NT:(nt + 1) * NT], start=(c == 0), stop=last),
                        basew, tS if last else None)
            mmd = tS.n
            consume_wb(o % 2, tS, mmd)
            if o + 2 < NC:
                load_wblock(("o", o + 2), w_o[:, :, (o + 2) * 128:(o + 3) * 128], o % 2)
            elif o + 2 == NC:
                load_wblock(("f", 0, 0, 0), w_r[:, :, 0:128], 0)
            elif o + 2 == NC + 1:
                load_wblock(("f", 0, 0, 1), w_r[:, :, 128:256], 1)
            P.step("scalar", lambda e, o=o, reg=reg: e.activation(
                B[:, o * T:(o + 1) * T], reg[:], AF.Identity, bias=boc[:, o:o + 1]),
                [(tS, mmd)], sS)
            woev[o % 2] = sS.n
        aod = sS.n
        dump("dbg_ao", B[:], [(sS, aod)])

        x1c = []
        for c in range(NC):
            wv = [(vS, x1c[c - 1])] if c > 0 else [(sS, aod)]
            P.step("sync", lambda e, c=c: e.dma_start(out=XS[:], in_=x_in[:, c, :]),
                   wv, sw, 16)
            P.step("vector", lambda e, c=c: e.tensor_add(Ach[c], XS[:],
                                                         B[:, c * T:(c + 1) * T]),
                   [(sw, sw.n), (sS, aod)], vS)
            x1c.append(vS.n)
        Dsq = lambda c: D[:, c * T:(c + 1) * T]
        m2 = ln_stats_apply(Ach, Dsq, [(vS, x1c[c]) for c in range(NC)],
                            mm_extra=[(sS, aod)],
                            mm_extra2=[(sS, aod)],
                            ebuf_guard=[(tS, pv_final)],
                            g_guard=[(vS, zdone_v), (si, zdone_i)])
        ln2tiles = m2["tiles"]

        ln2v = D[:, 0:8192].rearrange("p (c t) -> p c t", c=NC, t=1024)
        xrv = D[:, 8192:16384].rearrange("p (c t) -> p c t", c=NC, t=1024)
        xkv = C[:, 0:8192].rearrange("p (c t) -> p c t", c=NC, t=1024)
        rv = C[:, 8192:16384].rearrange("p (c t) -> p c t", c=NC, t=1024)
        kav = Fb[:, 0:8192].rearrange("p (c t) -> p c t", c=NC, t=1024)
        psF = [ps1[:, 0:1024], ps1[:, 1024:2048], ps2[:, 0:1024], ps2[:, 1024:2048]]
        ffev = [(vS.n, sS.n)] * 4
        ydma_slot = [0, 0]
        xsfree = vS.n
        revac = {}
        prev_mix = vS.n
        for hh in range(2):
            t0 = hh * 1024
            apl = []
            for c in range(NC):
                aw = [(vS, ln2tiles), (vS, ffev[0][0]), (vS, ffev[1][0]),
                      (vS, ffev[2][0]), (vS, ffev[3][0]), (vS, prev_mix)]
                P.step("vector", lambda e, c=c, t0=t0: e.tensor_mul(
                    VBT[:], Ach[c][:, t0:t0 + 1024],
                    G[:, 2 * T + t0: 2 * T + t0 + 1024]), aw, vS)
                P.step("vector", lambda e, c=c, t0=t0: e.tensor_sub(
                    ln2v[:, c, :], VBT[:], G[:, 3 * T + t0: 3 * T + t0 + 1024]),
                    (), vS)
                apl.append(vS.n)
            if hh == 0:
                dump("dbg_ln2", D[:, 0:8192], [(vS, vS.n)])
            mixd = []
            for c in range(NC):
                for (xv, mu, om) in [(xrv, murc, omrc), (xkv, mukc, omkc)]:
                    P.step("vector", lambda e, c=c, mu=mu: e.tensor_scalar_mul(
                        VBT[:], ln2v[:, c, :], mu[:, c:c + 1]), [(vS, apl[c])], vS)
                    P.step("vector", lambda e, c=c, xv=xv, om=om: e.scalar_tensor_tensor(
                        xv[:, c, 1:1024], ln2v[:, c, 0:1023], om[:, c:c + 1],
                        VBT[:, 1:1024], ALU.mult, ALU.add), (), vS)
                    P.step("vector", lambda e, c=c, xv=xv, om=om: e.scalar_tensor_tensor(
                        xv[:, c, 0:1], bndc[:, c:c + 1], om[:, c:c + 1],
                        VBT[:, 0:1], ALU.mult, ALU.add), (), vS)
                mixd.append(vS.n)
            prev_mix = vS.n
            if hh == 0:
                P.step("vector", lambda e: e.tensor_copy(
                    bndc[:], ln2v[:, :, 1023:1024].rearrange("p c t -> p (c t)")),
                    (), vS)
                prev_mix = vS.n
                dump("dbg_xr", D[:, 8192:16384], [(vS, mixd[-1])])
            matsrc = {0: w_r, 1: w_k2, 2: w_v2}

            def next_block(hh, mat, o):
                if o + 1 < NC:
                    return (hh, mat, o + 1)
                if mat + 1 <= 2:
                    return (hh, mat + 1, 0)
                if hh == 0:
                    return (1, 0, 0)
                return None

            def prefetch_after(hh, mat, o):
                nb = next_block(hh, mat, o)
                if nb is None:
                    return
                nb2 = next_block(*nb)
                if nb2 is None:
                    return
                h2_, m2_, o2_ = nb2
                if ("f", h2_, m2_, o2_) in wld:
                    return
                load_wblock(("f", h2_, m2_, o2_),
                            matsrc[m2_][:, :, o2_ * 128:(o2_ + 1) * 128], o % 2)

            if hh == 0:
                pass
            for mat, (wsrc, xv, kind) in enumerate([(w_r, xrv, "r"), (w_k2, xkv, "k")]):
                for o in range(NC):
                    slot = o % 2
                    reg = psF[2 * mat + (o % 2)]
                    ri = 2 * mat + (o % 2)
                    basew = [(si, wld[("f", hh, mat, o)]), (vS, ffev[ri][0]),
                             (sS, ffev[ri][1]), (vS, mixd[-1])]
                    if hh == 0 and mat == 0 and o < 2:
                        basew += [(sS, exp_bank[0]), (sS, exp_bank[1]), (sS, aod)]
                    for c in range(NC):
                        for nt in range(2):
                            last = (c == NC - 1) and (nt == 1)
                            P.step("tensor", lambda e, c=c, nt=nt, reg=reg, slot=slot,
                                   xv=xv, last=last: e.matmul(
                                       reg[:, nt * NT:(nt + 1) * NT], wbv[slot][:, c, :],
                                       xv[:, c, nt * NT:(nt + 1) * NT],
                                       start=(c == 0), stop=last),
                                   basew, tS if last else None)
                    mmd = tS.n
                    consume_wb(slot, tS, mmd)
                    prefetch_after(hh, mat, o)
                    if kind == "r":
                        P.step("scalar", lambda e, o=o, reg=reg: e.activation(
                            rv[:, o, :], reg[:], AF.Sigmoid), [(tS, mmd)], sS)
                        revac[(hh, o)] = sS.n
                        ffev[ri] = (vS.n, sS.n)
                    else:
                        P.step("vector", lambda e, o=o, reg=reg: e.tensor_scalar_max(
                            VBT[:], reg[:], 0.0), [(tS, mmd)], vS)
                        P.step("vector", lambda e, o=o: e.tensor_mul(
                            kav[:, o, :], VBT[:], VBT[:]), (), vS)
                        ffev[ri] = (vS.n, sS.n)
            kall = vS.n
            if hh == 0:
                dump("dbg_r", C[:, 8192:16384], [(sS, sS.n)])
                dump("dbg_ka", Fb[:, 0:8192], [(vS, kall)])
            for o in range(NC):
                slot = o % 2
                reg = psF[2 + (o % 2)]
                ri = 2 + (o % 2)
                basew = [(si, wld[("f", hh, 2, o)]), (vS, ffev[ri][0]),
                         (sS, ffev[ri][1]), (vS, kall)]
                for c in range(NC):
                    for nt in range(2):
                        last = (c == NC - 1) and (nt == 1)
                        P.step("tensor", lambda e, c=c, nt=nt, reg=reg, slot=slot,
                               last=last: e.matmul(
                                   reg[:, nt * NT:(nt + 1) * NT], wbv[slot][:, c, :],
                                   kav[:, c, nt * NT:(nt + 1) * NT],
                                   start=(c == 0), stop=last),
                               basew, tS if last else None)
                mmd = tS.n
                consume_wb(slot, tS, mmd)
                prefetch_after(hh, 2, o)
                P.step("sync", lambda e, o=o, t0=t0: e.dma_start(
                    out=XS[:, 0:1024], in_=x_in[:, o, t0:t0 + 1024]),
                    [(vS, xsfree)], sw, 16)
                xld = sw.n
                par = o % 2
                yreg = YST[:, par * 1024:(par + 1) * 1024]
                P.step("vector", lambda e, o=o, hh=hh, reg=reg, yreg=yreg:
                       e.scalar_tensor_tensor(yreg, reg[:], 1.0, rv[:, o, :],
                                              ALU.mult, ALU.mult),
                       [(tS, mmd), (so, ydma_slot[par]), (sS, revac[(hh, o)])], vS)
                P.step("vector", lambda e, o=o, t0=t0, yreg=yreg: e.tensor_add(
                    XS[:, 1024:2048], yreg, B[:, o * T + t0: o * T + t0 + 1024]),
                    (), vS)
                P.step("vector", lambda e, yreg=yreg: e.tensor_add(
                    yreg, XS[:, 1024:2048], XS[:, 0:1024]), [(sw, xld)], vS)
                xsfree = vS.n
                find = vS.n
                P.step("sync", lambda e, o=o, t0=t0, yreg=yreg: e.dma_start(
                    out=y_out[:, o, t0:t0 + 1024], in_=yreg), [(vS, find)], so, 16)
                ydma_slot[par] = so.n
                ffev[ri] = (vS.n, sS.n)
        P.step("sync", lambda e, n=so.n: e.wait_ge(so.h, n), (), None)

        @block.sync
        def _(e):
            for fn in P.ops["sync"]:
                fn(e)

        @block.gpsimd
        def _(e):
            for fn in P.ops["gpsimd"]:
                fn(e)

        @block.scalar
        def _(e):
            for fn in P.ops["scalar"]:
                fn(e)

        @block.vector
        def _(e):
            for fn in P.ops["vector"]:
                fn(e)

        @block.tensor
        def _(e):
            for fn in P.ops["tensor"]:
                fn(e)

    return nc


def _prep_inputs(L0_attn_x, L0_attn_Wqkv, L0_attn_bqkv, L0_attn_Wo, L0_attn_bo,
                 L0_attn_wq, L0_attn_wk, L0_ffn_W_r, L0_ffn_W_k, L0_ffn_W_v,
                 L0_ffn_mu_r, L0_ffn_mu_k):
    f32 = np.float32
    bf16 = ml_dtypes.bfloat16
    x = np.asarray(L0_attn_x, f32)

    def fm(w):
        wt = np.ascontiguousarray(np.asarray(w, f32).T)
        return np.ascontiguousarray(wt.reshape(8, 128, -1).transpose(1, 0, 2))

    wqkv = fm(L0_attn_Wqkv)
    wo = fm(L0_attn_Wo)
    wr = fm(L0_ffn_W_r)
    wk2 = fm(L0_ffn_W_k)
    wv2 = fm(L0_ffn_W_v)
    bqkv = np.asarray(L0_attn_bqkv, f32)
    b_qkv = np.ascontiguousarray(bqkv.reshape(24, 128).T)
    b_vrow = np.ascontiguousarray(bqkv[2048:3072].reshape(1, 1024))
    b_o = np.ascontiguousarray(np.asarray(L0_attn_bo, f32).reshape(8, 128).T)
    mu_r = np.ascontiguousarray(np.asarray(L0_ffn_mu_r, f32).reshape(8, 128).T)
    mu_k = np.ascontiguousarray(np.asarray(L0_ffn_mu_k, f32).reshape(8, 128).T)
    wq_col = np.ascontiguousarray(np.tile(np.asarray(L0_attn_wq, f32), 2).reshape(128, 1))
    wk_col = np.ascontiguousarray(np.tile(np.asarray(L0_attn_wk, f32), 2).reshape(128, 1))
    cst_ln = np.zeros((128, 4), bf16)
    cst_ln[:, 0] = bf16(1.0 / 1024.0)
    cst_ln[:, 3] = bf16(1.0 / 1024.0)
    cst_ind = np.zeros((128, 4), bf16)
    cst_ind[0:64, 0] = bf16(1.0 / 64.0)
    cst_ind[64:128, 1] = bf16(1.0 / 64.0)
    cst_b2 = np.zeros((2, 128), bf16)
    cst_b2[0, :] = bf16(1.0)
    cst_b2b = np.zeros((2, 128), bf16)
    cst_b2b[1, :] = bf16(1.0)
    cst_i4r = np.zeros((4, 128), bf16)
    cst_i4r[0, 0:64] = bf16(1.0)
    cst_i4r[1, 64:128] = bf16(1.0)
    cst_i4n = np.zeros((4, 128), bf16)
    cst_i4n[2, 0:64] = bf16(1.0)
    cst_i4n[3, 64:128] = bf16(1.0)

    shared = dict(w_qkv=wqkv, w_o=wo, w_r=wr, w_k2=wk2, w_v2=wv2, b_qkv=b_qkv,
                  b_vrow=b_vrow, b_o=b_o, mu_r_c=mu_r, mu_k_c=mu_k,
                  wq_col=wq_col, wk_col=wk_col, cst_ln=cst_ln, cst_ind=cst_ind,
                  cst_b2=cst_b2, cst_b2b=cst_b2b, cst_i4r=cst_i4r, cst_i4n=cst_i4n)
    in_maps = []
    for b in range(8):
        xb = np.ascontiguousarray(x[b].T.reshape(8, 128, T).transpose(1, 0, 2))
        m = dict(shared)
        m["x_in"] = xb
        in_maps.append(m)
    return in_maps


def kernel(**inputs):
    in_maps = _prep_inputs(**inputs)
    nc = build_program()
    res = run_bass_kernel_spmd(nc, in_maps, core_ids=list(range(8)))
    outs = []
    for b in range(8):
        yb = res.results[b]["y"]
        outs.append(yb.transpose(1, 0, 2).reshape(1024, T).T)
    return np.ascontiguousarray(np.stack(outs, axis=0), dtype=np.float32)


# version 11
# speedup vs baseline: 1.1610x; 1.0689x over previous
import contextlib
import numpy as np
import ml_dtypes
import concourse.bass as bass
import concourse.mybir as mybir
from concourse.bass_utils import run_bass_kernel_spmd

F32 = mybir.dt.float32
BF16 = mybir.dt.bfloat16
AX = mybir.AxisListType.X
AF = mybir.ActivationFunctionType
ALU = mybir.AluOpType

T = 2048
NC = 8
NT = 512
NH = 16
HD = 64
NP = 8
KT = 16
LN_EPS = 1e-5
RMS_EPS = 1e-6
BIAS_SHIFT = 30.0


class Sem:
    def __init__(self, h, owner):
        self.h, self.owner, self.n = h, owner, 0


class Plan:
    ENGS = ("sync", "gpsimd", "scalar", "vector", "tensor")

    def __init__(self):
        self.ops = {e: [] for e in self.ENGS}
        self.last_wait = {e: {} for e in self.ENGS}

    def step(self, eng, emitfn, waits=(), inc=None, inc_amt=1):
        lw = self.last_wait[eng]
        ws = []
        for s, n in waits:
            if s.owner == eng or n <= 0:
                continue
            if lw.get(id(s), -1) >= n:
                continue
            assert n <= s.n, f"wait on future count {n} > {s.n} ({eng})"
            lw[id(s)] = n
            ws.append((s, n))
        if inc is not None:
            inc.n += inc_amt

        def f(e):
            for s, n in ws:
                e.wait_ge(s.h, n)
            ins = emitfn(e)
            if inc is not None:
                ins.then_inc(inc.h, inc_amt)

        self.ops[eng].append(f)


def build_program(dbg=frozenset()):
    nc = bass.Bass("TRN2", target_bir_lowering=False)

    x_in = nc.dram_tensor("x_in", [128, NC, T], F32, kind="ExternalInput").ap()
    w_qkv_qk = nc.dram_tensor("w_qkv_qk", [128, 16, 1024], F32, kind="ExternalInput").ap()
    w_qkv_v = nc.dram_tensor("w_qkv_v", [128, NC, 1024], F32, kind="ExternalInput").ap()
    w_o = nc.dram_tensor("w_o", [128, NC, 1024], F32, kind="ExternalInput").ap()
    w_r = nc.dram_tensor("w_r", [128, NC, 1024], F32, kind="ExternalInput").ap()
    w_k2 = nc.dram_tensor("w_k2", [128, NC, 1024], F32, kind="ExternalInput").ap()
    w_v2 = nc.dram_tensor("w_v2", [128, NC, 1024], F32, kind="ExternalInput").ap()
    b_qkv = nc.dram_tensor("b_qkv", [128, 24], F32, kind="ExternalInput").ap()
    b_vrow = nc.dram_tensor("b_vrow", [1, 1024], F32, kind="ExternalInput").ap()
    b_o = nc.dram_tensor("b_o", [128, 8], F32, kind="ExternalInput").ap()
    mu_r_c = nc.dram_tensor("mu_r_c", [128, 8], F32, kind="ExternalInput").ap()
    mu_k_c = nc.dram_tensor("mu_k_c", [128, 8], F32, kind="ExternalInput").ap()
    wq_col = nc.dram_tensor("wq_col", [128, 1], F32, kind="ExternalInput").ap()
    wk_col = nc.dram_tensor("wk_col", [128, 1], F32, kind="ExternalInput").ap()
    cst_ln = nc.dram_tensor("cst_ln", [128, 4], BF16, kind="ExternalInput").ap()
    cst_b2 = nc.dram_tensor("cst_b2", [2, 128], BF16, kind="ExternalInput").ap()
    cst_b2b = nc.dram_tensor("cst_b2b", [2, 128], BF16, kind="ExternalInput").ap()
    cst_i4r = nc.dram_tensor("cst_i4r", [4, 128], BF16, kind="ExternalInput").ap()
    cst_i4n = nc.dram_tensor("cst_i4n", [4, 128], BF16, kind="ExternalInput").ap()
    cst_ind = nc.dram_tensor("cst_ind", [128, 4], BF16, kind="ExternalInput").ap()

    y_out = nc.dram_tensor("y", [128, NC, T], F32, kind="ExternalOutput").ap()
    dbg_t = {}
    for name, shape, dt in [
        ("dbg_ln1", [128, NC * T], BF16),
        ("dbg_q", [128, T], BF16),
        ("dbg_sclq", [128, T], BF16),
        ("dbg_pt", [128, T], BF16),
        ("dbg_zraw", [65, T], BF16),
        ("dbg_zbuf", [128, NC * T], BF16),
        ("dbg_ao", [128, NC * T], BF16),
        ("dbg_vp", [128, NH * 65], BF16),
        ("dbg_ln2", [128, NC * 1024], BF16),
        ("dbg_xr", [128, NC * 1024], BF16),
        ("dbg_r", [128, NC * 1024], BF16),
        ("dbg_ka", [128, NC * 1024], BF16),
        ("dbg_bcol", [128, 6], F32),
        ("dbg_qr", [128, T], BF16),
        ("dbg_rows", [128, 2 * T], BF16),
        ("dbg_lhq", [128, 4], BF16),
    ]:
        if name in dbg:
            dbg_t[name] = nc.dram_tensor(name, shape, dt, kind="ExternalOutput").ap()

    P = Plan()
    with contextlib.ExitStack() as st:
        E = st.enter_context
        A = E(nc.sbuf_tensor("A", [128, NC * T], BF16))
        B = E(nc.sbuf_tensor("B", [128, NC * T], BF16))
        C = E(nc.sbuf_tensor("C", [128, KT * NH * 65], BF16))
        D = E(nc.sbuf_tensor("D", [128, NC * T], BF16))
        Ebuf = E(nc.sbuf_tensor("Ebuf", [128, 2 * T], BF16))
        Fb = E(nc.sbuf_tensor("Fb", [128, 4 * T], BF16))
        G = E(nc.sbuf_tensor("G", [128, 4 * T], BF16))
        ROWS = E(nc.sbuf_tensor("ROWS", [128, 2 * T], BF16))
        XS = E(nc.sbuf_tensor("XS", [128, T], F32))
        YST = E(nc.sbuf_tensor("YST", [128, T], F32))
        WS = E(nc.sbuf_tensor("WS", [128, 2 * 1024], F32))
        WB = E(nc.sbuf_tensor("WB", [128, 2 * 1024], BF16))
        bqc = E(nc.sbuf_tensor("bqc", [128, 24], F32))
        boc = E(nc.sbuf_tensor("boc", [128, 8], F32))
        murc = E(nc.sbuf_tensor("murc", [128, 8], F32))
        mukc = E(nc.sbuf_tensor("mukc", [128, 8], F32))
        omrc = E(nc.sbuf_tensor("omrc", [128, 8], F32))
        omkc = E(nc.sbuf_tensor("omkc", [128, 8], F32))
        wqc = E(nc.sbuf_tensor("wqc", [128, 1], F32))
        wkc = E(nc.sbuf_tensor("wkc", [128, 1], F32))
        lnT = E(nc.sbuf_tensor("lnT", [128, 4], BF16))
        lhq = E(nc.sbuf_tensor("lhq", [128, 4], BF16))
        lhk = E(nc.sbuf_tensor("lhk", [128, 4], BF16))
        b2 = E(nc.sbuf_tensor("b2", [2, 128], BF16))
        b2b = E(nc.sbuf_tensor("b2b", [2, 128], BF16))
        i4r = E(nc.sbuf_tensor("i4r", [4, 128], BF16))
        i4n = E(nc.sbuf_tensor("i4n", [4, 128], BF16))
        CCOL = E(nc.sbuf_tensor("CCOL", [128, 8], F32))
        bcole = E(nc.sbuf_tensor("bcole", [128, 2], F32))
        bcolo = E(nc.sbuf_tensor("bcolo", [128, 2], F32))
        bndc = E(nc.sbuf_tensor("bndc", [128, 8], BF16))
        epsc = E(nc.sbuf_tensor("epsc", [128, 2], F32))
        ps1 = E(nc.psum_tensor("ps1", [128, T], F32))
        ps2 = E(nc.psum_tensor("ps2", [128, T], F32))
        sw = Sem(E(nc.semaphore("sw")), "sync")
        sw0 = Sem(E(nc.semaphore("sw0")), "sync")
        sw1 = Sem(E(nc.semaphore("sw1")), "sync")
        sx0 = Sem(E(nc.semaphore("sx0")), "sync")
        sx1 = Sem(E(nc.semaphore("sx1")), "sync")
        si = Sem(E(nc.semaphore("si")), "gpsimd")
        sb = Sem(E(nc.semaphore("sb")), "gpsimd")
        so = Sem(E(nc.semaphore("so")), "sync")
        tS = Sem(E(nc.semaphore("tS")), "tensor")
        vS = Sem(E(nc.semaphore("vS")), "vector")
        sS = Sem(E(nc.semaphore("sS")), "scalar")
        block = E(nc.Block())

        Vv = C[:].rearrange("p (t h e) -> p t h e", t=KT, h=NH, e=65)
        zb = D[:].rearrange("p (c t) -> p c t", c=NC)
        wbv = [WB[:, i * 1024:(i + 1) * 1024].rearrange("p (c o) -> p c o", c=NC)
               for i in range(2)]
        Ach = [A[:, c * T:(c + 1) * T] for c in range(NC)]
        f = lambda i: Fb[:, i * T:(i + 1) * T]
        g = lambda i: G[:, i * T:(i + 1) * T]
        e0 = Ebuf[:, 0:T]
        e1 = Ebuf[:, T:2 * T]
        ptb = lambda p: Ebuf[:, p * T:(p + 1) * T]
        ptE = lambda k: Ebuf[:, (k % 4) * NT:(k % 4 + 1) * NT]
        ptO = lambda k: Ebuf[:, T + (k % 4) * NT:T + (k % 4 + 1) * NT]
        VBT = G[:, 0:1024]

        def dump(name, ap, waits):
            if name in dbg_t:
                tgt = dbg_t[name]
                P.step("sync", lambda e, a=ap, t=tgt: e.dma_start(out=t, in_=a),
                       waits, so, 16)

        ws_cast = [0, 0]
        wb_read = [(tS, 0), (tS, 0)]
        wld = {}

        def load_wblock(key, src, slot):
            s_ = sw0 if slot == 0 else sw1
            P.step("sync", lambda e, src=src, slot=slot: e.dma_start(
                out=WS[:, slot * 1024:(slot + 1) * 1024], in_=src),
                [(si, ws_cast[slot])], s_, 16)
            P.step("gpsimd", lambda e, slot=slot: e.tensor_copy(
                WB[:, slot * 1024:(slot + 1) * 1024], WS[:, slot * 1024:(slot + 1) * 1024]),
                [(s_, s_.n), wb_read[slot]], si, 1)
            wld[key] = si.n
            ws_cast[slot] = si.n

        def consume_wb(slot, sem, count):
            wb_read[slot] = (sem, count)

        for dst, src in [(bqc, b_qkv), (boc, b_o), (murc, mu_r_c), (mukc, mu_k_c),
                         (wqc, wq_col), (wkc, wk_col), (lnT, cst_ln)]:
            P.step("sync", lambda e, d=dst, s=src: e.dma_start(out=d[:], in_=s),
                   (), sw, 16)
        P.step("sync", lambda e: e.dma_start(out=lhq[:], in_=cst_ind), (), sw, 16)
        P.step("sync", lambda e: e.dma_start(out=lhk[:], in_=cst_ind), (), sw, 16)
        P.step("sync", lambda e: e.dma_start(out=b2[:], in_=cst_b2), (), sw, 16)
        P.step("sync", lambda e: e.dma_start(out=b2b[:], in_=cst_b2b), (), sw, 16)
        P.step("sync", lambda e: e.dma_start(out=i4r[:], in_=cst_i4r), (), sw, 16)
        P.step("sync", lambda e: e.dma_start(out=i4n[:], in_=cst_i4n), (), sw, 16)
        cw = sw.n
        P.step("scalar", lambda e: e.activation(lhq[0:64, 2:3], wqc[0:64, :], AF.Square),
               [(sw, cw)], sS)
        P.step("scalar", lambda e: e.activation(lhq[64:128, 3:4], wqc[64:128, :], AF.Square),
               (), sS)
        P.step("scalar", lambda e: e.activation(lhk[0:64, 2:3], wkc[0:64, :], AF.Square),
               (), sS)
        P.step("scalar", lambda e: e.activation(lhk[64:128, 3:4], wkc[64:128, :], AF.Square),
               (), sS)
        sq2 = sS.n
        P.step("vector", lambda e: e.tensor_scalar(omrc[:], murc[:], -1.0, 1.0,
                                                   ALU.mult, ALU.add), [(sw, cw)], vS)
        P.step("vector", lambda e: e.tensor_scalar(omkc[:], mukc[:], -1.0, 1.0,
                                                   ALU.mult, ALU.add), [(sw, cw)], vS)
        P.step("vector", lambda e: e.memset(Vv[:, :, :, 64:65], 1.0), (), vS)
        P.step("vector", lambda e: e.memset(bndc[:], 0.0), (), vS)
        P.step("vector", lambda e: e.memset(epsc[:, 0:1], LN_EPS), (), vS)
        P.step("vector", lambda e: e.memset(epsc[:, 1:2], RMS_EPS), (), vS)
        cdone = vS.n

        def ln_stats_apply(xch, sq_dst, inb, mm_extra=(), mm_extra2=(), ebuf_guard=(),
                           g_guard=()):
            for c in range(NC):
                P.step("vector",
                       lambda e, c=c: e.tensor_mul(sq_dst(c), xch[c], xch[c]),
                       list(inb[c]), vS)
            sqd = vS.n
            for c in range(NC):
                for nt in range(4):
                    P.step("tensor",
                           lambda e, c=c, nt=nt: e.matmul(
                               ps2[0:2, nt * NT:(nt + 1) * NT], lnT[:, 0:2],
                               xch[c][:, nt * NT:(nt + 1) * NT],
                               start=(c == 0), stop=False),
                           list(inb[c]) + [(sw, cw)] + list(mm_extra), None)
            for c in range(NC):
                for nt in range(4):
                    last = (c == NC - 1) and (nt == 3)
                    P.step("tensor",
                           lambda e, c=c, nt=nt, last=last: e.matmul(
                               ps2[0:2, nt * NT:(nt + 1) * NT], lnT[:, 2:4],
                               sq_dst(c)[:, nt * NT:(nt + 1) * NT],
                               start=False, stop=last),
                           [(vS, sqd)], tS if last else None)
            mmd = tS.n
            P.step("vector", lambda e: e.tensor_copy(ROWS[0:2, 0:T], ps2[0:2, :]),
                   [(tS, mmd)], vS)
            rowd = vS.n
            for nt in range(4):
                P.step("tensor", lambda e, nt=nt: e.matmul(
                    ps1[:, nt * NT:(nt + 1) * NT], b2[:],
                    ROWS[0:2, nt * NT:(nt + 1) * NT], start=True, stop=True),
                    [(vS, rowd)] + list(mm_extra2), tS if nt == 3 else None)
            mtd = tS.n
            for nt in range(4):
                P.step("tensor", lambda e, nt=nt: e.matmul(
                    ps2[:, nt * NT:(nt + 1) * NT], b2b[:],
                    ROWS[0:2, nt * NT:(nt + 1) * NT], start=True, stop=True),
                    [(vS, rowd)], tS if nt == 3 else None)
            qtd = tS.n
            P.step("scalar", lambda e: e.activation(e0, ps1[:], AF.Square),
                   [(tS, mtd)] + list(ebuf_guard), sS)
            sqd2 = sS.n
            P.step("vector", lambda e: e.tensor_sub(e1, ps2[:], e0),
                   [(tS, qtd), (sS, sqd2)] + list(ebuf_guard), vS)
            vd = vS.n
            P.step("scalar", lambda e: e.activation(e0, e1, AF.Ln, bias=epsc[:, 0:1]),
                   [(vS, vd)], sS)
            P.step("scalar", lambda e: e.activation(g(2), e0, AF.Exp, scale=-0.5),
                   list(g_guard), sS)
            sd = sS.n
            P.step("vector", lambda e: e.tensor_mul(g(3), ps1[:], g(2)),
                   [(sS, sd)] + list(g_guard), vS)
            return {"rows": rowd, "tiles": vS.n, "ps_used": tS.n}

        xcast = []
        for c in range(NC):
            stg = XS if c % 2 == 0 else YST
            s_ = sx0 if c % 2 == 0 else sx1
            wv = [(vS, xcast[c - 2])] if c > 1 else [(vS, cdone)]
            P.step("sync", lambda e, c=c, stg=stg: e.dma_start(out=stg[:], in_=x_in[:, c, :]),
                   wv, s_, 16)
            P.step("vector", lambda e, c=c, stg=stg: e.tensor_copy(Ach[c], stg[:]),
                   [(s_, s_.n)], vS)
            xcast.append(vS.n)
        Bsq = lambda c: B[:, c * T:(c + 1) * T]
        m1 = ln_stats_apply(Ach, Bsq, [[(vS, xcast[c])] for c in range(NC)])
        for c in range(NC):
            P.step("vector", lambda e, c=c: e.tensor_mul(Bsq(c), Ach[c], g(2)),
                   [(vS, m1["tiles"])], vS)
            P.step("vector", lambda e, c=c: e.tensor_sub(Ach[c], Bsq(c), g(3)), (), vS)
        ln1d = vS.n
        dump("dbg_ln1", A[:], [(vS, ln1d)])

        P.step("sync", lambda e: e.dma_start(out=XS[0:1, 0:1024], in_=b_vrow),
               [(vS, ln1d)], sw, 16)
        P.step("vector", lambda e: e.tensor_copy(ROWS[0:1, 0:1024], XS[0:1, 0:1024]),
               [(sw, sw.n), (sb, sb.n)], vS)
        P.step("gpsimd", lambda e: e.dma_start(
            out=VBT[:], in_=ROWS[0:1, 0:1024].unsqueeze(1).to_broadcast([1, 128, 1024])),
            [(vS, vS.n)], sb, 16)
        vbtd = sb.n
        wvd = []
        for c in range(NC):
            load_wblock(("wv", c), w_qkv_v[:, c, :], c % 2)
            P.step("vector", lambda e, c=c: e.tensor_copy(
                D[:, c * 1024:(c + 1) * 1024],
                WB[:, (c % 2) * 1024:((c % 2) + 1) * 1024]), [(si, wld[("wv", c)])], vS)
            wvd.append(vS.n)
            consume_wb(c % 2, vS, vS.n)
        psV = [ps1[:, 0:1024], ps1[:, 1024:2048], ps2[:, 0:1024], ps2[:, 1024:2048]]
        vevac = [0, 0, 0, 0]
        for tt in range(KT):
            reg = psV[tt % 4]
            extra = [(vS, m1["tiles"])]
            for c in range(NC):
                for nt in range(2):
                    last = (c == NC - 1) and (nt == 1)
                    P.step("tensor", lambda e, tt=tt, c=c, nt=nt, reg=reg, last=last: e.matmul(
                        reg[:, nt * NT:(nt + 1) * NT],
                        A[:, c * T + tt * 128: c * T + (tt + 1) * 128],
                        D[:, c * 1024 + nt * NT: c * 1024 + (nt + 1) * NT],
                        start=(c == 0), stop=last),
                        [(vS, wvd[c]), (vS, vevac[tt % 4]), (vS, ln1d)] + extra,
                        tS if last else None)
            mmd = tS.n
            P.step("vector", lambda e, tt=tt, reg=reg: e.tensor_add(
                Vv[:, tt, :, 0:64],
                reg[:].rearrange("p (h e) -> p h e", h=NH),
                VBT[:].rearrange("p (h e) -> p h e", h=NH)),
                [(tS, mmd), (sb, vbtd)], vS)
            vevac[tt % 4] = vS.n
        dump("dbg_vp", Vv[:, 0, :, :], [(vS, vS.n)])

        qsq = B[:, 2 * T:3 * T]
        ksq = B[:, 3 * T:4 * T]
        zr = lambda p: B[0:65, p * T:(p + 1) * T]
        zrl = lambda p: B[64:65, p * T:(p + 1) * T]
        LF = B[:, 4 * T:8 * T].bitcast(F32)
        sE = lambda k: ps1[:, (k % 2) * NT:(k % 2 + 1) * NT]
        sO = lambda k: ps1[:, 1024 + (k % 2) * NT:1024 + (k % 2 + 1) * NT]
        zE = ps2[0:65, 0:NT]
        zO = ps2[0:65, NT:2 * NT]
        FA = ps2[:, 1024:2048]
        FAh = [ps2[:, 1024:1536], ps2[:, 1536:2048]]

        last_zevac = vS.n
        last_div = {0: vS.n, 1: vS.n}
        pv_slot = {}
        exp_bank = {0: sS.n, 1: sS.n}
        sub_free = [(vS, 0), (vS, 0)]
        fa_free = (vS, 0)
        bcol_n = {}
        recs_n = {}
        lin_n = [0]
        appd_n = {}
        rows_n = {}
        chain_out = {}

        def _sub(idx, sem, cnt):
            sub_free[idx] = (sem, cnt)

        sq_n = {}

        def pk_qkv_group(p, side, nt):
            sub = nt % 2
            reg = FAh[sub]
            wkey = ("q", p) if side == 0 else ("k", p)
            fraw = f(0) if side == 0 else f(1)
            sq = qsq if side == 0 else ksq
            bias = bqc[:, p:p + 1] if side == 0 else bqc[:, 8 + p:9 + p]

            def emit():
                for c in range(NC):
                    last = c == NC - 1
                    P.step("tensor", lambda e, c=c, last=last: e.matmul(
                        reg, wbv[side][:, c, :],
                        A[:, c * T + nt * NT: c * T + (nt + 1) * NT],
                        start=(c == 0), stop=last),
                        [(si, wld[wkey]), sub_free[sub]], tS if last else None)
                mmn = tS.n
                if side == 1 and nt == 3:
                    consume_wb(1, tS, mmn)
                if side == 0 and nt == 3:
                    consume_wb(0, tS, mmn)
                P.step("vector", lambda e: e.tensor_scalar_add(
                    fraw[:, nt * NT:(nt + 1) * NT], reg, bias), [(tS, mmn)], vS)
                P.step("vector", lambda e: e.tensor_mul(
                    sq[:, nt * NT:(nt + 1) * NT], fraw[:, nt * NT:(nt + 1) * NT],
                    fraw[:, nt * NT:(nt + 1) * NT]), (), vS)
                _sub(sub, vS, vS.n)
                sq_n[(side, nt)] = vS.n
            return emit

        def pk_stats(p, side):
            sq = qsq if side == 0 else ksq
            lh = lhq if side == 0 else lhk
            off = 0 if side == 0 else T

            def emit():
                for nt in range(4):
                    sub = nt % 2
                    P.step("tensor", lambda e, nt=nt, sub=sub: e.matmul(
                        FAh[sub][0:4, :], lh[:],
                        sq[:, nt * NT:(nt + 1) * NT], start=True, stop=True),
                        [sub_free[sub], (vS, sq_n[(side, nt)])], tS)
                    mmn = tS.n
                    P.step("vector", lambda e, nt=nt, sub=sub: e.tensor_copy(
                        ROWS[0:4, off + nt * NT: off + (nt + 1) * NT], FAh[sub][0:4, :]),
                        [(tS, mmn)], vS)
                    _sub(sub, vS, vS.n)
                rows_n[(p, side)] = vS.n
            return emit

        def pk_chain_half(p, side, hh):
            off = 0 if side == 0 else T
            sclg = g(0) if side == 0 else g(1)
            nrm = g(2) if side == 0 else g(3)
            wcol = wqc if side == 0 else wkc
            stg = XS if side == 0 else YST

            def emit():
                for i2 in range(2):
                    P.step("tensor", lambda e, i2=i2: e.matmul(
                        FAh[i2], i4r[:],
                        ROWS[0:4, off + hh * 1024 + i2 * NT: off + hh * 1024 + (i2 + 1) * NT],
                        start=True, stop=True),
                        [sub_free[i2], (vS, rows_n[(p, side)])], tS if i2 else None)
                mtn = tS.n
                P.step("scalar", lambda e: e.activation(stg[:, 0:1024], FA, AF.Ln,
                                                        bias=epsc[:, 1:2]), [(tS, mtn)], sS)
                P.step("scalar", lambda e: e.activation(stg[:, 1024:2048], stg[:, 0:1024],
                                                        AF.Exp, scale=-0.5), (), sS)
                lnn = sS.n
                for i2 in range(2):
                    P.step("tensor", lambda e, i2=i2: e.matmul(
                        FAh[i2], i4n[:],
                        ROWS[0:4, off + hh * 1024 + i2 * NT: off + hh * 1024 + (i2 + 1) * NT],
                        start=True, stop=True),
                        [(sS, lnn)], tS if i2 else None)
                ntn = tS.n
                P.step("vector", lambda e: e.tensor_mul(
                    sclg[:, hh * 1024:(hh + 1) * 1024], stg[:, 1024:2048], FA),
                    [(tS, ntn)], vS)
                gw = [(si, si.n)] if side == 1 else []
                P.step("vector", lambda e: e.tensor_mul(
                    nrm[:, hh * 1024:(hh + 1) * 1024],
                    sclg[:, hh * 1024:(hh + 1) * 1024], stg[:, 1024:2048]), gw, vS)
                P.step("vector", lambda e: e.tensor_scalar_mul(
                    sclg[:, hh * 1024:(hh + 1) * 1024], stg[:, 1024:2048], wcol[:, 0:1]),
                    (), vS)
                _sub(0, vS, vS.n)
                _sub(1, vS, vS.n)
                chain_out[(p, side, hh)] = vS.n
            return emit

        def pk_bias(p):
            par = p % 2

            def emit():
                w0 = [(vS, chain_out[(p, 0, 0)]), (vS, chain_out[(p, 0, 1)]),
                      (vS, chain_out[(p, 1, 0)]), (vS, chain_out[(p, 1, 1)])]
                P.step("vector", lambda e: e.reduce_max(CCOL[:, 0:1], g(2), axis=AX),
                       w0, vS)
                P.step("vector", lambda e: e.reduce_max(CCOL[:, 1:2], g(3), axis=AX),
                       (), vS)
                vmax = vS.n
                P.step("scalar", lambda e: e.activation(CCOL[:, 4:6], CCOL[:, 0:2], AF.Ln),
                       [(vS, vmax)], sS)
                lnn = sS.n
                P.step("vector", lambda e: e.reduce_sum(CCOL[:, 6:7], CCOL[:, 4:6],
                                                        axis=AX), [(sS, lnn)], vS)
                vsum = vS.n
                P.step("scalar", lambda e: e.activation(CCOL[:, 2:3], CCOL[:, 6:7],
                                                        AF.Exp, scale=0.5), [(vS, vsum)], sS)
                expn = sS.n
                pb = bcol_n.get(p - 2, (0, 0))[0]
                P.step("vector", lambda e: e.tensor_scalar(CCOL[:, 3:4], CCOL[:, 2:3],
                                                           -1.0, BIAS_SHIFT,
                                                           ALU.mult, ALU.add),
                       [(sS, expn), (sb, pb)], vS)
                biasd = vS.n
                P.step("sync", lambda e, par=par: e.dma_start(
                    out=bcole[:, par:par + 1],
                    in_=CCOL[0:1, 3:4].unsqueeze(1).to_broadcast([1, 128, 1])),
                    [(vS, biasd)], sb, 16)
                P.step("sync", lambda e, par=par: e.dma_start(
                    out=bcolo[:, par:par + 1],
                    in_=CCOL[64:65, 3:4].unsqueeze(1).to_broadcast([1, 128, 1])),
                    [(vS, biasd)], sb, 16)
                bcol_n[p] = (sb.n, par)
                if p == 0 and "dbg_bcol" in dbg_t:
                    dump("dbg_bcol", CCOL[:, 0:6], [(vS, biasd)])
            return emit

        def pk_divides(jp):
            def emit():
                P.step("vector", lambda e: e.memset(G[64:65, 2 * T:2 * T + 64], 1.0),
                       [(vS, recs_n[jp])], vS)
                onesd = vS.n
                for h2 in range(2):
                    for hh in range(2):
                        for i2 in range(2):
                            P.step("tensor", lambda e, hh=hh, i2=i2, h2=h2: e.matmul(
                                FAh[i2][0:64, :], G[64:65, 2 * T:2 * T + 64],
                                ROWS[64:65, h2 * T + hh * 1024 + i2 * NT:
                                     h2 * T + hh * 1024 + (i2 + 1) * NT],
                                start=True, stop=True),
                                [(vS, onesd), sub_free[i2]], tS if i2 else None)
                        linn = tS.n
                        lin_n[0] = linn
                        if h2 == 0:
                            P.step("vector", lambda e, jp=jp, hh=hh: e.tensor_mul(
                                zb[0:64, jp, hh * 1024:(hh + 1) * 1024],
                                zr(0)[0:64, hh * 1024:(hh + 1) * 1024], FA[0:64, :]),
                                [(tS, linn)], vS)
                            last_div[0] = vS.n
                        else:
                            P.step("vector", lambda e, jp=jp, hh=hh: e.tensor_mul(
                                g(3)[0:64, hh * 1024:(hh + 1) * 1024],
                                zr(1)[0:64, hh * 1024:(hh + 1) * 1024], FA[0:64, :]),
                                [(tS, linn), (si, si.n)], vS)
                            last_div[1] = vS.n
                        _sub(0, vS, vS.n)
                        _sub(1, vS, vS.n)
                P.step("gpsimd", lambda e, jp=jp: e.dma_start(
                    out=zb[64:128, jp, :], in_=g(3)[0:64, :]),
                    [(vS, last_div[1])], si, 16)
            return emit

        def emit_attention(j, packets):
            nonlocal last_zevac
            pki = [0]

            def pop(n=1):
                for _ in range(n):
                    if pki[0] < len(packets):
                        packets[pki[0]]()
                        pki[0] += 1

            par = bcol_n[j][1]
            bcE = bcole[:, par:par + 1]
            bcO = bcolo[:, par:par + 1]
            hE, hO = 2 * j, 2 * j + 1
            zfree = {0: (vS, last_zevac), 1: (vS, last_zevac)}
            expn = {}
            for qq in range(4):
                for kt in range(KT):
                    swl = [(sS, exp_bank[kt % 2]), (vS, appd_n[j]),
                           (sb, bcol_n[j][0])]
                    P.step("tensor", lambda e, kt=kt, qq=qq: e.matmul(
                        sE(kt), f(3)[0:64, kt * 128:(kt + 1) * 128],
                        f(2)[0:64, qq * NT:(qq + 1) * NT],
                        start=True, stop=True, tile_position=(0, 0)), swl, None)
                    P.step("tensor", lambda e, kt=kt, qq=qq: e.matmul(
                        sO(kt), f(3)[64:128, kt * 128:(kt + 1) * 128],
                        f(2)[64:128, qq * NT:(qq + 1) * NT],
                        start=True, stop=True, tile_position=(64, 0)), (), tS)
                    smm = tS.n
                    ew = [(tS, smm)]
                    if (0, kt % 4) in pv_slot:
                        ew.append((tS, pv_slot[(0, kt % 4)]))
                    P.step("scalar", lambda e, kt=kt, bcE=bcE: e.activation(
                        ptE(kt), sE(kt), AF.Exp, bias=bcE), ew, sS)
                    ew2 = []
                    if (1, kt % 4) in pv_slot:
                        ew2.append((tS, pv_slot[(1, kt % 4)]))
                    P.step("scalar", lambda e, kt=kt, bcO=bcO: e.activation(
                        ptO(kt), sO(kt), AF.Exp, bias=bcO), ew2, sS)
                    exp_bank[kt % 2] = sS.n
                    expn[kt] = sS.n
                    if kt > 1:
                        kp = kt - 2
                        pwE = [(sS, expn[kp])]
                        if kp == 0:
                            pwE.append(zfree[0])
                            pwE.append(zfree[1])
                        P.step("tensor", lambda e, kp=kp, qq=qq, hE=hE: e.matmul(
                            zE, Vv[:, kp, hE, :], ptE(kp),
                            start=(kp == 0), stop=False), pwE, None)
                        P.step("tensor", lambda e, kp=kp, qq=qq, hO=hO: e.matmul(
                            zO, Vv[:, kp, hO, :], ptO(kp),
                            start=(kp == 0), stop=False), (), tS)
                        pv_slot[(0, kp % 4)] = tS.n
                        pv_slot[(1, kp % 4)] = tS.n
                    if kt >= 5 and kt % 2 == 1:
                        pop()
                for kp in (KT - 2, KT - 1):
                    P.step("tensor", lambda e, kp=kp, qq=qq, hE=hE: e.matmul(
                        zE, Vv[:, kp, hE, :], ptE(kp),
                        start=False, stop=(kp == KT - 1)), [(sS, expn[kp])], None)
                    P.step("tensor", lambda e, kp=kp, qq=qq, hO=hO: e.matmul(
                        zO, Vv[:, kp, hO, :], ptO(kp),
                        start=False, stop=(kp == KT - 1)), (), tS)
                    pv_slot[(0, kp % 4)] = tS.n
                    pv_slot[(1, kp % 4)] = tS.n
                pvall = tS.n
                P.step("vector", lambda e, qq=qq: e.tensor_copy(
                    zr(0)[:, qq * NT:(qq + 1) * NT], zE),
                    [(tS, pvall), (vS, last_div[0])], vS)
                zfree[0] = (vS, vS.n)
                P.step("vector", lambda e, qq=qq: e.tensor_copy(
                    zr(1)[:, qq * NT:(qq + 1) * NT], zO),
                    [(vS, last_div[1])], vS)
                zfree[1] = (vS, vS.n)
                last_zevac = vS.n

                P.step("vector", lambda e, qq=qq: e.tensor_copy(
                    LF[64:65, qq * NT:(qq + 1) * NT], ps2[64:65, 0:NT]), (), vS)
                P.step("vector", lambda e, qq=qq: e.tensor_copy(
                    LF[64:65, 2048 + qq * NT: 2048 + (qq + 1) * NT],
                    ps2[64:65, NT:2 * NT]), (), vS)
                zfree[0] = (vS, vS.n)
                zfree[1] = (vS, vS.n)
                for part, off in ((0, qq * NT), (1, 2048 + qq * NT)):
                    def _rc(e, off=off, part=part, qq=qq):
                        with nc.allow_low_precision(reason="softmax denom to bf16"):
                            return e.reciprocal(
                                ROWS[64:65, part * T + qq * NT: part * T + (qq + 1) * NT],
                                LF[64:65, off:off + NT])
                    P.step("vector", _rc,
                           [(tS, lin_n[0])] if (qq == 0 and part == 0) else (), vS)
            if j == 0:
                dump("dbg_zraw", zr(0), [(vS, last_zevac)])
            recs_n[j] = vS.n
            pop(len(packets))

        load_wblock(("q", 0), w_qkv_qk[:, 0, :], 0)
        load_wblock(("k", 0), w_qkv_qk[:, 8, :], 1)
        sub_free = [(vS, vevac[3]), (vS, vevac[3])]
        for side in range(2):
            for nt in range(4):
                pk_qkv_group(0, side, nt)()
            pk_stats(0, side)()
        load_wblock(("q", 1), w_qkv_qk[:, 1, :], 0)
        load_wblock(("k", 1), w_qkv_qk[:, 9, :], 1)
        for side in range(2):
            for hh in range(2):
                pk_chain_half(0, side, hh)()
        pk_bias(0)()
        if "dbg_qr" in dbg_t:
            dump("dbg_qr", f(0), [(vS, vS.n)])
            dump("dbg_lhq", lhq[:], [(vS, cdone)])
            dump("dbg_rows", ROWS[:], [(vS, rows_n[(0, 1)])])
        P.step("vector", lambda e: e.tensor_mul(f(2), f(0), g(0)),
               [(vS, chain_out[(0, 0, 1)]), (vS, chain_out[(0, 1, 1)])], vS)
        P.step("vector", lambda e: e.tensor_mul(f(3), f(1), g(1)), (), vS)
        appd_n[0] = vS.n
        if "dbg_q" in dbg_t:
            dump("dbg_q", f(2), [(vS, appd_n[0])])
            dump("dbg_sclq", g(0), [(vS, appd_n[0])])

        for j in range(NP):
            packets = []
            if j > 0:
                packets.append(pk_divides(j - 1))
            if j + 1 < NP:
                p = j + 1
                for side in range(2):
                    for nt in range(4):
                        packets.append(pk_qkv_group(p, side, nt))
                    packets.append(pk_stats(p, side))
                for side in range(2):
                    for hh in range(2):
                        packets.append(pk_chain_half(p, side, hh))
                packets.append(pk_bias(p))
            emit_attention(j, packets)

            if j + 1 < NP:
                if j + 2 < NP:
                    load_wblock(("q", j + 2), w_qkv_qk[:, j + 2, :], 0)
                    load_wblock(("k", j + 2), w_qkv_qk[:, 10 + j, :], 1)
                else:
                    load_wblock(("o", 0), w_o[:, 0, :], 0)
                    load_wblock(("o", 1), w_o[:, 1, :], 1)
                lastS = tS.n
                P.step("vector", lambda e: e.tensor_mul(f(2), f(0), g(0)),
                       [(vS, chain_out[(j + 1, 0, 1)]), (vS, chain_out[(j + 1, 1, 1)]),
                        (tS, lastS)], vS)
                P.step("vector", lambda e: e.tensor_mul(f(3), f(1), g(1)), (), vS)
                appd_n[j + 1] = vS.n

        pk_divides(NP - 1)()
        zdone_v = vS.n
        zdone_i = si.n
        pv_final = tS.n
        dump("dbg_zbuf", D[:], [(vS, zdone_v), (si, zdone_i)])

        psW = [ps1, ps2]
        woev = [sS.n, sS.n]
        woev_list = []
        wo_last_mm = [0]
        for o in range(NC):
            reg = psW[o % 2]
            basew = [(si, wld[("o", o)]), (vS, zdone_v), (si, zdone_i),
                     (sS, woev[o % 2])]
            if o < 2:
                basew += [(sS, exp_bank[0]), (sS, exp_bank[1])]
            for c in range(NC):
                for nt in range(4):
                    last = (c == NC - 1) and (nt == 3)
                    P.step("tensor", lambda e, o=o, c=c, nt=nt, reg=reg, last=last: e.matmul(
                        reg[:, nt * NT:(nt + 1) * NT], wbv[o % 2][:, c, :],
                        zb[:, c, nt * NT:(nt + 1) * NT], start=(c == 0), stop=last),
                        basew, tS if last else None)
            mmd = tS.n
            wo_last_mm[0] = mmd
            consume_wb(o % 2, tS, mmd)
            if o + 2 < NC:
                load_wblock(("o", o + 2), w_o[:, o + 2, :], o % 2)
            elif o + 2 == NC:
                load_wblock(("f2", "r", 0), w_r[:, 0, :], 0)
            elif o + 2 == NC + 1:
                load_wblock(("f2", "r", 1), w_r[:, 1, :], 1)
            P.step("scalar", lambda e, o=o, reg=reg: e.activation(
                B[:, o * T:(o + 1) * T], reg[:], AF.Identity, bias=boc[:, o:o + 1]),
                [(tS, mmd)], sS)
            woev[o % 2] = sS.n
            woev_list.append(sS.n)
        aod = sS.n
        dump("dbg_ao", B[:], [(sS, aod)])

        x1c = []
        for c in range(NC):
            stg = XS if c % 2 == 0 else YST
            if c == 0:
                wv = [(vS, chain_out[(NP - 1, 0, 1)])]
            elif c == 1:
                wv = [(vS, chain_out[(NP - 1, 1, 1)])]
            else:
                wv = [(vS, x1c[c - 2])]
            s_ = sx0 if c % 2 == 0 else sx1
            P.step("sync", lambda e, c=c, stg=stg: e.dma_start(out=stg[:], in_=x_in[:, c, :]),
                   wv, s_, 16)
            P.step("vector", lambda e, c=c, stg=stg: e.tensor_add(Ach[c], stg[:],
                                                                  B[:, c * T:(c + 1) * T]),
                   [(s_, s_.n), (sS, woev_list[c])], vS)
            x1c.append(vS.n)
        Dsq = lambda c: D[:, c * T:(c + 1) * T]
        m2 = ln_stats_apply(Ach, Dsq,
                            [[(vS, x1c[c]), (tS, wo_last_mm[0])] for c in range(NC)],
                            mm_extra=[(sS, aod)],
                            mm_extra2=[(sS, aod)],
                            ebuf_guard=[(tS, pv_final)],
                            g_guard=[(vS, zdone_v), (si, zdone_i)])
        ln2tiles = m2["tiles"]

        ln2f = D[:].rearrange("p (c t) -> p c t", c=NC)
        xf = C[:, 0:16384].rearrange("p (c t) -> p c t", c=NC)
        xkf = A[:].rearrange("p (c t) -> p c t", c=NC)
        kaf = D[:].rearrange("p (c t) -> p c t", c=NC)

        def r_ap(o):
            if o < 4:
                return Fb[:, o * T:(o + 1) * T]
            if o < 6:
                return Ebuf[:, (o - 4) * T:(o - 3) * T]
            return G[:, (o - 4) * T:(o - 3) * T]

        psW2 = [ps1, ps2]
        XSB = XS[:].bitcast(BF16)
        apl = []
        for c in range(NC):
            aw = [(vS, ln2tiles)]
            P.step("vector", lambda e, c=c: e.tensor_mul(
                XSB[:, 0:T], Ach[c], g(2)), aw, vS)
            P.step("vector", lambda e, c=c: e.tensor_sub(
                ln2f[:, c, :], XSB[:, 0:T], g(3)), (), vS)
            apl.append(vS.n)
        if "dbg_ln2" in dbg_t:
            dump("dbg_ln2", D[:, 0:8192], [(vS, vS.n)])
        mixd = {}

        def emit_mix(xv, mu, om, tag):
            for c in range(NC):
                P.step("vector", lambda e, c=c, mu=mu: e.tensor_scalar_mul(
                    VBT2[c], ln2f[:, c, :], mu[:, c:c + 1]), [(vS, apl[c])] + mixd.get(
                        ("free", tag), []), vS)
                P.step("vector", lambda e, c=c, xv=xv, om=om: e.scalar_tensor_tensor(
                    xv[:, c, 1:T], ln2f[:, c, 0:T - 1], om[:, c:c + 1],
                    VBT2[c][:, 1:T], ALU.mult, ALU.add), (), vS)
                P.step("vector", lambda e, c=c, xv=xv, om=om: e.tensor_scalar_mul(
                    xv[:, c, 0:1], VBT2[c][:, 0:1], 1.0), (), vS)
            mixd[tag] = vS.n

        VBT2 = [XSB[:, 0:T] for _ in range(NC)]

        def emit_mat(tag, wsrc, rhs_v, out_cb, first_extra=()):
            for o in range(NC):
                slot = o % 2
                if ("f2", tag, o) not in wld:
                    load_wblock(("f2", tag, o), wsrc[:, o, :], slot)
                reg = psW2[o % 2]
                basew = [(si, wld[("f2", tag, o)]), (vS, mixd[tag]),
                         matev[o % 2][0], matev[o % 2][1]]
                if o < 2:
                    basew += list(first_extra)
                for c in range(NC):
                    for nt in range(4):
                        last = (c == NC - 1) and (nt == 3)
                        P.step("tensor", lambda e, c=c, nt=nt, reg=reg, slot=slot,
                               rhs_v=rhs_v, last=last: e.matmul(
                                   reg[:, nt * NT:(nt + 1) * NT], wbv[slot][:, c, :],
                                   rhs_v[:, c, nt * NT:(nt + 1) * NT],
                                   start=(c == 0), stop=last),
                               basew, tS if last else None)
                mmn = tS.n
                consume_wb(slot, tS, mmn)
                out_cb(o, reg, mmn)

        matev = [[(vS, ln2tiles), (sS, sS.n)], [(vS, ln2tiles), (sS, sS.n)]]
        rsig_n = {}

        emit_mix(xf, murc, omrc, "r")
        emit_mix(xkf, mukc, omkc, "k")
        rmm_last = [0]

        def r_out(o, reg, mmn):
            P.step("scalar", lambda e, o=o, reg=reg: e.activation(
                r_ap(o), reg[:], AF.Sigmoid), [(tS, mmn)], sS)
            rsig_n[o] = sS.n
            matev[o % 2] = [(vS, vS.n), (sS, sS.n)]
            rmm_last[0] = mmn
        emit_mat("r", w_r, xf, r_out)
        if "dbg_r" in dbg_t:
            dump("dbg_r", Fb[:, 0:8192], [(sS, sS.n)])

        def k_out(o, reg, mmn):
            P.step("vector", lambda e, o=o, reg=reg: e.tensor_scalar_max(
                B2T2[o], reg[:], 0.0), [(tS, mmn)], vS)
            P.step("vector", lambda e, o=o: e.tensor_mul(
                kaf[:, o, :], B2T2[o], B2T2[o]), (), vS)
            matev[o % 2] = [(vS, vS.n), (sS, sS.n)]
        YSTB = YST[:].bitcast(BF16)
        B2T2 = [YSTB[:, 0:T] for _ in range(NC)]
        emit_mat("k", w_k2, xkf, k_out, first_extra=[(vS, mixd["k"])])
        if "dbg_ka" in dbg_t:
            dump("dbg_ka", D[:, 0:8192], [(vS, vS.n)])
        mixd["v"] = vS.n
        ydma_slot = [0, 0]
        xsfree = [vS.n]

        def v_out(o, reg, mmn):
            P.step("gpsimd", lambda e, o=o: e.dma_start(
                out=XS[:], in_=x_in[:, o, :]), [(vS, xsfree[0])], sb, 16)
            xld = sb.n
            P.step("vector", lambda e, o=o, reg=reg: e.scalar_tensor_tensor(
                YST[:, 0:1024], reg[:, 0:1024], 1.0, r_ap(o)[:, 0:1024],
                ALU.mult, ALU.mult),
                [(tS, mmn), (so, ydma_slot[0]), (sS, rsig_n[o])], vS)
            P.step("vector", lambda e, o=o: e.tensor_add(
                YST[:, 0:1024], YST[:, 0:1024], B[:, o * T:o * T + 1024]), (), vS)
            P.step("vector", lambda e, o=o: e.tensor_add(
                YST[:, 0:1024], YST[:, 0:1024], XS[:, 0:1024]), [(sb, xld)], vS)
            f1 = vS.n
            P.step("scalar", lambda e, o=o: e.dma_start(
                out=y_out[:, o, 0:1024], in_=YST[:, 0:1024]), [(vS, f1)], so, 16)
            P.step("vector", lambda e, o=o, reg=reg: e.scalar_tensor_tensor(
                YST[:, 1024:2048], reg[:, 1024:2048], 1.0, r_ap(o)[:, 1024:2048],
                ALU.mult, ALU.mult), (), vS)
            P.step("vector", lambda e, o=o: e.tensor_add(
                YST[:, 1024:2048], YST[:, 1024:2048],
                B[:, o * T + 1024:(o + 1) * T]), (), vS)
            P.step("vector", lambda e, o=o: e.tensor_add(
                YST[:, 1024:2048], YST[:, 1024:2048], XS[:, 1024:2048]), (), vS)
            f2_ = vS.n
            xsfree[0] = vS.n
            P.step("scalar", lambda e, o=o: e.dma_start(
                out=y_out[:, o, 1024:2048], in_=YST[:, 1024:2048]), [(vS, f2_)], so, 16)
            ydma_slot[0] = so.n
            matev[o % 2] = [(vS, vS.n), (sS, sS.n)]
        emit_mat("v", w_v2, kaf, v_out)
        P.step("sync", lambda e, n=so.n: e.wait_ge(so.h, n), (), None)

        @block.sync
        def _(e):
            for fn in P.ops["sync"]:
                fn(e)

        @block.gpsimd
        def _(e):
            for fn in P.ops["gpsimd"]:
                fn(e)

        @block.scalar
        def _(e):
            for fn in P.ops["scalar"]:
                fn(e)

        @block.vector
        def _(e):
            for fn in P.ops["vector"]:
                fn(e)

        @block.tensor
        def _(e):
            for fn in P.ops["tensor"]:
                fn(e)

    return nc


def _prep_inputs(L0_attn_x, L0_attn_Wqkv, L0_attn_bqkv, L0_attn_Wo, L0_attn_bo,
                 L0_attn_wq, L0_attn_wk, L0_ffn_W_r, L0_ffn_W_k, L0_ffn_W_v,
                 L0_ffn_mu_r, L0_ffn_mu_k):
    f32 = np.float32
    bf16 = ml_dtypes.bfloat16
    x = np.asarray(L0_attn_x, f32)

    def fm(w):
        wt = np.ascontiguousarray(np.asarray(w, f32).T)
        return np.ascontiguousarray(wt.reshape(8, 128, -1).transpose(1, 0, 2))

    def fm_slab(w):
        wt = np.asarray(w, f32).T
        O = wt.shape[1]
        a = wt.reshape(8, 128, O // 128, 128).transpose(1, 2, 0, 3)
        return np.ascontiguousarray(a.reshape(128, O // 128, 1024))

    wqkv_full = np.asarray(L0_attn_Wqkv, f32)
    wqkv_qk = fm_slab(wqkv_full[0:2048, :])
    wqkv_v = fm(wqkv_full[2048:3072, :])
    wo = fm_slab(L0_attn_Wo)
    wr = fm_slab(L0_ffn_W_r)
    wk2 = fm_slab(L0_ffn_W_k)
    wv2 = fm_slab(L0_ffn_W_v)
    bqkv = np.asarray(L0_attn_bqkv, f32)
    b_qkv = np.ascontiguousarray(bqkv.reshape(24, 128).T)
    b_vrow = np.ascontiguousarray(bqkv[2048:3072].reshape(1, 1024))
    b_o = np.ascontiguousarray(np.asarray(L0_attn_bo, f32).reshape(8, 128).T)
    mu_r = np.ascontiguousarray(np.asarray(L0_ffn_mu_r, f32).reshape(8, 128).T)
    mu_k = np.ascontiguousarray(np.asarray(L0_ffn_mu_k, f32).reshape(8, 128).T)
    wq_col = np.ascontiguousarray(np.tile(np.asarray(L0_attn_wq, f32), 2).reshape(128, 1))
    wk_col = np.ascontiguousarray(np.tile(np.asarray(L0_attn_wk, f32), 2).reshape(128, 1))
    cst_ln = np.zeros((128, 4), bf16)
    cst_ln[:, 0] = bf16(1.0 / 1024.0)
    cst_ln[:, 3] = bf16(1.0 / 1024.0)
    cst_ind = np.zeros((128, 4), bf16)
    cst_ind[0:64, 0] = bf16(1.0 / 64.0)
    cst_ind[64:128, 1] = bf16(1.0 / 64.0)
    cst_b2 = np.zeros((2, 128), bf16)
    cst_b2[0, :] = bf16(1.0)
    cst_b2b = np.zeros((2, 128), bf16)
    cst_b2b[1, :] = bf16(1.0)
    cst_i4r = np.zeros((4, 128), bf16)
    cst_i4r[0, 0:64] = bf16(1.0)
    cst_i4r[1, 64:128] = bf16(1.0)
    cst_i4n = np.zeros((4, 128), bf16)
    cst_i4n[2, 0:64] = bf16(1.0)
    cst_i4n[3, 64:128] = bf16(1.0)

    shared = dict(w_qkv_qk=wqkv_qk, w_qkv_v=wqkv_v, w_o=wo, w_r=wr, w_k2=wk2,
                  w_v2=wv2, b_qkv=b_qkv,
                  b_vrow=b_vrow, b_o=b_o, mu_r_c=mu_r, mu_k_c=mu_k,
                  wq_col=wq_col, wk_col=wk_col, cst_ln=cst_ln, cst_ind=cst_ind,
                  cst_b2=cst_b2, cst_b2b=cst_b2b, cst_i4r=cst_i4r, cst_i4n=cst_i4n)
    in_maps = []
    for b in range(8):
        xb = np.ascontiguousarray(x[b].T.reshape(8, 128, T).transpose(1, 0, 2))
        m = dict(shared)
        m["x_in"] = xb
        in_maps.append(m)
    return in_maps


def kernel(**inputs):
    in_maps = _prep_inputs(**inputs)
    nc = build_program()
    res = run_bass_kernel_spmd(nc, in_maps, core_ids=list(range(8)))
    outs = []
    for b in range(8):
        yb = res.results[b]["y"]
        outs.append(yb.transpose(1, 0, 2).reshape(1024, T).T)
    return np.ascontiguousarray(np.stack(outs, axis=0), dtype=np.float32)


# version 12
# speedup vs baseline: 1.2231x; 1.0535x over previous
import contextlib
import numpy as np
import ml_dtypes
import concourse.bass as bass
import concourse.mybir as mybir
from concourse.bass_utils import run_bass_kernel_spmd

F32 = mybir.dt.float32
BF16 = mybir.dt.bfloat16
AX = mybir.AxisListType.X
AF = mybir.ActivationFunctionType
ALU = mybir.AluOpType

T = 2048
NC = 8
NT = 512
NH = 16
HD = 64
NP = 8
KT = 16
LN_EPS = 1e-5
RMS_EPS = 1e-6
BIAS_SHIFT = 30.0


class Sem:
    def __init__(self, h, owner):
        self.h, self.owner, self.n = h, owner, 0


class Plan:
    ENGS = ("sync", "gpsimd", "scalar", "vector", "tensor")

    def __init__(self):
        self.ops = {e: [] for e in self.ENGS}
        self.last_wait = {e: {} for e in self.ENGS}

    def step(self, eng, emitfn, waits=(), inc=None, inc_amt=1):
        lw = self.last_wait[eng]
        ws = []
        for s, n in waits:
            if s.owner == eng or n <= 0:
                continue
            if lw.get(id(s), -1) >= n:
                continue
            assert n <= s.n, f"wait on future count {n} > {s.n} ({eng})"
            lw[id(s)] = n
            ws.append((s, n))
        if inc is not None:
            inc.n += inc_amt

        def f(e):
            for s, n in ws:
                e.wait_ge(s.h, n)
            ins = emitfn(e)
            if inc is not None:
                ins.then_inc(inc.h, inc_amt)

        self.ops[eng].append(f)


def build_program(dbg=frozenset()):
    nc = bass.Bass("TRN2", target_bir_lowering=False)

    x_in = nc.dram_tensor("x_in", [128, NC, T], F32, kind="ExternalInput").ap()
    w_qkv_qk = nc.dram_tensor("w_qkv_qk", [128, 16, 1024], F32, kind="ExternalInput").ap()
    w_qkv_v = nc.dram_tensor("w_qkv_v", [128, NC, 1024], F32, kind="ExternalInput").ap()
    w_o = nc.dram_tensor("w_o", [128, NC, 1024], F32, kind="ExternalInput").ap()
    w_r = nc.dram_tensor("w_r", [128, NC, 1024], F32, kind="ExternalInput").ap()
    w_k2 = nc.dram_tensor("w_k2", [128, NC, 1024], F32, kind="ExternalInput").ap()
    w_v2 = nc.dram_tensor("w_v2", [128, NC, 1024], F32, kind="ExternalInput").ap()
    b_qkv = nc.dram_tensor("b_qkv", [128, 24], F32, kind="ExternalInput").ap()
    b_vrow = nc.dram_tensor("b_vrow", [1, 1024], F32, kind="ExternalInput").ap()
    b_o = nc.dram_tensor("b_o", [128, 8], F32, kind="ExternalInput").ap()
    mu_r_c = nc.dram_tensor("mu_r_c", [128, 8], F32, kind="ExternalInput").ap()
    mu_k_c = nc.dram_tensor("mu_k_c", [128, 8], F32, kind="ExternalInput").ap()
    wq_col = nc.dram_tensor("wq_col", [128, 1], F32, kind="ExternalInput").ap()
    wk_col = nc.dram_tensor("wk_col", [128, 1], F32, kind="ExternalInput").ap()
    cst_ln = nc.dram_tensor("cst_ln", [128, 4], BF16, kind="ExternalInput").ap()
    cst_b2 = nc.dram_tensor("cst_b2", [2, 128], BF16, kind="ExternalInput").ap()
    cst_b2b = nc.dram_tensor("cst_b2b", [2, 128], BF16, kind="ExternalInput").ap()
    cst_i4r = nc.dram_tensor("cst_i4r", [4, 128], BF16, kind="ExternalInput").ap()
    cst_i4n = nc.dram_tensor("cst_i4n", [4, 128], BF16, kind="ExternalInput").ap()
    cst_ind = nc.dram_tensor("cst_ind", [128, 4], BF16, kind="ExternalInput").ap()

    y_out = nc.dram_tensor("y", [128, NC, T], F32, kind="ExternalOutput").ap()
    dbg_t = {}
    for name, shape, dt in [
        ("dbg_ln1", [128, NC * T], BF16),
        ("dbg_q", [128, T], BF16),
        ("dbg_sclq", [128, T], BF16),
        ("dbg_pt", [128, T], BF16),
        ("dbg_zraw", [65, T], BF16),
        ("dbg_zbuf", [128, NC * T], BF16),
        ("dbg_ao", [128, NC * T], BF16),
        ("dbg_vp", [128, NH * 65], BF16),
        ("dbg_ln2", [128, NC * 1024], BF16),
        ("dbg_xr", [128, NC * 1024], BF16),
        ("dbg_r", [128, NC * 1024], BF16),
        ("dbg_ka", [128, NC * 1024], BF16),
        ("dbg_bcol", [128, 6], F32),
        ("dbg_qr", [128, T], BF16),
        ("dbg_rows", [128, 2 * T], BF16),
        ("dbg_lhq", [128, 4], BF16),
    ]:
        if name in dbg:
            dbg_t[name] = nc.dram_tensor(name, shape, dt, kind="ExternalOutput").ap()

    P = Plan()
    with contextlib.ExitStack() as st:
        E = st.enter_context
        A = E(nc.sbuf_tensor("A", [128, NC * T], BF16))
        B = E(nc.sbuf_tensor("B", [128, NC * T], BF16))
        C = E(nc.sbuf_tensor("C", [128, KT * NH * 65], BF16))
        D = E(nc.sbuf_tensor("D", [128, NC * T], BF16))
        Ebuf = E(nc.sbuf_tensor("Ebuf", [128, 2 * T], BF16))
        Fb = E(nc.sbuf_tensor("Fb", [128, 4 * T], BF16))
        G = E(nc.sbuf_tensor("G", [128, 4 * T], BF16))
        ROWS = E(nc.sbuf_tensor("ROWS", [128, 2 * T], BF16))
        XS = E(nc.sbuf_tensor("XS", [128, T], F32))
        YST = E(nc.sbuf_tensor("YST", [128, T], F32))
        WS = E(nc.sbuf_tensor("WS", [128, 2 * 1024], F32))
        WB = E(nc.sbuf_tensor("WB", [128, 2 * 1024], BF16))
        bqc = E(nc.sbuf_tensor("bqc", [128, 24], F32))
        boc = E(nc.sbuf_tensor("boc", [128, 8], F32))
        murc = E(nc.sbuf_tensor("murc", [128, 8], F32))
        mukc = E(nc.sbuf_tensor("mukc", [128, 8], F32))
        omrc = E(nc.sbuf_tensor("omrc", [128, 8], F32))
        omkc = E(nc.sbuf_tensor("omkc", [128, 8], F32))
        wqc = E(nc.sbuf_tensor("wqc", [128, 1], F32))
        wkc = E(nc.sbuf_tensor("wkc", [128, 1], F32))
        lnT = E(nc.sbuf_tensor("lnT", [128, 4], BF16))
        lhq = E(nc.sbuf_tensor("lhq", [128, 4], BF16))
        lhk = E(nc.sbuf_tensor("lhk", [128, 4], BF16))
        b2 = E(nc.sbuf_tensor("b2", [2, 128], BF16))
        b2b = E(nc.sbuf_tensor("b2b", [2, 128], BF16))
        i4r = E(nc.sbuf_tensor("i4r", [4, 128], BF16))
        i4n = E(nc.sbuf_tensor("i4n", [4, 128], BF16))
        CCOL = E(nc.sbuf_tensor("CCOL", [128, 8], F32))
        bcole = E(nc.sbuf_tensor("bcole", [128, 2], F32))
        bcolo = E(nc.sbuf_tensor("bcolo", [128, 2], F32))
        bndc = E(nc.sbuf_tensor("bndc", [128, 8], BF16))
        epsc = E(nc.sbuf_tensor("epsc", [128, 2], F32))
        ps1 = E(nc.psum_tensor("ps1", [128, T], F32))
        ps2 = E(nc.psum_tensor("ps2", [128, T], F32))
        sw = Sem(E(nc.semaphore("sw")), "sync")
        sw0 = Sem(E(nc.semaphore("sw0")), "sync")
        sw1 = Sem(E(nc.semaphore("sw1")), "sync")
        sx0 = Sem(E(nc.semaphore("sx0")), "sync")
        sx1 = Sem(E(nc.semaphore("sx1")), "sync")
        si = Sem(E(nc.semaphore("si")), "gpsimd")
        sb = Sem(E(nc.semaphore("sb")), "gpsimd")
        so = Sem(E(nc.semaphore("so")), "sync")
        tS = Sem(E(nc.semaphore("tS")), "tensor")
        vS = Sem(E(nc.semaphore("vS")), "vector")
        sS = Sem(E(nc.semaphore("sS")), "scalar")
        block = E(nc.Block())

        Vv = C[:].rearrange("p (t h e) -> p t h e", t=KT, h=NH, e=65)
        zb = D[:].rearrange("p (c t) -> p c t", c=NC)
        wbv = [WB[:, i * 1024:(i + 1) * 1024].rearrange("p (c o) -> p c o", c=NC)
               for i in range(2)]
        Ach = [A[:, c * T:(c + 1) * T] for c in range(NC)]
        f = lambda i: Fb[:, i * T:(i + 1) * T]
        g = lambda i: G[:, i * T:(i + 1) * T]
        e0 = Ebuf[:, 0:T]
        e1 = Ebuf[:, T:2 * T]
        ptb = lambda p: Ebuf[:, p * T:(p + 1) * T]
        ptE = lambda k: Ebuf[:, (k % 4) * NT:(k % 4 + 1) * NT]
        ptO = lambda k: Ebuf[:, T + (k % 4) * NT:T + (k % 4 + 1) * NT]
        VBT = G[:, 0:1024]

        def dump(name, ap, waits):
            if name in dbg_t:
                tgt = dbg_t[name]
                P.step("sync", lambda e, a=ap, t=tgt: e.dma_start(out=t, in_=a),
                       waits, so, 16)

        ws_cast = [0, 0]
        wb_read = [(tS, 0), (tS, 0)]
        wld = {}

        def load_wblock(key, src, slot):
            s_ = sw0 if slot == 0 else sw1
            P.step("sync", lambda e, src=src, slot=slot: e.dma_start(
                out=WS[:, slot * 1024:(slot + 1) * 1024], in_=src),
                [(si, ws_cast[slot])], s_, 16)
            P.step("gpsimd", lambda e, slot=slot: e.tensor_copy(
                WB[:, slot * 1024:(slot + 1) * 1024], WS[:, slot * 1024:(slot + 1) * 1024]),
                [(s_, s_.n), wb_read[slot]], si, 1)
            wld[key] = si.n
            ws_cast[slot] = si.n

        def consume_wb(slot, sem, count):
            wb_read[slot] = (sem, count)

        for dst, src in [(bqc, b_qkv), (boc, b_o), (murc, mu_r_c), (mukc, mu_k_c),
                         (wqc, wq_col), (wkc, wk_col), (lnT, cst_ln)]:
            P.step("sync", lambda e, d=dst, s=src: e.dma_start(out=d[:], in_=s),
                   (), sw, 16)
        P.step("sync", lambda e: e.dma_start(out=lhq[:], in_=cst_ind), (), sw, 16)
        P.step("sync", lambda e: e.dma_start(out=lhk[:], in_=cst_ind), (), sw, 16)
        P.step("sync", lambda e: e.dma_start(out=b2[:], in_=cst_b2), (), sw, 16)
        P.step("sync", lambda e: e.dma_start(out=b2b[:], in_=cst_b2b), (), sw, 16)
        P.step("sync", lambda e: e.dma_start(out=i4r[:], in_=cst_i4r), (), sw, 16)
        P.step("sync", lambda e: e.dma_start(out=i4n[:], in_=cst_i4n), (), sw, 16)
        cw = sw.n
        P.step("scalar", lambda e: e.activation(lhq[0:64, 2:3], wqc[0:64, :], AF.Square),
               [(sw, cw)], sS)
        P.step("scalar", lambda e: e.activation(lhq[64:128, 3:4], wqc[64:128, :], AF.Square),
               (), sS)
        P.step("scalar", lambda e: e.activation(lhk[0:64, 2:3], wkc[0:64, :], AF.Square),
               (), sS)
        P.step("scalar", lambda e: e.activation(lhk[64:128, 3:4], wkc[64:128, :], AF.Square),
               (), sS)
        sq2 = sS.n
        P.step("vector", lambda e: e.tensor_scalar(omrc[:], murc[:], -1.0, 1.0,
                                                   ALU.mult, ALU.add), [(sw, cw)], vS)
        P.step("vector", lambda e: e.tensor_scalar(omkc[:], mukc[:], -1.0, 1.0,
                                                   ALU.mult, ALU.add), [(sw, cw)], vS)
        P.step("vector", lambda e: e.memset(Vv[:, :, :, 64:65], 1.0), (), vS)
        P.step("vector", lambda e: e.memset(bndc[:], 0.0), (), vS)
        P.step("vector", lambda e: e.memset(epsc[:, 0:1], LN_EPS), (), vS)
        P.step("vector", lambda e: e.memset(epsc[:, 1:2], RMS_EPS), (), vS)
        cdone = vS.n

        def ln_stats_apply(xch, sq_dst, inb, row_reg=None, t1_reg=None, t2_reg=None,
                           row_extra=(), t1_extra=(), ebuf_guard=(), g_guard=(),
                           sq_pre=None):
            row_reg = ps2 if row_reg is None else row_reg
            t1_reg = ps1 if t1_reg is None else t1_reg
            t2_reg = ps2 if t2_reg is None else t2_reg
            if sq_pre is None:
                for c in range(NC):
                    P.step("vector",
                           lambda e, c=c: e.tensor_mul(sq_dst(c), xch[c], xch[c]),
                           list(inb[c]), vS)
                sq_mark = [vS.n] * NC
            else:
                sq_mark = sq_pre
            for c in range(NC):
                for nt in range(4):
                    P.step("tensor",
                           lambda e, c=c, nt=nt: e.matmul(
                               row_reg[0:2, nt * NT:(nt + 1) * NT], lnT[:, 0:2],
                               xch[c][:, nt * NT:(nt + 1) * NT],
                               start=(c == 0), stop=False),
                           list(inb[c]) + [(sw, cw)] + list(row_extra), None)
            for c in range(NC):
                for nt in range(4):
                    last = (c == NC - 1) and (nt == 3)
                    P.step("tensor",
                           lambda e, c=c, nt=nt, last=last: e.matmul(
                               row_reg[0:2, nt * NT:(nt + 1) * NT], lnT[:, 2:4],
                               sq_dst(c)[:, nt * NT:(nt + 1) * NT],
                               start=False, stop=last),
                           [(vS, sq_mark[c])], tS if last else None)
            mmd = tS.n
            P.step("vector", lambda e: e.tensor_copy(ROWS[0:2, 0:T], row_reg[0:2, :]),
                   [(tS, mmd)], vS)
            rowd = vS.n
            for nt in range(4):
                P.step("tensor", lambda e, nt=nt: e.matmul(
                    t1_reg[:, nt * NT:(nt + 1) * NT], b2[:],
                    ROWS[0:2, nt * NT:(nt + 1) * NT], start=True, stop=True),
                    [(vS, rowd)] + list(t1_extra), tS if nt == 3 else None)
            mtd = tS.n
            for nt in range(4):
                P.step("tensor", lambda e, nt=nt: e.matmul(
                    t2_reg[:, nt * NT:(nt + 1) * NT], b2b[:],
                    ROWS[0:2, nt * NT:(nt + 1) * NT], start=True, stop=True),
                    [(vS, rowd)], tS if nt == 3 else None)
            qtd = tS.n
            P.step("scalar", lambda e: e.activation(e0, t1_reg[:], AF.Square),
                   [(tS, mtd)] + list(ebuf_guard), sS)
            sqd2 = sS.n
            P.step("vector", lambda e: e.tensor_sub(e1, t2_reg[:], e0),
                   [(tS, qtd), (sS, sqd2)] + list(ebuf_guard), vS)
            vd = vS.n
            P.step("scalar", lambda e: e.activation(e0, e1, AF.Ln, bias=epsc[:, 0:1]),
                   [(vS, vd)], sS)
            P.step("scalar", lambda e: e.activation(g(2), e0, AF.Exp, scale=-0.5),
                   list(g_guard), sS)
            sd = sS.n
            P.step("vector", lambda e: e.tensor_mul(g(3), t1_reg[:], g(2)),
                   [(sS, sd)] + list(g_guard), vS)
            return {"rows": rowd, "tiles": vS.n, "ps_used": tS.n}

        xcast = []
        for c in range(NC):
            stg = XS if c % 2 == 0 else YST
            s_ = sx0 if c % 2 == 0 else sx1
            wv = [(vS, xcast[c - 2])] if c > 1 else [(vS, cdone)]
            P.step("sync", lambda e, c=c, stg=stg: e.dma_start(out=stg[:], in_=x_in[:, c, :]),
                   wv, s_, 16)
            P.step("vector", lambda e, c=c, stg=stg: e.tensor_copy(Ach[c], stg[:]),
                   [(s_, s_.n)], vS)
            xcast.append(vS.n)
        Bsq = lambda c: B[:, c * T:(c + 1) * T]
        m1 = ln_stats_apply(Ach, Bsq, [[(vS, xcast[c])] for c in range(NC)])
        for c in range(NC):
            P.step("vector", lambda e, c=c: e.tensor_mul(Bsq(c), Ach[c], g(2)),
                   [(vS, m1["tiles"])], vS)
            P.step("vector", lambda e, c=c: e.tensor_sub(Ach[c], Bsq(c), g(3)), (), vS)
        ln1d = vS.n
        dump("dbg_ln1", A[:], [(vS, ln1d)])

        P.step("sync", lambda e: e.dma_start(out=XS[0:1, 0:1024], in_=b_vrow),
               [(vS, ln1d)], sw, 16)
        P.step("vector", lambda e: e.tensor_copy(ROWS[0:1, 0:1024], XS[0:1, 0:1024]),
               [(sw, sw.n), (sb, sb.n)], vS)
        P.step("gpsimd", lambda e: e.dma_start(
            out=VBT[:], in_=ROWS[0:1, 0:1024].unsqueeze(1).to_broadcast([1, 128, 1024])),
            [(vS, vS.n)], sb, 16)
        vbtd = sb.n
        wvd = []
        for c in range(NC):
            load_wblock(("wv", c), w_qkv_v[:, c, :], c % 2)
            P.step("vector", lambda e, c=c: e.tensor_copy(
                D[:, c * 1024:(c + 1) * 1024],
                WB[:, (c % 2) * 1024:((c % 2) + 1) * 1024]), [(si, wld[("wv", c)])], vS)
            wvd.append(vS.n)
            consume_wb(c % 2, vS, vS.n)
        psV = [ps1[:, 0:1024], ps1[:, 1024:2048], ps2[:, 0:1024], ps2[:, 1024:2048]]
        vevac = [0, 0, 0, 0]
        for tt in range(KT):
            reg = psV[tt % 4]
            extra = [(vS, m1["tiles"])]
            for c in range(NC):
                for nt in range(2):
                    last = (c == NC - 1) and (nt == 1)
                    P.step("tensor", lambda e, tt=tt, c=c, nt=nt, reg=reg, last=last: e.matmul(
                        reg[:, nt * NT:(nt + 1) * NT],
                        A[:, c * T + tt * 128: c * T + (tt + 1) * 128],
                        D[:, c * 1024 + nt * NT: c * 1024 + (nt + 1) * NT],
                        start=(c == 0), stop=last),
                        [(vS, wvd[c]), (vS, vevac[tt % 4]), (vS, ln1d)] + extra,
                        tS if last else None)
            mmd = tS.n
            P.step("vector", lambda e, tt=tt, reg=reg: e.tensor_add(
                Vv[:, tt, :, 0:64],
                reg[:].rearrange("p (h e) -> p h e", h=NH),
                VBT[:].rearrange("p (h e) -> p h e", h=NH)),
                [(tS, mmd), (sb, vbtd)], vS)
            vevac[tt % 4] = vS.n
        dump("dbg_vp", Vv[:, 0, :, :], [(vS, vS.n)])

        qsq = B[:, 2 * T:3 * T]
        ksq = B[:, 3 * T:4 * T]
        zr = lambda p: B[0:65, p * T:(p + 1) * T]
        zrl = lambda p: B[64:65, p * T:(p + 1) * T]
        LF = B[:, 4 * T:8 * T].bitcast(F32)
        sE = lambda k: ps1[:, (k % 2) * NT:(k % 2 + 1) * NT]
        sO = lambda k: ps1[:, 1024 + (k % 2) * NT:1024 + (k % 2 + 1) * NT]
        zE = ps2[0:65, 0:NT]
        zO = ps2[0:65, NT:2 * NT]
        FA = ps2[:, 1024:2048]
        FAh = [ps2[:, 1024:1536], ps2[:, 1536:2048]]

        last_zevac = vS.n
        last_div = {0: vS.n, 1: vS.n}
        pv_slot = {}
        exp_bank = {0: sS.n, 1: sS.n}
        sub_free = [(vS, 0), (vS, 0)]
        fa_free = (vS, 0)
        bcol_n = {}
        recs_n = {}
        lin_n = [0]
        appd_n = {}
        rows_n = {}
        chain_out = {}

        def _sub(idx, sem, cnt):
            sub_free[idx] = (sem, cnt)

        sq_n = {}

        def pk_qkv_group(p, side, nt):
            sub = nt % 2
            reg = FAh[sub]
            wkey = ("q", p) if side == 0 else ("k", p)
            fraw = f(0) if side == 0 else f(1)
            sq = qsq if side == 0 else ksq
            bias = bqc[:, p:p + 1] if side == 0 else bqc[:, 8 + p:9 + p]

            def emit():
                for c in range(NC):
                    last = c == NC - 1
                    P.step("tensor", lambda e, c=c, last=last: e.matmul(
                        reg, wbv[side][:, c, :],
                        A[:, c * T + nt * NT: c * T + (nt + 1) * NT],
                        start=(c == 0), stop=last),
                        [(si, wld[wkey]), sub_free[sub]], tS if last else None)
                mmn = tS.n
                if side == 1 and nt == 3:
                    consume_wb(1, tS, mmn)
                if side == 0 and nt == 3:
                    consume_wb(0, tS, mmn)
                P.step("vector", lambda e: e.tensor_scalar_add(
                    fraw[:, nt * NT:(nt + 1) * NT], reg, bias), [(tS, mmn)], vS)
                P.step("vector", lambda e: e.tensor_mul(
                    sq[:, nt * NT:(nt + 1) * NT], fraw[:, nt * NT:(nt + 1) * NT],
                    fraw[:, nt * NT:(nt + 1) * NT]), (), vS)
                _sub(sub, vS, vS.n)
                sq_n[(side, nt)] = vS.n
            return emit

        def pk_stats(p, side):
            sq = qsq if side == 0 else ksq
            lh = lhq if side == 0 else lhk
            off = 0 if side == 0 else T

            def emit():
                for nt in range(4):
                    sub = nt % 2
                    P.step("tensor", lambda e, nt=nt, sub=sub: e.matmul(
                        FAh[sub][0:4, :], lh[:],
                        sq[:, nt * NT:(nt + 1) * NT], start=True, stop=True),
                        [sub_free[sub], (vS, sq_n[(side, nt)])], tS)
                    mmn = tS.n
                    P.step("vector", lambda e, nt=nt, sub=sub: e.tensor_copy(
                        ROWS[0:4, off + nt * NT: off + (nt + 1) * NT], FAh[sub][0:4, :]),
                        [(tS, mmn)], vS)
                    _sub(sub, vS, vS.n)
                rows_n[(p, side)] = vS.n
            return emit

        def pk_chain_half(p, side, hh):
            off = 0 if side == 0 else T
            sclg = g(0) if side == 0 else g(1)
            nrm = g(2) if side == 0 else g(3)
            wcol = wqc if side == 0 else wkc
            stg = XS if side == 0 else YST

            def emit():
                for i2 in range(2):
                    P.step("tensor", lambda e, i2=i2: e.matmul(
                        FAh[i2], i4r[:],
                        ROWS[0:4, off + hh * 1024 + i2 * NT: off + hh * 1024 + (i2 + 1) * NT],
                        start=True, stop=True),
                        [sub_free[i2], (vS, rows_n[(p, side)])], tS if i2 else None)
                mtn = tS.n
                P.step("scalar", lambda e: e.activation(stg[:, 0:1024], FA, AF.Ln,
                                                        bias=epsc[:, 1:2]), [(tS, mtn)], sS)
                P.step("scalar", lambda e: e.activation(stg[:, 1024:2048], stg[:, 0:1024],
                                                        AF.Exp, scale=-0.5), (), sS)
                lnn = sS.n
                for i2 in range(2):
                    P.step("tensor", lambda e, i2=i2: e.matmul(
                        FAh[i2], i4n[:],
                        ROWS[0:4, off + hh * 1024 + i2 * NT: off + hh * 1024 + (i2 + 1) * NT],
                        start=True, stop=True),
                        [(sS, lnn)], tS if i2 else None)
                ntn = tS.n
                P.step("vector", lambda e: e.tensor_mul(
                    sclg[:, hh * 1024:(hh + 1) * 1024], stg[:, 1024:2048], FA),
                    [(tS, ntn)], vS)
                gw = [(si, si.n)] if side == 1 else []
                P.step("vector", lambda e: e.tensor_mul(
                    nrm[:, hh * 1024:(hh + 1) * 1024],
                    sclg[:, hh * 1024:(hh + 1) * 1024], stg[:, 1024:2048]), gw, vS)
                P.step("vector", lambda e: e.tensor_scalar_mul(
                    sclg[:, hh * 1024:(hh + 1) * 1024], stg[:, 1024:2048], wcol[:, 0:1]),
                    (), vS)
                _sub(0, vS, vS.n)
                _sub(1, vS, vS.n)
                chain_out[(p, side, hh)] = vS.n
            return emit

        def pk_bias(p):
            par = p % 2

            def emit():
                w0 = [(vS, chain_out[(p, 0, 0)]), (vS, chain_out[(p, 0, 1)]),
                      (vS, chain_out[(p, 1, 0)]), (vS, chain_out[(p, 1, 1)])]
                P.step("vector", lambda e: e.reduce_max(CCOL[:, 0:1], g(2), axis=AX),
                       w0, vS)
                P.step("vector", lambda e: e.reduce_max(CCOL[:, 1:2], g(3), axis=AX),
                       (), vS)
                vmax = vS.n
                P.step("scalar", lambda e: e.activation(CCOL[:, 4:6], CCOL[:, 0:2], AF.Ln),
                       [(vS, vmax)], sS)
                lnn = sS.n
                P.step("vector", lambda e: e.reduce_sum(CCOL[:, 6:7], CCOL[:, 4:6],
                                                        axis=AX), [(sS, lnn)], vS)
                vsum = vS.n
                P.step("scalar", lambda e: e.activation(CCOL[:, 2:3], CCOL[:, 6:7],
                                                        AF.Exp, scale=0.5), [(vS, vsum)], sS)
                expn = sS.n
                pb = bcol_n.get(p - 2, (0, 0))[0]
                P.step("vector", lambda e: e.tensor_scalar(CCOL[:, 3:4], CCOL[:, 2:3],
                                                           -1.0, BIAS_SHIFT,
                                                           ALU.mult, ALU.add),
                       [(sS, expn), (sb, pb)], vS)
                biasd = vS.n
                P.step("sync", lambda e, par=par: e.dma_start(
                    out=bcole[:, par:par + 1],
                    in_=CCOL[0:1, 3:4].unsqueeze(1).to_broadcast([1, 128, 1])),
                    [(vS, biasd)], sb, 16)
                P.step("sync", lambda e, par=par: e.dma_start(
                    out=bcolo[:, par:par + 1],
                    in_=CCOL[64:65, 3:4].unsqueeze(1).to_broadcast([1, 128, 1])),
                    [(vS, biasd)], sb, 16)
                bcol_n[p] = (sb.n, par)
                if p == 0 and "dbg_bcol" in dbg_t:
                    dump("dbg_bcol", CCOL[:, 0:6], [(vS, biasd)])
            return emit

        def pk_divides(jp):
            def emit():
                P.step("vector", lambda e: e.memset(G[64:65, 2 * T:2 * T + 64], 1.0),
                       [(vS, recs_n[jp])], vS)
                onesd = vS.n
                for h2 in range(2):
                    for hh in range(2):
                        for i2 in range(2):
                            P.step("tensor", lambda e, hh=hh, i2=i2, h2=h2: e.matmul(
                                FAh[i2][0:64, :], G[64:65, 2 * T:2 * T + 64],
                                ROWS[64:65, h2 * T + hh * 1024 + i2 * NT:
                                     h2 * T + hh * 1024 + (i2 + 1) * NT],
                                start=True, stop=True),
                                [(vS, onesd), sub_free[i2]], tS if i2 else None)
                        linn = tS.n
                        lin_n[0] = linn
                        if h2 == 0:
                            P.step("vector", lambda e, jp=jp, hh=hh: e.tensor_mul(
                                zb[0:64, jp, hh * 1024:(hh + 1) * 1024],
                                zr(0)[0:64, hh * 1024:(hh + 1) * 1024], FA[0:64, :]),
                                [(tS, linn)], vS)
                            last_div[0] = vS.n
                        else:
                            P.step("vector", lambda e, jp=jp, hh=hh: e.tensor_mul(
                                g(3)[0:64, hh * 1024:(hh + 1) * 1024],
                                zr(1)[0:64, hh * 1024:(hh + 1) * 1024], FA[0:64, :]),
                                [(tS, linn), (si, si.n)], vS)
                            last_div[1] = vS.n
                        _sub(0, vS, vS.n)
                        _sub(1, vS, vS.n)
                P.step("gpsimd", lambda e, jp=jp: e.dma_start(
                    out=zb[64:128, jp, :], in_=g(3)[0:64, :]),
                    [(vS, last_div[1])], si, 16)
            return emit

        def emit_attention(j, packets):
            nonlocal last_zevac
            pki = [0]

            def pop(n=1):
                for _ in range(n):
                    if pki[0] < len(packets):
                        packets[pki[0]]()
                        pki[0] += 1

            par = bcol_n[j][1]
            bcE = bcole[:, par:par + 1]
            bcO = bcolo[:, par:par + 1]
            hE, hO = 2 * j, 2 * j + 1
            zfree = {0: (vS, last_zevac), 1: (vS, last_zevac)}
            expn = {}
            smm_kt = {}
            slast_n = 0
            defer_recips = (j + 1 < NP)
            deferred = []
            for qq in range(4):
                for kt in range(KT):
                    swl = [(sS, exp_bank[kt % 2]), (vS, appd_n[j]),
                           (sb, bcol_n[j][0])]
                    P.step("tensor", lambda e, kt=kt, qq=qq: e.matmul(
                        sE(kt), f(3)[0:64, kt * 128:(kt + 1) * 128],
                        f(2)[0:64, qq * NT:(qq + 1) * NT],
                        start=True, stop=True, tile_position=(0, 0)), swl, None)
                    P.step("tensor", lambda e, kt=kt, qq=qq: e.matmul(
                        sO(kt), f(3)[64:128, kt * 128:(kt + 1) * 128],
                        f(2)[64:128, qq * NT:(qq + 1) * NT],
                        start=True, stop=True, tile_position=(64, 0)), (), tS)
                    smm_kt[kt] = tS.n
                    slast_n = tS.n
                    ew = [(tS, smm_kt[kt])]
                    if (0, kt % 4) in pv_slot:
                        ew.append((tS, pv_slot[(0, kt % 4)]))
                    P.step("scalar", lambda e, kt=kt, bcE=bcE: e.activation(
                        ptE(kt), sE(kt), AF.Exp, bias=bcE), ew, sS)
                    ew2 = []
                    if (1, kt % 4) in pv_slot:
                        ew2.append((tS, pv_slot[(1, kt % 4)]))
                    P.step("scalar", lambda e, kt=kt, bcO=bcO: e.activation(
                        ptO(kt), sO(kt), AF.Exp, bias=bcO), ew2, sS)
                    exp_bank[kt % 2] = sS.n
                    expn[kt] = sS.n
                    if kt > 1:
                        kp = kt - 2
                        pwE = [(sS, expn[kp])]
                        if kp == 0:
                            pwE.append(zfree[0])
                            pwE.append(zfree[1])
                        P.step("tensor", lambda e, kp=kp, qq=qq, hE=hE: e.matmul(
                            zE, Vv[:, kp, hE, :], ptE(kp),
                            start=(kp == 0), stop=False), pwE, None)
                        P.step("tensor", lambda e, kp=kp, qq=qq, hO=hO: e.matmul(
                            zO, Vv[:, kp, hO, :], ptO(kp),
                            start=(kp == 0), stop=False), (), tS)
                        pv_slot[(0, kp % 4)] = tS.n
                        pv_slot[(1, kp % 4)] = tS.n
                    if kt >= 5 and kt % 2 == 1:
                        pop()
                for kp in (KT - 2, KT - 1):
                    P.step("tensor", lambda e, kp=kp, qq=qq, hE=hE: e.matmul(
                        zE, Vv[:, kp, hE, :], ptE(kp),
                        start=False, stop=(kp == KT - 1)), [(sS, expn[kp])], None)
                    P.step("tensor", lambda e, kp=kp, qq=qq, hO=hO: e.matmul(
                        zO, Vv[:, kp, hO, :], ptO(kp),
                        start=False, stop=(kp == KT - 1)), (), tS)
                    pv_slot[(0, kp % 4)] = tS.n
                    pv_slot[(1, kp % 4)] = tS.n
                pvall = tS.n
                P.step("vector", lambda e, qq=qq: e.tensor_copy(
                    zr(0)[:, qq * NT:(qq + 1) * NT], zE),
                    [(tS, pvall), (vS, last_div[0])], vS)
                zfree[0] = (vS, vS.n)
                P.step("vector", lambda e, qq=qq: e.tensor_copy(
                    zr(1)[:, qq * NT:(qq + 1) * NT], zO),
                    [(vS, last_div[1])], vS)
                zfree[1] = (vS, vS.n)
                last_zevac = vS.n

                P.step("vector", lambda e, qq=qq: e.tensor_copy(
                    LF[64:65, qq * NT:(qq + 1) * NT], ps2[64:65, 0:NT]), (), vS)
                P.step("vector", lambda e, qq=qq: e.tensor_copy(
                    LF[64:65, 2048 + qq * NT: 2048 + (qq + 1) * NT],
                    ps2[64:65, NT:2 * NT]), (), vS)
                zfree[0] = (vS, vS.n)
                zfree[1] = (vS, vS.n)
                for part, off in ((0, qq * NT), (1, 2048 + qq * NT)):
                    def _rc(e, off=off, part=part, qq=qq):
                        with nc.allow_low_precision(reason="softmax denom to bf16"):
                            return e.reciprocal(
                                ROWS[64:65, part * T + qq * NT: part * T + (qq + 1) * NT],
                                LF[64:65, off:off + NT])
                    rw = [(tS, lin_n[0])] if (qq == 0 and part == 0) else ()
                    if qq == 3 and defer_recips:
                        deferred.append(lambda _rc=_rc, rw=rw: P.step("vector", _rc, rw, vS))
                    else:
                        P.step("vector", _rc, rw, vS)
            if j == 0:
                dump("dbg_zraw", zr(0), [(vS, last_zevac)])
            if not defer_recips:
                recs_n[j] = vS.n
            pop(len(packets))
            return slast_n, deferred

        load_wblock(("q", 0), w_qkv_qk[:, 0, :], 0)
        load_wblock(("k", 0), w_qkv_qk[:, 8, :], 1)
        sub_free = [(vS, vevac[3]), (vS, vevac[3])]
        for side in range(2):
            for nt in range(4):
                pk_qkv_group(0, side, nt)()
            pk_stats(0, side)()
        load_wblock(("q", 1), w_qkv_qk[:, 1, :], 0)
        load_wblock(("k", 1), w_qkv_qk[:, 9, :], 1)
        for side in range(2):
            for hh in range(2):
                pk_chain_half(0, side, hh)()
        pk_bias(0)()
        if "dbg_qr" in dbg_t:
            dump("dbg_qr", f(0), [(vS, vS.n)])
            dump("dbg_lhq", lhq[:], [(vS, cdone)])
            dump("dbg_rows", ROWS[:], [(vS, rows_n[(0, 1)])])
        P.step("vector", lambda e: e.tensor_mul(f(2), f(0), g(0)),
               [(vS, chain_out[(0, 0, 1)]), (vS, chain_out[(0, 1, 1)])], vS)
        P.step("vector", lambda e: e.tensor_mul(f(3), f(1), g(1)), (), vS)
        appd_n[0] = vS.n
        if "dbg_q" in dbg_t:
            dump("dbg_q", f(2), [(vS, appd_n[0])])
            dump("dbg_sclq", g(0), [(vS, appd_n[0])])

        for j in range(NP):
            packets = []
            if j > 0:
                packets.append(pk_divides(j - 1))
            if j + 1 < NP:
                p = j + 1
                for side in range(2):
                    for nt in range(4):
                        packets.append(pk_qkv_group(p, side, nt))
                    packets.append(pk_stats(p, side))
                for side in range(2):
                    for hh in range(2):
                        packets.append(pk_chain_half(p, side, hh))
                packets.append(pk_bias(p))
            slast, deferred = emit_attention(j, packets)

            if j + 1 < NP:
                if j + 2 < NP:
                    load_wblock(("q", j + 2), w_qkv_qk[:, j + 2, :], 0)
                    load_wblock(("k", j + 2), w_qkv_qk[:, 10 + j, :], 1)
                else:
                    load_wblock(("o", 0), w_o[:, 0, :], 0)
                    load_wblock(("o", 1), w_o[:, 1, :], 1)
                P.step("vector", lambda e: e.tensor_mul(f(2), f(0), g(0)),
                       [(vS, chain_out[(j + 1, 0, 1)]), (vS, chain_out[(j + 1, 1, 1)]),
                        (tS, slast)], vS)
                P.step("vector", lambda e: e.tensor_mul(f(3), f(1), g(1)), (), vS)
                appd_n[j + 1] = vS.n
                for dr in deferred:
                    dr()
                recs_n[j] = vS.n

        pk_divides(NP - 1)()
        zdone_v = vS.n
        zdone_i = si.n
        pv_final = tS.n
        dump("dbg_zbuf", D[:], [(vS, zdone_v), (si, zdone_i)])

        psW = [ps1, ps2]
        woev = [sS.n, sS.n]
        woev_list = []
        wo_last_mm = [0]
        xdma_n = {}
        x1c = []
        sqm = []
        Csq = lambda c: C[:, c * T:(c + 1) * T]

        def emit_x1_add(c):
            stg = XS if c % 2 == 0 else YST
            s_ = sx0 if c % 2 == 0 else sx1
            P.step("vector", lambda e, c=c, stg=stg: e.tensor_add(
                Ach[c], stg[:], B[:, c * T:(c + 1) * T]),
                [(s_, xdma_n[c]), (sS, woev_list[c])], vS)
            x1c.append(vS.n)
            P.step("vector", lambda e, c=c: e.tensor_mul(Csq(c), Ach[c], Ach[c]),
                   (), vS)
            sqm.append(vS.n)

        for o in range(NC):
            reg = psW[o % 2]
            basew = [(si, wld[("o", o)]), (vS, zdone_v), (si, zdone_i),
                     (sS, woev[o % 2])]
            if o < 2:
                basew += [(sS, exp_bank[0]), (sS, exp_bank[1])]
            for c in range(NC):
                for nt in range(4):
                    last = (c == NC - 1) and (nt == 3)
                    P.step("tensor", lambda e, o=o, c=c, nt=nt, reg=reg, last=last: e.matmul(
                        reg[:, nt * NT:(nt + 1) * NT], wbv[o % 2][:, c, :],
                        zb[:, c, nt * NT:(nt + 1) * NT], start=(c == 0), stop=last),
                        basew, tS if last else None)
            mmd = tS.n
            wo_last_mm[0] = mmd
            consume_wb(o % 2, tS, mmd)
            if o >= 2:
                emit_x1_add(o - 2)
            stgx = XS if o % 2 == 0 else YST
            sx_ = sx0 if o % 2 == 0 else sx1
            if o == 0:
                wvx = [(vS, chain_out[(NP - 1, 0, 1)])]
            elif o == 1:
                wvx = [(vS, chain_out[(NP - 1, 1, 1)])]
            else:
                wvx = [(vS, x1c[o - 2])]
            P.step("sync", lambda e, o=o, stgx=stgx: e.dma_start(
                out=stgx[:], in_=x_in[:, o, :]), wvx, sx_, 16)
            xdma_n[o] = sx_.n
            if o + 2 < NC:
                load_wblock(("o", o + 2), w_o[:, o + 2, :], o % 2)
            elif o + 2 == NC:
                load_wblock(("f2", "r", 0), w_r[:, 0, :], 0)
            elif o + 2 == NC + 1:
                load_wblock(("f2", "r", 1), w_r[:, 1, :], 1)
            P.step("scalar", lambda e, o=o, reg=reg: e.activation(
                B[:, o * T:(o + 1) * T], reg[:], AF.Identity, bias=boc[:, o:o + 1]),
                [(tS, mmd)], sS)
            woev[o % 2] = sS.n
            woev_list.append(sS.n)
        aod = sS.n
        emit_x1_add(6)
        emit_x1_add(7)
        dump("dbg_ao", B[:], [(sS, aod)])

        m2 = ln_stats_apply(Ach, Csq,
                            [[(vS, x1c[c])] for c in range(NC)],
                            row_reg=ps1, t1_reg=ps2, t2_reg=ps1,
                            row_extra=[(sS, woev_list[6])],
                            t1_extra=[(sS, aod)],
                            ebuf_guard=[(tS, pv_final)],
                            g_guard=[(vS, zdone_v), (si, zdone_i)],
                            sq_pre=sqm)
        ln2tiles = m2["tiles"]

        ln2f = D[:].rearrange("p (c t) -> p c t", c=NC)
        xf = C[:, 0:16384].rearrange("p (c t) -> p c t", c=NC)
        xkf = A[:].rearrange("p (c t) -> p c t", c=NC)
        kaf = D[:].rearrange("p (c t) -> p c t", c=NC)

        def r_ap(o):
            if o < 4:
                return Fb[:, o * T:(o + 1) * T]
            if o < 6:
                return Ebuf[:, (o - 4) * T:(o - 3) * T]
            return G[:, (o - 4) * T:(o - 3) * T]

        psW2 = [ps1, ps2]
        XSB = XS[:].bitcast(BF16)
        apl = []
        for c in range(NC):
            aw = [(vS, ln2tiles)]
            P.step("vector", lambda e, c=c: e.tensor_mul(
                XSB[:, 0:T], Ach[c], g(2)), aw, vS)
            P.step("vector", lambda e, c=c: e.tensor_sub(
                ln2f[:, c, :], XSB[:, 0:T], g(3)), (), vS)
            apl.append(vS.n)
        if "dbg_ln2" in dbg_t:
            dump("dbg_ln2", D[:, 0:8192], [(vS, vS.n)])
        mixd = {}

        def emit_mix(xv, mu, om, tag):
            for c in range(NC):
                P.step("vector", lambda e, c=c, mu=mu: e.tensor_scalar_mul(
                    VBT2[c], ln2f[:, c, :], mu[:, c:c + 1]), [(vS, apl[c])] + mixd.get(
                        ("free", tag), []), vS)
                P.step("vector", lambda e, c=c, xv=xv, om=om: e.scalar_tensor_tensor(
                    xv[:, c, 1:T], ln2f[:, c, 0:T - 1], om[:, c:c + 1],
                    VBT2[c][:, 1:T], ALU.mult, ALU.add), (), vS)
                P.step("vector", lambda e, c=c, xv=xv, om=om: e.tensor_scalar_mul(
                    xv[:, c, 0:1], VBT2[c][:, 0:1], 1.0), (), vS)
            mixd[tag] = vS.n

        VBT2 = [XSB[:, 0:T] for _ in range(NC)]

        def emit_mat(tag, wsrc, rhs_v, out_cb, first_extra=()):
            for o in range(NC):
                slot = o % 2
                if ("f2", tag, o) not in wld:
                    load_wblock(("f2", tag, o), wsrc[:, o, :], slot)
                reg = psW2[o % 2]
                basew = [(si, wld[("f2", tag, o)]), (vS, mixd[tag]),
                         matev[o % 2][0], matev[o % 2][1]]
                if o < 2:
                    basew += list(first_extra)
                for c in range(NC):
                    for nt in range(4):
                        last = (c == NC - 1) and (nt == 3)
                        P.step("tensor", lambda e, c=c, nt=nt, reg=reg, slot=slot,
                               rhs_v=rhs_v, last=last: e.matmul(
                                   reg[:, nt * NT:(nt + 1) * NT], wbv[slot][:, c, :],
                                   rhs_v[:, c, nt * NT:(nt + 1) * NT],
                                   start=(c == 0), stop=last),
                               basew, tS if last else None)
                mmn = tS.n
                consume_wb(slot, tS, mmn)
                out_cb(o, reg, mmn)

        matev = [[(vS, ln2tiles), (sS, sS.n)], [(vS, ln2tiles), (sS, sS.n)]]
        rsig_n = {}

        emit_mix(xf, murc, omrc, "r")
        emit_mix(xkf, mukc, omkc, "k")
        rmm_last = [0]

        def r_out(o, reg, mmn):
            P.step("scalar", lambda e, o=o, reg=reg: e.activation(
                r_ap(o), reg[:], AF.Sigmoid), [(tS, mmn)], sS)
            rsig_n[o] = sS.n
            matev[o % 2] = [(vS, vS.n), (sS, sS.n)]
            rmm_last[0] = mmn
        emit_mat("r", w_r, xf, r_out)
        if "dbg_r" in dbg_t:
            dump("dbg_r", Fb[:, 0:8192], [(sS, sS.n)])

        def k_out(o, reg, mmn):
            P.step("vector", lambda e, o=o, reg=reg: e.tensor_scalar_max(
                B2T2[o], reg[:], 0.0), [(tS, mmn)], vS)
            P.step("vector", lambda e, o=o: e.tensor_mul(
                kaf[:, o, :], B2T2[o], B2T2[o]), (), vS)
            matev[o % 2] = [(vS, vS.n), (sS, sS.n)]
        YSTB = YST[:].bitcast(BF16)
        B2T2 = [YSTB[:, 0:T] for _ in range(NC)]
        emit_mat("k", w_k2, xkf, k_out, first_extra=[(vS, mixd["k"])])
        if "dbg_ka" in dbg_t:
            dump("dbg_ka", D[:, 0:8192], [(vS, vS.n)])
        mixd["v"] = vS.n
        ydma_slot = [0, 0]
        xsfree = [vS.n]

        def v_out(o, reg, mmn):
            P.step("gpsimd", lambda e, o=o: e.dma_start(
                out=XS[:], in_=x_in[:, o, :]), [(vS, xsfree[0])], sb, 16)
            xld = sb.n
            P.step("vector", lambda e, o=o, reg=reg: e.scalar_tensor_tensor(
                YST[:, 0:1024], reg[:, 0:1024], 1.0, r_ap(o)[:, 0:1024],
                ALU.mult, ALU.mult),
                [(tS, mmn), (so, ydma_slot[0]), (sS, rsig_n[o])], vS)
            P.step("vector", lambda e, o=o: e.tensor_add(
                YST[:, 0:1024], YST[:, 0:1024], B[:, o * T:o * T + 1024]), (), vS)
            P.step("vector", lambda e, o=o: e.tensor_add(
                YST[:, 0:1024], YST[:, 0:1024], XS[:, 0:1024]), [(sb, xld)], vS)
            f1 = vS.n
            P.step("scalar", lambda e, o=o: e.dma_start(
                out=y_out[:, o, 0:1024], in_=YST[:, 0:1024]), [(vS, f1)], so, 16)
            P.step("vector", lambda e, o=o, reg=reg: e.scalar_tensor_tensor(
                YST[:, 1024:2048], reg[:, 1024:2048], 1.0, r_ap(o)[:, 1024:2048],
                ALU.mult, ALU.mult), (), vS)
            P.step("vector", lambda e, o=o: e.tensor_add(
                YST[:, 1024:2048], YST[:, 1024:2048],
                B[:, o * T + 1024:(o + 1) * T]), (), vS)
            P.step("vector", lambda e, o=o: e.tensor_add(
                YST[:, 1024:2048], YST[:, 1024:2048], XS[:, 1024:2048]), (), vS)
            f2_ = vS.n
            xsfree[0] = vS.n
            P.step("scalar", lambda e, o=o: e.dma_start(
                out=y_out[:, o, 1024:2048], in_=YST[:, 1024:2048]), [(vS, f2_)], so, 16)
            ydma_slot[0] = so.n
            matev[o % 2] = [(vS, vS.n), (sS, sS.n)]
        emit_mat("v", w_v2, kaf, v_out)
        P.step("sync", lambda e, n=so.n: e.wait_ge(so.h, n), (), None)

        @block.sync
        def _(e):
            for fn in P.ops["sync"]:
                fn(e)

        @block.gpsimd
        def _(e):
            for fn in P.ops["gpsimd"]:
                fn(e)

        @block.scalar
        def _(e):
            for fn in P.ops["scalar"]:
                fn(e)

        @block.vector
        def _(e):
            for fn in P.ops["vector"]:
                fn(e)

        @block.tensor
        def _(e):
            for fn in P.ops["tensor"]:
                fn(e)

    return nc


def _prep_inputs(L0_attn_x, L0_attn_Wqkv, L0_attn_bqkv, L0_attn_Wo, L0_attn_bo,
                 L0_attn_wq, L0_attn_wk, L0_ffn_W_r, L0_ffn_W_k, L0_ffn_W_v,
                 L0_ffn_mu_r, L0_ffn_mu_k):
    f32 = np.float32
    bf16 = ml_dtypes.bfloat16
    x = np.asarray(L0_attn_x, f32)

    def fm(w):
        wt = np.ascontiguousarray(np.asarray(w, f32).T)
        return np.ascontiguousarray(wt.reshape(8, 128, -1).transpose(1, 0, 2))

    def fm_slab(w):
        wt = np.asarray(w, f32).T
        O = wt.shape[1]
        a = wt.reshape(8, 128, O // 128, 128).transpose(1, 2, 0, 3)
        return np.ascontiguousarray(a.reshape(128, O // 128, 1024))

    wqkv_full = np.asarray(L0_attn_Wqkv, f32)
    wqkv_qk = fm_slab(wqkv_full[0:2048, :])
    wqkv_v = fm(wqkv_full[2048:3072, :])
    wo = fm_slab(L0_attn_Wo)
    wr = fm_slab(L0_ffn_W_r)
    wk2 = fm_slab(L0_ffn_W_k)
    wv2 = fm_slab(L0_ffn_W_v)
    bqkv = np.asarray(L0_attn_bqkv, f32)
    b_qkv = np.ascontiguousarray(bqkv.reshape(24, 128).T)
    b_vrow = np.ascontiguousarray(bqkv[2048:3072].reshape(1, 1024))
    b_o = np.ascontiguousarray(np.asarray(L0_attn_bo, f32).reshape(8, 128).T)
    mu_r = np.ascontiguousarray(np.asarray(L0_ffn_mu_r, f32).reshape(8, 128).T)
    mu_k = np.ascontiguousarray(np.asarray(L0_ffn_mu_k, f32).reshape(8, 128).T)
    wq_col = np.ascontiguousarray(np.tile(np.asarray(L0_attn_wq, f32), 2).reshape(128, 1))
    wk_col = np.ascontiguousarray(np.tile(np.asarray(L0_attn_wk, f32), 2).reshape(128, 1))
    cst_ln = np.zeros((128, 4), bf16)
    cst_ln[:, 0] = bf16(1.0 / 1024.0)
    cst_ln[:, 3] = bf16(1.0 / 1024.0)
    cst_ind = np.zeros((128, 4), bf16)
    cst_ind[0:64, 0] = bf16(1.0 / 64.0)
    cst_ind[64:128, 1] = bf16(1.0 / 64.0)
    cst_b2 = np.zeros((2, 128), bf16)
    cst_b2[0, :] = bf16(1.0)
    cst_b2b = np.zeros((2, 128), bf16)
    cst_b2b[1, :] = bf16(1.0)
    cst_i4r = np.zeros((4, 128), bf16)
    cst_i4r[0, 0:64] = bf16(1.0)
    cst_i4r[1, 64:128] = bf16(1.0)
    cst_i4n = np.zeros((4, 128), bf16)
    cst_i4n[2, 0:64] = bf16(1.0)
    cst_i4n[3, 64:128] = bf16(1.0)

    shared = dict(w_qkv_qk=wqkv_qk, w_qkv_v=wqkv_v, w_o=wo, w_r=wr, w_k2=wk2,
                  w_v2=wv2, b_qkv=b_qkv,
                  b_vrow=b_vrow, b_o=b_o, mu_r_c=mu_r, mu_k_c=mu_k,
                  wq_col=wq_col, wk_col=wk_col, cst_ln=cst_ln, cst_ind=cst_ind,
                  cst_b2=cst_b2, cst_b2b=cst_b2b, cst_i4r=cst_i4r, cst_i4n=cst_i4n)
    in_maps = []
    for b in range(8):
        xb = np.ascontiguousarray(x[b].T.reshape(8, 128, T).transpose(1, 0, 2))
        m = dict(shared)
        m["x_in"] = xb
        in_maps.append(m)
    return in_maps


def kernel(**inputs):
    in_maps = _prep_inputs(**inputs)
    nc = build_program()
    res = run_bass_kernel_spmd(nc, in_maps, core_ids=list(range(8)))
    outs = []
    for b in range(8):
        yb = res.results[b]["y"]
        outs.append(yb.transpose(1, 0, 2).reshape(1024, T).T)
    return np.ascontiguousarray(np.stack(outs, axis=0), dtype=np.float32)


# version 13
# speedup vs baseline: 1.2284x; 1.0043x over previous
import contextlib
import numpy as np
import ml_dtypes
import concourse.bass as bass
import concourse.mybir as mybir
from concourse.bass_utils import run_bass_kernel_spmd

F32 = mybir.dt.float32
BF16 = mybir.dt.bfloat16
AX = mybir.AxisListType.X
AF = mybir.ActivationFunctionType
ALU = mybir.AluOpType

T = 2048
NC = 8
NT = 512
NH = 16
HD = 64
NP = 8
KT = 16
LN_EPS = 1e-5
RMS_EPS = 1e-6
BIAS_SHIFT = 30.0


class Sem:
    def __init__(self, h, owner):
        self.h, self.owner, self.n = h, owner, 0


class Plan:
    ENGS = ("sync", "gpsimd", "scalar", "vector", "tensor")

    def __init__(self):
        self.ops = {e: [] for e in self.ENGS}
        self.last_wait = {e: {} for e in self.ENGS}

    def step(self, eng, emitfn, waits=(), inc=None, inc_amt=1):
        lw = self.last_wait[eng]
        ws = []
        for s, n in waits:
            if s.owner == eng or n <= 0:
                continue
            if lw.get(id(s), -1) >= n:
                continue
            assert n <= s.n, f"wait on future count {n} > {s.n} ({eng})"
            lw[id(s)] = n
            ws.append((s, n))
        if inc is not None:
            inc.n += inc_amt

        def f(e):
            for s, n in ws:
                e.wait_ge(s.h, n)
            ins = emitfn(e)
            if inc is not None:
                ins.then_inc(inc.h, inc_amt)

        self.ops[eng].append(f)


def build_program(dbg=frozenset()):
    nc = bass.Bass("TRN2", target_bir_lowering=False)

    x_in = nc.dram_tensor("x_in", [128, NC, T], F32, kind="ExternalInput").ap()
    w_qkv_qk = nc.dram_tensor("w_qkv_qk", [128, 16, 1024], F32, kind="ExternalInput").ap()
    w_qkv_v = nc.dram_tensor("w_qkv_v", [128, NC, 1024], F32, kind="ExternalInput").ap()
    w_o = nc.dram_tensor("w_o", [128, NC, 1024], F32, kind="ExternalInput").ap()
    w_r = nc.dram_tensor("w_r", [128, NC, 1024], F32, kind="ExternalInput").ap()
    w_k2 = nc.dram_tensor("w_k2", [128, NC, 1024], F32, kind="ExternalInput").ap()
    w_v2 = nc.dram_tensor("w_v2", [128, NC, 1024], F32, kind="ExternalInput").ap()
    b_qkv = nc.dram_tensor("b_qkv", [128, 24], F32, kind="ExternalInput").ap()
    b_vrow = nc.dram_tensor("b_vrow", [1, 1024], F32, kind="ExternalInput").ap()
    b_o = nc.dram_tensor("b_o", [128, 8], F32, kind="ExternalInput").ap()
    mu_r_c = nc.dram_tensor("mu_r_c", [128, 8], F32, kind="ExternalInput").ap()
    mu_k_c = nc.dram_tensor("mu_k_c", [128, 8], F32, kind="ExternalInput").ap()
    wq_col = nc.dram_tensor("wq_col", [128, 1], F32, kind="ExternalInput").ap()
    wk_col = nc.dram_tensor("wk_col", [128, 1], F32, kind="ExternalInput").ap()
    cst_ln = nc.dram_tensor("cst_ln", [128, 4], BF16, kind="ExternalInput").ap()
    cst_b2 = nc.dram_tensor("cst_b2", [2, 128], BF16, kind="ExternalInput").ap()
    cst_b2b = nc.dram_tensor("cst_b2b", [2, 128], BF16, kind="ExternalInput").ap()
    cst_i4r = nc.dram_tensor("cst_i4r", [4, 128], BF16, kind="ExternalInput").ap()
    cst_i4n = nc.dram_tensor("cst_i4n", [4, 128], BF16, kind="ExternalInput").ap()
    cst_ind = nc.dram_tensor("cst_ind", [128, 4], BF16, kind="ExternalInput").ap()

    y_out = nc.dram_tensor("y", [128, NC, T], F32, kind="ExternalOutput").ap()
    dbg_t = {}
    for name, shape, dt in [
        ("dbg_ln1", [128, NC * T], BF16),
        ("dbg_q", [128, T], BF16),
        ("dbg_sclq", [128, T], BF16),
        ("dbg_pt", [128, T], BF16),
        ("dbg_zraw", [65, T], BF16),
        ("dbg_zbuf", [128, NC * T], BF16),
        ("dbg_ao", [128, NC * T], BF16),
        ("dbg_vp", [128, NH * 65], BF16),
        ("dbg_ln2", [128, NC * 1024], BF16),
        ("dbg_xr", [128, NC * 1024], BF16),
        ("dbg_r", [128, NC * 1024], BF16),
        ("dbg_ka", [128, NC * 1024], BF16),
        ("dbg_bcol", [128, 6], F32),
        ("dbg_qr", [128, T], BF16),
        ("dbg_rows", [128, 2 * T], BF16),
        ("dbg_lhq", [128, 4], BF16),
    ]:
        if name in dbg:
            dbg_t[name] = nc.dram_tensor(name, shape, dt, kind="ExternalOutput").ap()

    P = Plan()
    with contextlib.ExitStack() as st:
        E = st.enter_context
        A = E(nc.sbuf_tensor("A", [128, NC * T], BF16))
        B = E(nc.sbuf_tensor("B", [128, NC * T], BF16))
        C = E(nc.sbuf_tensor("C", [128, KT * NH * 65], BF16))
        D = E(nc.sbuf_tensor("D", [128, NC * T], BF16))
        Ebuf = E(nc.sbuf_tensor("Ebuf", [128, 2 * T], BF16))
        Fb = E(nc.sbuf_tensor("Fb", [128, 4 * T], BF16))
        G = E(nc.sbuf_tensor("G", [128, 4 * T], BF16))
        ROWS = E(nc.sbuf_tensor("ROWS", [128, 2 * T], BF16))
        XS = E(nc.sbuf_tensor("XS", [128, T], F32))
        YST = E(nc.sbuf_tensor("YST", [128, T], F32))
        WS = E(nc.sbuf_tensor("WS", [128, 2 * 1024], F32))
        WB = E(nc.sbuf_tensor("WB", [128, 2 * 1024], BF16))
        bqc = E(nc.sbuf_tensor("bqc", [128, 24], F32))
        boc = E(nc.sbuf_tensor("boc", [128, 8], F32))
        murc = E(nc.sbuf_tensor("murc", [128, 8], F32))
        mukc = E(nc.sbuf_tensor("mukc", [128, 8], F32))
        omrc = E(nc.sbuf_tensor("omrc", [128, 8], F32))
        omkc = E(nc.sbuf_tensor("omkc", [128, 8], F32))
        wqc = E(nc.sbuf_tensor("wqc", [128, 1], F32))
        wkc = E(nc.sbuf_tensor("wkc", [128, 1], F32))
        lnT = E(nc.sbuf_tensor("lnT", [128, 4], BF16))
        lhq = E(nc.sbuf_tensor("lhq", [128, 4], BF16))
        lhk = E(nc.sbuf_tensor("lhk", [128, 4], BF16))
        b2 = E(nc.sbuf_tensor("b2", [2, 128], BF16))
        b2b = E(nc.sbuf_tensor("b2b", [2, 128], BF16))
        i4r = E(nc.sbuf_tensor("i4r", [4, 128], BF16))
        i4n = E(nc.sbuf_tensor("i4n", [4, 128], BF16))
        CCOL = E(nc.sbuf_tensor("CCOL", [128, 8], F32))
        bcole = E(nc.sbuf_tensor("bcole", [128, 2], F32))
        bcolo = E(nc.sbuf_tensor("bcolo", [128, 2], F32))
        bndc = E(nc.sbuf_tensor("bndc", [128, 8], BF16))
        epsc = E(nc.sbuf_tensor("epsc", [128, 2], F32))
        ps1 = E(nc.psum_tensor("ps1", [128, T], F32))
        ps2 = E(nc.psum_tensor("ps2", [128, T], F32))
        sw = Sem(E(nc.semaphore("sw")), "sync")
        sw0 = Sem(E(nc.semaphore("sw0")), "sync")
        sw1 = Sem(E(nc.semaphore("sw1")), "sync")
        sx0 = Sem(E(nc.semaphore("sx0")), "sync")
        sx1 = Sem(E(nc.semaphore("sx1")), "sync")
        si = Sem(E(nc.semaphore("si")), "gpsimd")
        sb = Sem(E(nc.semaphore("sb")), "gpsimd")
        so = Sem(E(nc.semaphore("so")), "sync")
        tS = Sem(E(nc.semaphore("tS")), "tensor")
        vS = Sem(E(nc.semaphore("vS")), "vector")
        sS = Sem(E(nc.semaphore("sS")), "scalar")
        block = E(nc.Block())

        Vv = C[:].rearrange("p (t h e) -> p t h e", t=KT, h=NH, e=65)
        zb = D[:].rearrange("p (c t) -> p c t", c=NC)
        wbv = [WB[:, i * 1024:(i + 1) * 1024].rearrange("p (c o) -> p c o", c=NC)
               for i in range(2)]
        Ach = [A[:, c * T:(c + 1) * T] for c in range(NC)]
        f = lambda i: Fb[:, i * T:(i + 1) * T]
        g = lambda i: G[:, i * T:(i + 1) * T]
        e0 = Ebuf[:, 0:T]
        e1 = Ebuf[:, T:2 * T]
        ptb = lambda p: Ebuf[:, p * T:(p + 1) * T]
        ptE = lambda k: Ebuf[:, (k % 4) * NT:(k % 4 + 1) * NT]
        ptO = lambda k: Ebuf[:, T + (k % 4) * NT:T + (k % 4 + 1) * NT]
        VBT = G[:, 0:1024]

        def dump(name, ap, waits):
            if name in dbg_t:
                tgt = dbg_t[name]
                P.step("sync", lambda e, a=ap, t=tgt: e.dma_start(out=t, in_=a),
                       waits, so, 16)

        ws_cast = [0, 0]
        wb_read = [(tS, 0), (tS, 0)]
        wld = {}

        def load_wblock(key, src, slot):
            s_ = sw0 if slot == 0 else sw1
            P.step("sync", lambda e, src=src, slot=slot: e.dma_start(
                out=WS[:, slot * 1024:(slot + 1) * 1024], in_=src),
                [(si, ws_cast[slot])], s_, 16)
            P.step("gpsimd", lambda e, slot=slot: e.tensor_copy(
                WB[:, slot * 1024:(slot + 1) * 1024], WS[:, slot * 1024:(slot + 1) * 1024]),
                [(s_, s_.n), wb_read[slot]], si, 1)
            wld[key] = si.n
            ws_cast[slot] = si.n

        def consume_wb(slot, sem, count):
            wb_read[slot] = (sem, count)

        for dst, src in [(bqc, b_qkv), (boc, b_o), (murc, mu_r_c), (mukc, mu_k_c),
                         (wqc, wq_col), (wkc, wk_col), (lnT, cst_ln)]:
            P.step("sync", lambda e, d=dst, s=src: e.dma_start(out=d[:], in_=s),
                   (), sw, 16)
        P.step("sync", lambda e: e.dma_start(out=lhq[:], in_=cst_ind), (), sw, 16)
        P.step("sync", lambda e: e.dma_start(out=lhk[:], in_=cst_ind), (), sw, 16)
        P.step("sync", lambda e: e.dma_start(out=b2[:], in_=cst_b2), (), sw, 16)
        P.step("sync", lambda e: e.dma_start(out=b2b[:], in_=cst_b2b), (), sw, 16)
        P.step("sync", lambda e: e.dma_start(out=i4r[:], in_=cst_i4r), (), sw, 16)
        P.step("sync", lambda e: e.dma_start(out=i4n[:], in_=cst_i4n), (), sw, 16)
        cw = sw.n
        P.step("scalar", lambda e: e.activation(lhq[0:64, 2:3], wqc[0:64, :], AF.Square),
               [(sw, cw)], sS)
        P.step("scalar", lambda e: e.activation(lhq[64:128, 3:4], wqc[64:128, :], AF.Square),
               (), sS)
        P.step("scalar", lambda e: e.activation(lhk[0:64, 2:3], wkc[0:64, :], AF.Square),
               (), sS)
        P.step("scalar", lambda e: e.activation(lhk[64:128, 3:4], wkc[64:128, :], AF.Square),
               (), sS)
        sq2 = sS.n
        P.step("vector", lambda e: e.tensor_scalar(omrc[:], murc[:], -1.0, 1.0,
                                                   ALU.mult, ALU.add), [(sw, cw)], vS)
        P.step("vector", lambda e: e.tensor_scalar(omkc[:], mukc[:], -1.0, 1.0,
                                                   ALU.mult, ALU.add), [(sw, cw)], vS)
        P.step("vector", lambda e: e.memset(Vv[:, :, :, 64:65], 1.0), (), vS)
        P.step("vector", lambda e: e.memset(bndc[:], 0.0), (), vS)
        P.step("vector", lambda e: e.memset(epsc[:, 0:1], LN_EPS), (), vS)
        P.step("vector", lambda e: e.memset(epsc[:, 1:2], RMS_EPS), (), vS)
        cdone = vS.n

        def ln_stats_apply(xch, sq_dst, inb, row_reg=None, t1_reg=None, t2_reg=None,
                           row_extra=(), t1_extra=(), ebuf_guard=(), g_guard=(),
                           sq_pre=None):
            row_reg = ps2 if row_reg is None else row_reg
            t1_reg = ps1 if t1_reg is None else t1_reg
            t2_reg = ps2 if t2_reg is None else t2_reg
            if sq_pre is None:
                for c in range(NC):
                    P.step("vector",
                           lambda e, c=c: e.tensor_mul(sq_dst(c), xch[c], xch[c]),
                           list(inb[c]), vS)
                sq_mark = [vS.n] * NC
            else:
                sq_mark = sq_pre
            for c in range(NC):
                for nt in range(4):
                    P.step("tensor",
                           lambda e, c=c, nt=nt: e.matmul(
                               row_reg[0:2, nt * NT:(nt + 1) * NT], lnT[:, 0:2],
                               xch[c][:, nt * NT:(nt + 1) * NT],
                               start=(c == 0), stop=False),
                           list(inb[c]) + [(sw, cw)] + list(row_extra), None)
            for c in range(NC):
                for nt in range(4):
                    last = (c == NC - 1) and (nt == 3)
                    P.step("tensor",
                           lambda e, c=c, nt=nt, last=last: e.matmul(
                               row_reg[0:2, nt * NT:(nt + 1) * NT], lnT[:, 2:4],
                               sq_dst(c)[:, nt * NT:(nt + 1) * NT],
                               start=False, stop=last),
                           [(vS, sq_mark[c])], tS if last else None)
            mmd = tS.n
            P.step("vector", lambda e: e.tensor_copy(ROWS[0:2, 0:T], row_reg[0:2, :]),
                   [(tS, mmd)], vS)
            rowd = vS.n
            for nt in range(4):
                P.step("tensor", lambda e, nt=nt: e.matmul(
                    t1_reg[:, nt * NT:(nt + 1) * NT], b2[:],
                    ROWS[0:2, nt * NT:(nt + 1) * NT], start=True, stop=True),
                    [(vS, rowd)] + list(t1_extra), tS if nt == 3 else None)
            mtd = tS.n
            for nt in range(4):
                P.step("tensor", lambda e, nt=nt: e.matmul(
                    t2_reg[:, nt * NT:(nt + 1) * NT], b2b[:],
                    ROWS[0:2, nt * NT:(nt + 1) * NT], start=True, stop=True),
                    [(vS, rowd)], tS if nt == 3 else None)
            qtd = tS.n
            P.step("scalar", lambda e: e.activation(e0, t1_reg[:], AF.Square),
                   [(tS, mtd)] + list(ebuf_guard), sS)
            sqd2 = sS.n
            P.step("vector", lambda e: e.tensor_sub(e1, t2_reg[:], e0),
                   [(tS, qtd), (sS, sqd2)] + list(ebuf_guard), vS)
            vd = vS.n
            P.step("scalar", lambda e: e.activation(e0, e1, AF.Ln, bias=epsc[:, 0:1]),
                   [(vS, vd)], sS)
            P.step("scalar", lambda e: e.activation(g(2), e0, AF.Exp, scale=-0.5),
                   list(g_guard), sS)
            sd = sS.n
            P.step("vector", lambda e: e.tensor_mul(g(3), t1_reg[:], g(2)),
                   [(sS, sd)] + list(g_guard), vS)
            return {"rows": rowd, "tiles": vS.n, "ps_used": tS.n}

        xcast = []
        for c in range(NC):
            stg = XS if c % 2 == 0 else YST
            s_ = sx0 if c % 2 == 0 else sx1
            wv = [(vS, xcast[c - 2])] if c > 1 else [(vS, cdone)]
            P.step("sync", lambda e, c=c, stg=stg: e.dma_start(out=stg[:], in_=x_in[:, c, :]),
                   wv, s_, 16)
            P.step("vector", lambda e, c=c, stg=stg: e.tensor_copy(Ach[c], stg[:]),
                   [(s_, s_.n)], vS)
            xcast.append(vS.n)
        Bsq = lambda c: B[:, c * T:(c + 1) * T]
        m1 = ln_stats_apply(Ach, Bsq, [[(vS, xcast[c])] for c in range(NC)])
        apl1 = []
        for c in range(NC):
            P.step("vector", lambda e, c=c: e.tensor_mul(Bsq(c), Ach[c], g(2)),
                   [(vS, m1["tiles"])], vS)
            P.step("vector", lambda e, c=c: e.tensor_sub(Ach[c], Bsq(c), g(3)), (), vS)
            apl1.append(vS.n)
        ln1d = vS.n
        dump("dbg_ln1", A[:], [(vS, ln1d)])

        qsq = B[:, 2 * T:3 * T]
        ksq = B[:, 3 * T:4 * T]
        zr = lambda p: B[0:65, p * T:(p + 1) * T]
        zrl = lambda p: B[64:65, p * T:(p + 1) * T]
        LF = B[:, 4 * T:8 * T].bitcast(F32)
        sE = lambda k: ps1[:, (k % 2) * NT:(k % 2 + 1) * NT]
        sO = lambda k: ps1[:, 1024 + (k % 2) * NT:1024 + (k % 2 + 1) * NT]
        zE = ps2[0:65, 0:NT]
        zO = ps2[0:65, NT:2 * NT]
        FA = ps2[:, 1024:2048]
        FAh = [ps2[:, 1024:1536], ps2[:, 1536:2048]]

        last_zevac = vS.n
        last_div = {0: vS.n, 1: vS.n}
        pv_slot = {}
        exp_bank = {0: sS.n, 1: sS.n}
        sub_free = [(vS, 0), (vS, 0)]
        fa_free = (vS, 0)
        bcol_n = {}
        recs_n = {}
        lin_n = [0]
        appd_n = {}
        rows_n = {}
        chain_out = {}

        def _sub(idx, sem, cnt):
            sub_free[idx] = (sem, cnt)

        sq_n = {}

        def pk_qkv_group(p, side, nt):
            sub = nt % 2
            reg = FAh[sub]
            wkey = ("q", p) if side == 0 else ("k", p)
            fraw = f(0) if side == 0 else f(1)
            sq = qsq if side == 0 else ksq
            bias = bqc[:, p:p + 1] if side == 0 else bqc[:, 8 + p:9 + p]

            def emit():
                aw = [(vS, ln1d)] if p == 0 else []
                for c in range(NC):
                    last = c == NC - 1
                    P.step("tensor", lambda e, c=c, last=last: e.matmul(
                        reg, wbv[side][:, c, :],
                        A[:, c * T + nt * NT: c * T + (nt + 1) * NT],
                        start=(c == 0), stop=last),
                        [(si, wld[wkey]), sub_free[sub]] + aw, tS if last else None)
                mmn = tS.n
                if side == 1 and nt == 3:
                    consume_wb(1, tS, mmn)
                if side == 0 and nt == 3:
                    consume_wb(0, tS, mmn)
                P.step("vector", lambda e: e.tensor_scalar_add(
                    fraw[:, nt * NT:(nt + 1) * NT], reg, bias), [(tS, mmn)], vS)
                P.step("vector", lambda e: e.tensor_mul(
                    sq[:, nt * NT:(nt + 1) * NT], fraw[:, nt * NT:(nt + 1) * NT],
                    fraw[:, nt * NT:(nt + 1) * NT]), (), vS)
                _sub(sub, vS, vS.n)
                sq_n[(side, nt)] = vS.n
            return emit

        def pk_stats(p, side):
            sq = qsq if side == 0 else ksq
            lh = lhq if side == 0 else lhk
            off = 0 if side == 0 else T

            def emit():
                bw = [(sb, vbtd)] if p == 0 else []
                for nt in range(4):
                    sub = nt % 2
                    P.step("tensor", lambda e, nt=nt, sub=sub: e.matmul(
                        FAh[sub][0:4, :], lh[:],
                        sq[:, nt * NT:(nt + 1) * NT], start=True, stop=True),
                        [sub_free[sub], (vS, sq_n[(side, nt)])], tS)
                    mmn = tS.n
                    P.step("vector", lambda e, nt=nt, sub=sub: e.tensor_copy(
                        ROWS[0:4, off + nt * NT: off + (nt + 1) * NT], FAh[sub][0:4, :]),
                        [(tS, mmn)] + bw, vS)
                    _sub(sub, vS, vS.n)
                rows_n[(p, side)] = vS.n
            return emit

        def pk_chain_half(p, side, hh):
            off = 0 if side == 0 else T
            sclg = g(0) if side == 0 else g(1)
            nrm = g(2) if side == 0 else g(3)
            wcol = wqc if side == 0 else wkc
            stg = XS if side == 0 else YST

            def emit():
                for i2 in range(2):
                    P.step("tensor", lambda e, i2=i2: e.matmul(
                        FAh[i2], i4r[:],
                        ROWS[0:4, off + hh * 1024 + i2 * NT: off + hh * 1024 + (i2 + 1) * NT],
                        start=True, stop=True),
                        [sub_free[i2], (vS, rows_n[(p, side)])], tS if i2 else None)
                mtn = tS.n
                P.step("scalar", lambda e: e.activation(stg[:, 0:1024], FA, AF.Ln,
                                                        bias=epsc[:, 1:2]), [(tS, mtn)], sS)
                P.step("scalar", lambda e: e.activation(stg[:, 1024:2048], stg[:, 0:1024],
                                                        AF.Exp, scale=-0.5), (), sS)
                lnn = sS.n
                for i2 in range(2):
                    P.step("tensor", lambda e, i2=i2: e.matmul(
                        FAh[i2], i4n[:],
                        ROWS[0:4, off + hh * 1024 + i2 * NT: off + hh * 1024 + (i2 + 1) * NT],
                        start=True, stop=True),
                        [(sS, lnn)], tS if i2 else None)
                ntn = tS.n
                P.step("vector", lambda e: e.tensor_mul(
                    sclg[:, hh * 1024:(hh + 1) * 1024], stg[:, 1024:2048], FA),
                    [(tS, ntn)], vS)
                gw = [(si, si.n)] if side == 1 else []
                P.step("vector", lambda e: e.tensor_mul(
                    nrm[:, hh * 1024:(hh + 1) * 1024],
                    sclg[:, hh * 1024:(hh + 1) * 1024], stg[:, 1024:2048]), gw, vS)
                P.step("vector", lambda e: e.tensor_scalar_mul(
                    sclg[:, hh * 1024:(hh + 1) * 1024], stg[:, 1024:2048], wcol[:, 0:1]),
                    (), vS)
                _sub(0, vS, vS.n)
                _sub(1, vS, vS.n)
                chain_out[(p, side, hh)] = vS.n
            return emit

        def pk_bias(p):
            par = p % 2

            def emit():
                w0 = [(vS, chain_out[(p, 0, 0)]), (vS, chain_out[(p, 0, 1)]),
                      (vS, chain_out[(p, 1, 0)]), (vS, chain_out[(p, 1, 1)])]
                P.step("vector", lambda e: e.reduce_max(CCOL[:, 0:1], g(2), axis=AX),
                       w0, vS)
                P.step("vector", lambda e: e.reduce_max(CCOL[:, 1:2], g(3), axis=AX),
                       (), vS)
                vmax = vS.n
                P.step("scalar", lambda e: e.activation(CCOL[:, 4:6], CCOL[:, 0:2], AF.Ln),
                       [(vS, vmax)], sS)
                lnn = sS.n
                P.step("vector", lambda e: e.reduce_sum(CCOL[:, 6:7], CCOL[:, 4:6],
                                                        axis=AX), [(sS, lnn)], vS)
                vsum = vS.n
                P.step("scalar", lambda e: e.activation(CCOL[:, 2:3], CCOL[:, 6:7],
                                                        AF.Exp, scale=0.5), [(vS, vsum)], sS)
                expn = sS.n
                pb = bcol_n.get(p - 2, (0, 0))[0]
                P.step("vector", lambda e: e.tensor_scalar(CCOL[:, 3:4], CCOL[:, 2:3],
                                                           -1.0, BIAS_SHIFT,
                                                           ALU.mult, ALU.add),
                       [(sS, expn), (sb, pb)], vS)
                biasd = vS.n
                P.step("sync", lambda e, par=par: e.dma_start(
                    out=bcole[:, par:par + 1],
                    in_=CCOL[0:1, 3:4].unsqueeze(1).to_broadcast([1, 128, 1])),
                    [(vS, biasd)], sb, 16)
                P.step("sync", lambda e, par=par: e.dma_start(
                    out=bcolo[:, par:par + 1],
                    in_=CCOL[64:65, 3:4].unsqueeze(1).to_broadcast([1, 128, 1])),
                    [(vS, biasd)], sb, 16)
                bcol_n[p] = (sb.n, par)
                if p == 0 and "dbg_bcol" in dbg_t:
                    dump("dbg_bcol", CCOL[:, 0:6], [(vS, biasd)])
            return emit

        def pk_divides(jp):
            def emit():
                P.step("vector", lambda e: e.memset(G[64:65, 2 * T:2 * T + 64], 1.0),
                       [(vS, recs_n[jp])], vS)
                onesd = vS.n
                for h2 in range(2):
                    for hh in range(2):
                        for i2 in range(2):
                            P.step("tensor", lambda e, hh=hh, i2=i2, h2=h2: e.matmul(
                                FAh[i2][0:64, :], G[64:65, 2 * T:2 * T + 64],
                                ROWS[64:65, h2 * T + hh * 1024 + i2 * NT:
                                     h2 * T + hh * 1024 + (i2 + 1) * NT],
                                start=True, stop=True),
                                [(vS, onesd), sub_free[i2]], tS if i2 else None)
                        linn = tS.n
                        lin_n[0] = linn
                        if h2 == 0:
                            P.step("vector", lambda e, jp=jp, hh=hh: e.tensor_mul(
                                zb[0:64, jp, hh * 1024:(hh + 1) * 1024],
                                zr(0)[0:64, hh * 1024:(hh + 1) * 1024], FA[0:64, :]),
                                [(tS, linn)], vS)
                            last_div[0] = vS.n
                        else:
                            P.step("vector", lambda e, jp=jp, hh=hh: e.tensor_mul(
                                g(3)[0:64, hh * 1024:(hh + 1) * 1024],
                                zr(1)[0:64, hh * 1024:(hh + 1) * 1024], FA[0:64, :]),
                                [(tS, linn), (si, si.n)], vS)
                            last_div[1] = vS.n
                        _sub(0, vS, vS.n)
                        _sub(1, vS, vS.n)
                P.step("gpsimd", lambda e, jp=jp: e.dma_start(
                    out=zb[64:128, jp, :], in_=g(3)[0:64, :]),
                    [(vS, last_div[1])], si, 16)
            return emit

        def emit_attention(j, packets):
            nonlocal last_zevac
            pki = [0]

            def pop(n=1):
                for _ in range(n):
                    if pki[0] < len(packets):
                        packets[pki[0]]()
                        pki[0] += 1

            par = bcol_n[j][1]
            bcE = bcole[:, par:par + 1]
            bcO = bcolo[:, par:par + 1]
            hE, hO = 2 * j, 2 * j + 1
            zfree = {0: (vS, last_zevac), 1: (vS, last_zevac)}
            expn = {}
            smm_kt = {}
            slast_n = 0
            defer_recips = (j + 1 < NP)
            deferred = []
            for qq in range(4):
                for kt in range(KT):
                    swl = [(sS, exp_bank[kt % 2]), (vS, appd_n[j]),
                           (sb, bcol_n[j][0])]
                    P.step("tensor", lambda e, kt=kt, qq=qq: e.matmul(
                        sE(kt), f(3)[0:64, kt * 128:(kt + 1) * 128],
                        f(2)[0:64, qq * NT:(qq + 1) * NT],
                        start=True, stop=True, tile_position=(0, 0)), swl, None)
                    P.step("tensor", lambda e, kt=kt, qq=qq: e.matmul(
                        sO(kt), f(3)[64:128, kt * 128:(kt + 1) * 128],
                        f(2)[64:128, qq * NT:(qq + 1) * NT],
                        start=True, stop=True, tile_position=(64, 0)), (), tS)
                    smm_kt[kt] = tS.n
                    slast_n = tS.n
                    ew = [(tS, smm_kt[kt])]
                    if (0, kt % 4) in pv_slot:
                        ew.append((tS, pv_slot[(0, kt % 4)]))
                    P.step("scalar", lambda e, kt=kt, bcE=bcE: e.activation(
                        ptE(kt), sE(kt), AF.Exp, bias=bcE), ew, sS)
                    ew2 = []
                    if (1, kt % 4) in pv_slot:
                        ew2.append((tS, pv_slot[(1, kt % 4)]))
                    P.step("scalar", lambda e, kt=kt, bcO=bcO: e.activation(
                        ptO(kt), sO(kt), AF.Exp, bias=bcO), ew2, sS)
                    exp_bank[kt % 2] = sS.n
                    expn[kt] = sS.n
                    if kt > 1:
                        kp = kt - 2
                        pwE = [(sS, expn[kp])]
                        if kp == 0:
                            pwE.append(zfree[0])
                            pwE.append(zfree[1])
                        P.step("tensor", lambda e, kp=kp, qq=qq, hE=hE: e.matmul(
                            zE, Vv[:, kp, hE, :], ptE(kp),
                            start=(kp == 0), stop=False), pwE, None)
                        P.step("tensor", lambda e, kp=kp, qq=qq, hO=hO: e.matmul(
                            zO, Vv[:, kp, hO, :], ptO(kp),
                            start=(kp == 0), stop=False), (), tS)
                        pv_slot[(0, kp % 4)] = tS.n
                        pv_slot[(1, kp % 4)] = tS.n
                    if kt >= 5 and kt % 2 == 1:
                        pop()
                for kp in (KT - 2, KT - 1):
                    P.step("tensor", lambda e, kp=kp, qq=qq, hE=hE: e.matmul(
                        zE, Vv[:, kp, hE, :], ptE(kp),
                        start=False, stop=(kp == KT - 1)), [(sS, expn[kp])], None)
                    P.step("tensor", lambda e, kp=kp, qq=qq, hO=hO: e.matmul(
                        zO, Vv[:, kp, hO, :], ptO(kp),
                        start=False, stop=(kp == KT - 1)), (), tS)
                    pv_slot[(0, kp % 4)] = tS.n
                    pv_slot[(1, kp % 4)] = tS.n
                pvall = tS.n
                P.step("vector", lambda e, qq=qq: e.tensor_copy(
                    zr(0)[:, qq * NT:(qq + 1) * NT], zE),
                    [(tS, pvall), (vS, last_div[0])], vS)
                zfree[0] = (vS, vS.n)
                P.step("vector", lambda e, qq=qq: e.tensor_copy(
                    zr(1)[:, qq * NT:(qq + 1) * NT], zO),
                    [(vS, last_div[1])], vS)
                zfree[1] = (vS, vS.n)
                last_zevac = vS.n

                P.step("vector", lambda e, qq=qq: e.tensor_copy(
                    LF[64:65, qq * NT:(qq + 1) * NT], ps2[64:65, 0:NT]), (), vS)
                P.step("vector", lambda e, qq=qq: e.tensor_copy(
                    LF[64:65, 2048 + qq * NT: 2048 + (qq + 1) * NT],
                    ps2[64:65, NT:2 * NT]), (), vS)
                zfree[0] = (vS, vS.n)
                zfree[1] = (vS, vS.n)
                for part, off in ((0, qq * NT), (1, 2048 + qq * NT)):
                    def _rc(e, off=off, part=part, qq=qq):
                        with nc.allow_low_precision(reason="softmax denom to bf16"):
                            return e.reciprocal(
                                ROWS[64:65, part * T + qq * NT: part * T + (qq + 1) * NT],
                                LF[64:65, off:off + NT])
                    rw = [(tS, lin_n[0])] if (qq == 0 and part == 0) else ()
                    if qq == 3 and defer_recips:
                        deferred.append(lambda _rc=_rc, rw=rw: P.step("vector", _rc, rw, vS))
                    else:
                        P.step("vector", _rc, rw, vS)
            if j == 0:
                dump("dbg_zraw", zr(0), [(vS, last_zevac)])
            if not defer_recips:
                recs_n[j] = vS.n
            pop(len(packets))
            return slast_n, deferred

        P.step("sync", lambda e: e.dma_start(out=XS[0:1, 0:1024], in_=b_vrow),
               [(vS, xcast[6])], sw, 16)
        P.step("vector", lambda e: e.tensor_copy(ROWS[0:1, 0:1024], XS[0:1, 0:1024]),
               [(sw, sw.n), (tS, m1["ps_used"])], vS)
        P.step("gpsimd", lambda e: e.dma_start(
            out=VBT[:], in_=ROWS[0:1, 0:1024].unsqueeze(1).to_broadcast([1, 128, 1024])),
            [(vS, vS.n)], sb, 16)
        vbtd = sb.n
        wvd = []
        for c in range(NC):
            load_wblock(("wv", c), w_qkv_v[:, c, :], c % 2)
            P.step("scalar", lambda e, c=c: e.activation(
                D[:, c * 1024:(c + 1) * 1024],
                WB[:, (c % 2) * 1024:((c % 2) + 1) * 1024], AF.Copy),
                [(si, wld[("wv", c)])], sS)
            wvd.append(sS.n)
            consume_wb(c % 2, sS, sS.n)
        load_wblock(("q", 0), w_qkv_qk[:, 0, :], 0)
        load_wblock(("k", 0), w_qkv_qk[:, 8, :], 1)
        psV = [ps1[:, 0:1024], ps1[:, 1024:2048]]
        vevac = [0, 0]
        pre_pk = [pk_qkv_group(0, 0, nt) for nt in range(4)] + [pk_stats(0, 0)] \
            + [pk_qkv_group(0, 1, nt) for nt in range(4)] + [pk_stats(0, 1)]
        ppi = [0]
        for tt in range(KT):
            reg = psV[tt % 2]
            for c in range(NC):
                for nt in range(2):
                    last = (c == NC - 1) and (nt == 1)
                    P.step("tensor", lambda e, tt=tt, c=c, nt=nt, reg=reg, last=last: e.matmul(
                        reg[:, nt * NT:(nt + 1) * NT],
                        A[:, c * T + tt * 128: c * T + (tt + 1) * 128],
                        D[:, c * 1024 + nt * NT: c * 1024 + (nt + 1) * NT],
                        start=(c == 0), stop=last),
                        [(sS, wvd[c]), (vS, vevac[tt % 2]), (vS, apl1[c])],
                        tS if last else None)
            mmd = tS.n
            P.step("vector", lambda e, tt=tt, reg=reg: e.tensor_add(
                Vv[:, tt, :, 0:64],
                reg[:].rearrange("p (h e) -> p h e", h=NH),
                VBT[:].rearrange("p (h e) -> p h e", h=NH)),
                [(tS, mmd), (sb, vbtd)], vS)
            vevac[tt % 2] = vS.n
            if tt >= 6 and ppi[0] < len(pre_pk):
                pre_pk[ppi[0]]()
                ppi[0] += 1
        dump("dbg_vp", Vv[:, 0, :, :], [(vS, vS.n)])


        load_wblock(("q", 1), w_qkv_qk[:, 1, :], 0)
        load_wblock(("k", 1), w_qkv_qk[:, 9, :], 1)
        for side in range(2):
            for hh in range(2):
                pk_chain_half(0, side, hh)()
        pk_bias(0)()
        if "dbg_qr" in dbg_t:
            dump("dbg_qr", f(0), [(vS, vS.n)])
            dump("dbg_lhq", lhq[:], [(vS, cdone)])
            dump("dbg_rows", ROWS[:], [(vS, rows_n[(0, 1)])])
        P.step("vector", lambda e: e.tensor_mul(f(2), f(0), g(0)),
               [(vS, chain_out[(0, 0, 1)]), (vS, chain_out[(0, 1, 1)])], vS)
        P.step("vector", lambda e: e.tensor_mul(f(3), f(1), g(1)), (), vS)
        appd_n[0] = vS.n
        if "dbg_q" in dbg_t:
            dump("dbg_q", f(2), [(vS, appd_n[0])])
            dump("dbg_sclq", g(0), [(vS, appd_n[0])])

        for j in range(NP):
            packets = []
            if j > 0:
                packets.append(pk_divides(j - 1))
            if j + 1 < NP:
                p = j + 1
                for side in range(2):
                    for nt in range(4):
                        packets.append(pk_qkv_group(p, side, nt))
                    packets.append(pk_stats(p, side))
                for side in range(2):
                    for hh in range(2):
                        packets.append(pk_chain_half(p, side, hh))
                packets.append(pk_bias(p))
            slast, deferred = emit_attention(j, packets)

            if j + 1 < NP:
                if j + 2 < NP:
                    load_wblock(("q", j + 2), w_qkv_qk[:, j + 2, :], 0)
                    load_wblock(("k", j + 2), w_qkv_qk[:, 10 + j, :], 1)
                else:
                    load_wblock(("o", 0), w_o[:, 0, :], 0)
                    load_wblock(("o", 1), w_o[:, 1, :], 1)
                P.step("vector", lambda e: e.tensor_mul(f(2), f(0), g(0)),
                       [(vS, chain_out[(j + 1, 0, 1)]), (vS, chain_out[(j + 1, 1, 1)]),
                        (tS, slast)], vS)
                P.step("vector", lambda e: e.tensor_mul(f(3), f(1), g(1)), (), vS)
                appd_n[j + 1] = vS.n
                for dr in deferred:
                    dr()
                recs_n[j] = vS.n

        pk_divides(NP - 1)()
        zdone_v = vS.n
        zdone_i = si.n
        pv_final = tS.n
        dump("dbg_zbuf", D[:], [(vS, zdone_v), (si, zdone_i)])

        psW = [ps1, ps2]
        woev = [sS.n, sS.n]
        woev_list = []
        wo_last_mm = [0]
        xdma_n = {}
        x1c = []
        sqm = []
        Csq = lambda c: C[:, c * T:(c + 1) * T]

        def emit_x1_add(c):
            stg = XS if c % 2 == 0 else YST
            s_ = sx0 if c % 2 == 0 else sx1
            P.step("vector", lambda e, c=c, stg=stg: e.tensor_add(
                Ach[c], stg[:], B[:, c * T:(c + 1) * T]),
                [(s_, xdma_n[c]), (sS, woev_list[c])], vS)
            x1c.append(vS.n)
            P.step("vector", lambda e, c=c: e.tensor_mul(Csq(c), Ach[c], Ach[c]),
                   (), vS)
            sqm.append(vS.n)

        for o in range(NC):
            reg = psW[o % 2]
            basew = [(si, wld[("o", o)]), (vS, zdone_v), (si, zdone_i),
                     (sS, woev[o % 2])]
            if o < 2:
                basew += [(sS, exp_bank[0]), (sS, exp_bank[1])]
            for c in range(NC):
                for nt in range(4):
                    last = (c == NC - 1) and (nt == 3)
                    P.step("tensor", lambda e, o=o, c=c, nt=nt, reg=reg, last=last: e.matmul(
                        reg[:, nt * NT:(nt + 1) * NT], wbv[o % 2][:, c, :],
                        zb[:, c, nt * NT:(nt + 1) * NT], start=(c == 0), stop=last),
                        basew, tS if last else None)
            mmd = tS.n
            wo_last_mm[0] = mmd
            consume_wb(o % 2, tS, mmd)
            if o >= 2:
                emit_x1_add(o - 2)
            stgx = XS if o % 2 == 0 else YST
            sx_ = sx0 if o % 2 == 0 else sx1
            if o == 0:
                wvx = [(vS, chain_out[(NP - 1, 0, 1)])]
            elif o == 1:
                wvx = [(vS, chain_out[(NP - 1, 1, 1)])]
            else:
                wvx = [(vS, x1c[o - 2])]
            P.step("sync", lambda e, o=o, stgx=stgx: e.dma_start(
                out=stgx[:], in_=x_in[:, o, :]), wvx, sx_, 16)
            xdma_n[o] = sx_.n
            if o + 2 < NC:
                load_wblock(("o", o + 2), w_o[:, o + 2, :], o % 2)
            elif o + 2 == NC:
                load_wblock(("f2", "r", 0), w_r[:, 0, :], 0)
            elif o + 2 == NC + 1:
                load_wblock(("f2", "r", 1), w_r[:, 1, :], 1)
            P.step("scalar", lambda e, o=o, reg=reg: e.activation(
                B[:, o * T:(o + 1) * T], reg[:], AF.Identity, bias=boc[:, o:o + 1]),
                [(tS, mmd)], sS)
            woev[o % 2] = sS.n
            woev_list.append(sS.n)
        aod = sS.n
        emit_x1_add(6)
        emit_x1_add(7)
        dump("dbg_ao", B[:], [(sS, aod)])

        m2 = ln_stats_apply(Ach, Csq,
                            [[(vS, x1c[c])] for c in range(NC)],
                            row_reg=ps1, t1_reg=ps2, t2_reg=ps1,
                            row_extra=[(sS, woev_list[6])],
                            t1_extra=[(sS, aod)],
                            ebuf_guard=[(tS, pv_final)],
                            g_guard=[(vS, zdone_v), (si, zdone_i)],
                            sq_pre=sqm)
        ln2tiles = m2["tiles"]

        ln2f = D[:].rearrange("p (c t) -> p c t", c=NC)
        xf = C[:, 0:16384].rearrange("p (c t) -> p c t", c=NC)
        xkf = A[:].rearrange("p (c t) -> p c t", c=NC)
        kaf = D[:].rearrange("p (c t) -> p c t", c=NC)

        def r_ap(o):
            if o < 4:
                return Fb[:, o * T:(o + 1) * T]
            if o < 6:
                return Ebuf[:, (o - 4) * T:(o - 3) * T]
            return G[:, (o - 4) * T:(o - 3) * T]

        psW2 = [ps1, ps2]
        XSB = XS[:].bitcast(BF16)
        apl = []
        for c in range(NC):
            aw = [(vS, ln2tiles)]
            P.step("vector", lambda e, c=c: e.tensor_mul(
                XSB[:, 0:T], Ach[c], g(2)), aw, vS)
            P.step("vector", lambda e, c=c: e.tensor_sub(
                ln2f[:, c, :], XSB[:, 0:T], g(3)), (), vS)
            apl.append(vS.n)
        if "dbg_ln2" in dbg_t:
            dump("dbg_ln2", D[:, 0:8192], [(vS, vS.n)])
        mixd = {}

        def emit_mix(xv, mu, om, tag):
            for c in range(NC):
                P.step("vector", lambda e, c=c, mu=mu: e.tensor_scalar_mul(
                    VBT2[c], ln2f[:, c, :], mu[:, c:c + 1]), [(vS, apl[c])] + mixd.get(
                        ("free", tag), []), vS)
                P.step("vector", lambda e, c=c, xv=xv, om=om: e.scalar_tensor_tensor(
                    xv[:, c, 1:T], ln2f[:, c, 0:T - 1], om[:, c:c + 1],
                    VBT2[c][:, 1:T], ALU.mult, ALU.add), (), vS)
                P.step("vector", lambda e, c=c, xv=xv, om=om: e.tensor_scalar_mul(
                    xv[:, c, 0:1], VBT2[c][:, 0:1], 1.0), (), vS)
            mixd[tag] = vS.n

        VBT2 = [XSB[:, 0:T] for _ in range(NC)]

        def emit_mat(tag, wsrc, rhs_v, out_cb, first_extra=()):
            for o in range(NC):
                slot = o % 2
                if ("f2", tag, o) not in wld:
                    load_wblock(("f2", tag, o), wsrc[:, o, :], slot)
                reg = psW2[o % 2]
                basew = [(si, wld[("f2", tag, o)]), (vS, mixd[tag]),
                         matev[o % 2][0], matev[o % 2][1]]
                if o < 2:
                    basew += list(first_extra)
                for c in range(NC):
                    for nt in range(4):
                        last = (c == NC - 1) and (nt == 3)
                        P.step("tensor", lambda e, c=c, nt=nt, reg=reg, slot=slot,
                               rhs_v=rhs_v, last=last: e.matmul(
                                   reg[:, nt * NT:(nt + 1) * NT], wbv[slot][:, c, :],
                                   rhs_v[:, c, nt * NT:(nt + 1) * NT],
                                   start=(c == 0), stop=last),
                               basew, tS if last else None)
                mmn = tS.n
                consume_wb(slot, tS, mmn)
                out_cb(o, reg, mmn)

        matev = [[(vS, ln2tiles), (sS, sS.n)], [(vS, ln2tiles), (sS, sS.n)]]
        rsig_n = {}

        emit_mix(xf, murc, omrc, "r")
        emit_mix(xkf, mukc, omkc, "k")
        rmm_last = [0]

        def r_out(o, reg, mmn):
            P.step("scalar", lambda e, o=o, reg=reg: e.activation(
                r_ap(o), reg[:], AF.Sigmoid), [(tS, mmn)], sS)
            rsig_n[o] = sS.n
            matev[o % 2] = [(vS, vS.n), (sS, sS.n)]
            rmm_last[0] = mmn
        emit_mat("r", w_r, xf, r_out)
        if "dbg_r" in dbg_t:
            dump("dbg_r", Fb[:, 0:8192], [(sS, sS.n)])

        def k_out(o, reg, mmn):
            P.step("vector", lambda e, o=o, reg=reg: e.tensor_scalar_max(
                B2T2[o], reg[:], 0.0), [(tS, mmn)], vS)
            P.step("vector", lambda e, o=o: e.tensor_mul(
                kaf[:, o, :], B2T2[o], B2T2[o]), (), vS)
            matev[o % 2] = [(vS, vS.n), (sS, sS.n)]
        YSTB = YST[:].bitcast(BF16)
        B2T2 = [YSTB[:, 0:T] for _ in range(NC)]
        emit_mat("k", w_k2, xkf, k_out, first_extra=[(vS, mixd["k"])])
        if "dbg_ka" in dbg_t:
            dump("dbg_ka", D[:, 0:8192], [(vS, vS.n)])
        mixd["v"] = vS.n
        ydma_slot = [0, 0]
        xsfree = [vS.n]

        def v_out(o, reg, mmn):
            P.step("gpsimd", lambda e, o=o: e.dma_start(
                out=XS[:], in_=x_in[:, o, :]), [(vS, xsfree[0])], sb, 16)
            xld = sb.n
            P.step("vector", lambda e, o=o, reg=reg: e.scalar_tensor_tensor(
                YST[:, 0:1024], reg[:, 0:1024], 1.0, r_ap(o)[:, 0:1024],
                ALU.mult, ALU.mult),
                [(tS, mmn), (so, ydma_slot[0]), (sS, rsig_n[o])], vS)
            P.step("vector", lambda e, o=o: e.tensor_add(
                YST[:, 0:1024], YST[:, 0:1024], B[:, o * T:o * T + 1024]), (), vS)
            P.step("vector", lambda e, o=o: e.tensor_add(
                YST[:, 0:1024], YST[:, 0:1024], XS[:, 0:1024]), [(sb, xld)], vS)
            f1 = vS.n
            P.step("scalar", lambda e, o=o: e.dma_start(
                out=y_out[:, o, 0:1024], in_=YST[:, 0:1024]), [(vS, f1)], so, 16)
            P.step("vector", lambda e, o=o, reg=reg: e.scalar_tensor_tensor(
                YST[:, 1024:2048], reg[:, 1024:2048], 1.0, r_ap(o)[:, 1024:2048],
                ALU.mult, ALU.mult), (), vS)
            P.step("vector", lambda e, o=o: e.tensor_add(
                YST[:, 1024:2048], YST[:, 1024:2048],
                B[:, o * T + 1024:(o + 1) * T]), (), vS)
            P.step("vector", lambda e, o=o: e.tensor_add(
                YST[:, 1024:2048], YST[:, 1024:2048], XS[:, 1024:2048]), (), vS)
            f2_ = vS.n
            xsfree[0] = vS.n
            P.step("scalar", lambda e, o=o: e.dma_start(
                out=y_out[:, o, 1024:2048], in_=YST[:, 1024:2048]), [(vS, f2_)], so, 16)
            ydma_slot[0] = so.n
            matev[o % 2] = [(vS, vS.n), (sS, sS.n)]
        emit_mat("v", w_v2, kaf, v_out)
        P.step("sync", lambda e, n=so.n: e.wait_ge(so.h, n), (), None)

        @block.sync
        def _(e):
            for fn in P.ops["sync"]:
                fn(e)

        @block.gpsimd
        def _(e):
            for fn in P.ops["gpsimd"]:
                fn(e)

        @block.scalar
        def _(e):
            for fn in P.ops["scalar"]:
                fn(e)

        @block.vector
        def _(e):
            for fn in P.ops["vector"]:
                fn(e)

        @block.tensor
        def _(e):
            for fn in P.ops["tensor"]:
                fn(e)

    return nc


def _prep_inputs(L0_attn_x, L0_attn_Wqkv, L0_attn_bqkv, L0_attn_Wo, L0_attn_bo,
                 L0_attn_wq, L0_attn_wk, L0_ffn_W_r, L0_ffn_W_k, L0_ffn_W_v,
                 L0_ffn_mu_r, L0_ffn_mu_k):
    f32 = np.float32
    bf16 = ml_dtypes.bfloat16
    x = np.asarray(L0_attn_x, f32)

    def fm(w):
        wt = np.ascontiguousarray(np.asarray(w, f32).T)
        return np.ascontiguousarray(wt.reshape(8, 128, -1).transpose(1, 0, 2))

    def fm_slab(w):
        wt = np.asarray(w, f32).T
        O = wt.shape[1]
        a = wt.reshape(8, 128, O // 128, 128).transpose(1, 2, 0, 3)
        return np.ascontiguousarray(a.reshape(128, O // 128, 1024))

    wqkv_full = np.asarray(L0_attn_Wqkv, f32)
    wqkv_qk = fm_slab(wqkv_full[0:2048, :])
    wqkv_v = fm(wqkv_full[2048:3072, :])
    wo = fm_slab(L0_attn_Wo)
    wr = fm_slab(L0_ffn_W_r)
    wk2 = fm_slab(L0_ffn_W_k)
    wv2 = fm_slab(L0_ffn_W_v)
    bqkv = np.asarray(L0_attn_bqkv, f32)
    b_qkv = np.ascontiguousarray(bqkv.reshape(24, 128).T)
    b_vrow = np.ascontiguousarray(bqkv[2048:3072].reshape(1, 1024))
    b_o = np.ascontiguousarray(np.asarray(L0_attn_bo, f32).reshape(8, 128).T)
    mu_r = np.ascontiguousarray(np.asarray(L0_ffn_mu_r, f32).reshape(8, 128).T)
    mu_k = np.ascontiguousarray(np.asarray(L0_ffn_mu_k, f32).reshape(8, 128).T)
    wq_col = np.ascontiguousarray(np.tile(np.asarray(L0_attn_wq, f32), 2).reshape(128, 1))
    wk_col = np.ascontiguousarray(np.tile(np.asarray(L0_attn_wk, f32), 2).reshape(128, 1))
    cst_ln = np.zeros((128, 4), bf16)
    cst_ln[:, 0] = bf16(1.0 / 1024.0)
    cst_ln[:, 3] = bf16(1.0 / 1024.0)
    cst_ind = np.zeros((128, 4), bf16)
    cst_ind[0:64, 0] = bf16(1.0 / 64.0)
    cst_ind[64:128, 1] = bf16(1.0 / 64.0)
    cst_b2 = np.zeros((2, 128), bf16)
    cst_b2[0, :] = bf16(1.0)
    cst_b2b = np.zeros((2, 128), bf16)
    cst_b2b[1, :] = bf16(1.0)
    cst_i4r = np.zeros((4, 128), bf16)
    cst_i4r[0, 0:64] = bf16(1.0)
    cst_i4r[1, 64:128] = bf16(1.0)
    cst_i4n = np.zeros((4, 128), bf16)
    cst_i4n[2, 0:64] = bf16(1.0)
    cst_i4n[3, 64:128] = bf16(1.0)

    shared = dict(w_qkv_qk=wqkv_qk, w_qkv_v=wqkv_v, w_o=wo, w_r=wr, w_k2=wk2,
                  w_v2=wv2, b_qkv=b_qkv,
                  b_vrow=b_vrow, b_o=b_o, mu_r_c=mu_r, mu_k_c=mu_k,
                  wq_col=wq_col, wk_col=wk_col, cst_ln=cst_ln, cst_ind=cst_ind,
                  cst_b2=cst_b2, cst_b2b=cst_b2b, cst_i4r=cst_i4r, cst_i4n=cst_i4n)
    in_maps = []
    for b in range(8):
        xb = np.ascontiguousarray(x[b].T.reshape(8, 128, T).transpose(1, 0, 2))
        m = dict(shared)
        m["x_in"] = xb
        in_maps.append(m)
    return in_maps


def kernel(**inputs):
    in_maps = _prep_inputs(**inputs)
    nc = build_program()
    res = run_bass_kernel_spmd(nc, in_maps, core_ids=list(range(8)))
    outs = []
    for b in range(8):
        yb = res.results[b]["y"]
        outs.append(yb.transpose(1, 0, 2).reshape(1024, T).T)
    return np.ascontiguousarray(np.stack(outs, axis=0), dtype=np.float32)


# version 14
# speedup vs baseline: 1.2388x; 1.0085x over previous
import contextlib
import numpy as np
import ml_dtypes
import concourse.bass as bass
import concourse.mybir as mybir
from concourse.bass_utils import run_bass_kernel_spmd

F32 = mybir.dt.float32
BF16 = mybir.dt.bfloat16
AX = mybir.AxisListType.X
AF = mybir.ActivationFunctionType
ALU = mybir.AluOpType

T = 2048
NC = 8
NT = 512
NH = 16
HD = 64
NP = 8
KT = 16
LN_EPS = 1e-5
RMS_EPS = 1e-6
BIAS_SHIFT = 30.0


class Sem:
    def __init__(self, h, owner):
        self.h, self.owner, self.n = h, owner, 0


class Plan:
    ENGS = ("sync", "gpsimd", "scalar", "vector", "tensor")

    def __init__(self):
        self.ops = {e: [] for e in self.ENGS}
        self.last_wait = {e: {} for e in self.ENGS}

    def step(self, eng, emitfn, waits=(), inc=None, inc_amt=1):
        lw = self.last_wait[eng]
        ws = []
        for s, n in waits:
            if s.owner == eng or n <= 0:
                continue
            if lw.get(id(s), -1) >= n:
                continue
            assert n <= s.n, f"wait on future count {n} > {s.n} ({eng})"
            lw[id(s)] = n
            ws.append((s, n))
        if inc is not None:
            inc.n += inc_amt

        def f(e):
            for s, n in ws:
                e.wait_ge(s.h, n)
            ins = emitfn(e)
            if inc is not None:
                ins.then_inc(inc.h, inc_amt)

        self.ops[eng].append(f)


def build_program(dbg=frozenset()):
    nc = bass.Bass("TRN2", target_bir_lowering=False)

    x_in = nc.dram_tensor("x_in", [128, NC, T], F32, kind="ExternalInput").ap()
    w_qkv_qk = nc.dram_tensor("w_qkv_qk", [128, 16, 1024], F32, kind="ExternalInput").ap()
    w_qkv_v = nc.dram_tensor("w_qkv_v", [128, NC, 1024], F32, kind="ExternalInput").ap()
    w_o = nc.dram_tensor("w_o", [128, NC, 1024], F32, kind="ExternalInput").ap()
    w_r = nc.dram_tensor("w_r", [128, NC, 1024], F32, kind="ExternalInput").ap()
    w_k2 = nc.dram_tensor("w_k2", [128, NC, 1024], F32, kind="ExternalInput").ap()
    w_v2 = nc.dram_tensor("w_v2", [128, NC, 1024], F32, kind="ExternalInput").ap()
    b_qkv = nc.dram_tensor("b_qkv", [128, 24], F32, kind="ExternalInput").ap()
    b_vrow = nc.dram_tensor("b_vrow", [1, 1024], F32, kind="ExternalInput").ap()
    b_o = nc.dram_tensor("b_o", [128, 8], F32, kind="ExternalInput").ap()
    mu_r_c = nc.dram_tensor("mu_r_c", [128, 8], F32, kind="ExternalInput").ap()
    mu_k_c = nc.dram_tensor("mu_k_c", [128, 8], F32, kind="ExternalInput").ap()
    wq_col = nc.dram_tensor("wq_col", [128, 1], F32, kind="ExternalInput").ap()
    wk_col = nc.dram_tensor("wk_col", [128, 1], F32, kind="ExternalInput").ap()
    cst_ln = nc.dram_tensor("cst_ln", [128, 4], BF16, kind="ExternalInput").ap()
    cst_b2 = nc.dram_tensor("cst_b2", [2, 128], BF16, kind="ExternalInput").ap()
    cst_b2b = nc.dram_tensor("cst_b2b", [2, 128], BF16, kind="ExternalInput").ap()
    cst_i4r = nc.dram_tensor("cst_i4r", [4, 128], BF16, kind="ExternalInput").ap()
    cst_i4n = nc.dram_tensor("cst_i4n", [4, 128], BF16, kind="ExternalInput").ap()
    cst_ind = nc.dram_tensor("cst_ind", [128, 4], BF16, kind="ExternalInput").ap()

    y_out = nc.dram_tensor("y", [128, NC, T], F32, kind="ExternalOutput").ap()
    dbg_t = {}
    for name, shape, dt in [
        ("dbg_ln1", [128, NC * T], BF16),
        ("dbg_q", [128, T], BF16),
        ("dbg_sclq", [128, T], BF16),
        ("dbg_pt", [128, T], BF16),
        ("dbg_zraw", [65, T], BF16),
        ("dbg_zbuf", [128, NC * T], BF16),
        ("dbg_ao", [128, NC * T], BF16),
        ("dbg_vp", [128, NH * 65], BF16),
        ("dbg_ln2", [128, NC * 1024], BF16),
        ("dbg_xr", [128, NC * 1024], BF16),
        ("dbg_r", [128, NC * 1024], BF16),
        ("dbg_ka", [128, NC * 1024], BF16),
        ("dbg_bcol", [128, 6], F32),
        ("dbg_qr", [128, T], BF16),
        ("dbg_rows", [128, 2 * T], BF16),
        ("dbg_lhq", [128, 4], BF16),
    ]:
        if name in dbg:
            dbg_t[name] = nc.dram_tensor(name, shape, dt, kind="ExternalOutput").ap()

    P = Plan()
    with contextlib.ExitStack() as st:
        E = st.enter_context
        A = E(nc.sbuf_tensor("A", [128, NC * T], BF16))
        B = E(nc.sbuf_tensor("B", [128, NC * T], BF16))
        C = E(nc.sbuf_tensor("C", [128, KT * NH * 65], BF16))
        D = E(nc.sbuf_tensor("D", [128, NC * T], BF16))
        Ebuf = E(nc.sbuf_tensor("Ebuf", [128, 2 * T], BF16))
        Fb = E(nc.sbuf_tensor("Fb", [128, 4 * T], BF16))
        G = E(nc.sbuf_tensor("G", [128, 4 * T], BF16))
        ROWS = E(nc.sbuf_tensor("ROWS", [128, 2 * T], BF16))
        XS = E(nc.sbuf_tensor("XS", [128, T], F32))
        YST = E(nc.sbuf_tensor("YST", [128, T], F32))
        WS = E(nc.sbuf_tensor("WS", [128, 2 * 1024], F32))
        WB = E(nc.sbuf_tensor("WB", [128, 2 * 1024], BF16))
        bqc = E(nc.sbuf_tensor("bqc", [128, 24], F32))
        boc = E(nc.sbuf_tensor("boc", [128, 8], F32))
        murc = E(nc.sbuf_tensor("murc", [128, 8], F32))
        mukc = E(nc.sbuf_tensor("mukc", [128, 8], F32))
        omrc = E(nc.sbuf_tensor("omrc", [128, 8], F32))
        omkc = E(nc.sbuf_tensor("omkc", [128, 8], F32))
        wqc = E(nc.sbuf_tensor("wqc", [128, 1], F32))
        wkc = E(nc.sbuf_tensor("wkc", [128, 1], F32))
        lnT = E(nc.sbuf_tensor("lnT", [128, 4], BF16))
        lhq = E(nc.sbuf_tensor("lhq", [128, 4], BF16))
        lhk = E(nc.sbuf_tensor("lhk", [128, 4], BF16))
        b2 = E(nc.sbuf_tensor("b2", [2, 128], BF16))
        b2b = E(nc.sbuf_tensor("b2b", [2, 128], BF16))
        i4r = E(nc.sbuf_tensor("i4r", [4, 128], BF16))
        i4n = E(nc.sbuf_tensor("i4n", [4, 128], BF16))
        CCOL = E(nc.sbuf_tensor("CCOL", [128, 8], F32))
        bcole = E(nc.sbuf_tensor("bcole", [128, 2], F32))
        bcolo = E(nc.sbuf_tensor("bcolo", [128, 2], F32))
        bndc = E(nc.sbuf_tensor("bndc", [128, 8], BF16))
        epsc = E(nc.sbuf_tensor("epsc", [128, 2], F32))
        ps1 = E(nc.psum_tensor("ps1", [128, T], F32))
        ps2 = E(nc.psum_tensor("ps2", [128, T], F32))
        sw = Sem(E(nc.semaphore("sw")), "sync")
        sw0 = Sem(E(nc.semaphore("sw0")), "sync")
        sw1 = Sem(E(nc.semaphore("sw1")), "sync")
        sx0 = Sem(E(nc.semaphore("sx0")), "sync")
        sx1 = Sem(E(nc.semaphore("sx1")), "sync")
        si = Sem(E(nc.semaphore("si")), "gpsimd")
        sb = Sem(E(nc.semaphore("sb")), "gpsimd")
        so = Sem(E(nc.semaphore("so")), "sync")
        tS = Sem(E(nc.semaphore("tS")), "tensor")
        vS = Sem(E(nc.semaphore("vS")), "vector")
        sS = Sem(E(nc.semaphore("sS")), "scalar")
        block = E(nc.Block())

        Vv = C[:].rearrange("p (t h e) -> p t h e", t=KT, h=NH, e=65)
        zb = D[:].rearrange("p (c t) -> p c t", c=NC)
        wbv = [WB[:, i * 1024:(i + 1) * 1024].rearrange("p (c o) -> p c o", c=NC)
               for i in range(2)]
        Ach = [A[:, c * T:(c + 1) * T] for c in range(NC)]
        f = lambda i: Fb[:, i * T:(i + 1) * T]
        g = lambda i: G[:, i * T:(i + 1) * T]
        e0 = Ebuf[:, 0:T]
        e1 = Ebuf[:, T:2 * T]
        ptb = lambda p: Ebuf[:, p * T:(p + 1) * T]
        ptE = lambda k: Ebuf[:, (k % 4) * NT:(k % 4 + 1) * NT]
        ptO = lambda k: Ebuf[:, T + (k % 4) * NT:T + (k % 4 + 1) * NT]
        VBT = D[:, 8192:9216]

        def dump(name, ap, waits):
            if name in dbg_t:
                tgt = dbg_t[name]
                P.step("sync", lambda e, a=ap, t=tgt: e.dma_start(out=t, in_=a),
                       waits, so, 16)

        ws_cast = [0, 0]
        wb_read = [(tS, 0), (tS, 0)]
        wld = {}

        def load_wblock(key, src, slot):
            s_ = sw0 if slot == 0 else sw1
            P.step("sync", lambda e, src=src, slot=slot: e.dma_start(
                out=WS[:, slot * 1024:(slot + 1) * 1024], in_=src),
                [(si, ws_cast[slot])], s_, 16)
            P.step("gpsimd", lambda e, slot=slot: e.tensor_copy(
                WB[:, slot * 1024:(slot + 1) * 1024], WS[:, slot * 1024:(slot + 1) * 1024]),
                [(s_, s_.n), wb_read[slot]], si, 1)
            wld[key] = si.n
            ws_cast[slot] = si.n

        def consume_wb(slot, sem, count):
            wb_read[slot] = (sem, count)

        for dst, src in [(bqc, b_qkv), (boc, b_o), (murc, mu_r_c), (mukc, mu_k_c),
                         (wqc, wq_col), (wkc, wk_col), (lnT, cst_ln)]:
            P.step("sync", lambda e, d=dst, s=src: e.dma_start(out=d[:], in_=s),
                   (), sw, 16)
        P.step("sync", lambda e: e.dma_start(out=lhq[:], in_=cst_ind), (), sw, 16)
        P.step("sync", lambda e: e.dma_start(out=lhk[:], in_=cst_ind), (), sw, 16)
        P.step("sync", lambda e: e.dma_start(out=b2[:], in_=cst_b2), (), sw, 16)
        P.step("sync", lambda e: e.dma_start(out=b2b[:], in_=cst_b2b), (), sw, 16)
        P.step("sync", lambda e: e.dma_start(out=i4r[:], in_=cst_i4r), (), sw, 16)
        P.step("sync", lambda e: e.dma_start(out=i4n[:], in_=cst_i4n), (), sw, 16)
        cw = sw.n
        P.step("scalar", lambda e: e.activation(lhq[0:64, 2:3], wqc[0:64, :], AF.Square),
               [(sw, cw)], sS)
        P.step("scalar", lambda e: e.activation(lhq[64:128, 3:4], wqc[64:128, :], AF.Square),
               (), sS)
        P.step("scalar", lambda e: e.activation(lhk[0:64, 2:3], wkc[0:64, :], AF.Square),
               (), sS)
        P.step("scalar", lambda e: e.activation(lhk[64:128, 3:4], wkc[64:128, :], AF.Square),
               (), sS)
        sq2 = sS.n
        P.step("vector", lambda e: e.tensor_scalar(omrc[:], murc[:], -1.0, 1.0,
                                                   ALU.mult, ALU.add), [(sw, cw)], vS)
        P.step("vector", lambda e: e.tensor_scalar(omkc[:], mukc[:], -1.0, 1.0,
                                                   ALU.mult, ALU.add), [(sw, cw)], vS)
        P.step("vector", lambda e: e.memset(Vv[:, :, :, 64:65], 1.0), (), vS)
        P.step("vector", lambda e: e.memset(bndc[:], 0.0), (), vS)
        P.step("vector", lambda e: e.memset(epsc[:, 0:1], LN_EPS), (), vS)
        P.step("vector", lambda e: e.memset(epsc[:, 1:2], RMS_EPS), (), vS)
        cdone = vS.n

        def ln_stats_apply(xch, sq_dst, inb, row_reg=None, t1_reg=None, t2_reg=None,
                           row_extra=(), t1_extra=(), ebuf_guard=(), g_guard=(),
                           sq_pre=None):
            row_reg = ps2 if row_reg is None else row_reg
            t1_reg = ps1 if t1_reg is None else t1_reg
            t2_reg = ps2 if t2_reg is None else t2_reg
            if sq_pre is None:
                for c in range(NC):
                    P.step("vector",
                           lambda e, c=c: e.tensor_mul(sq_dst(c), xch[c], xch[c]),
                           list(inb[c]), vS)
                sq_mark = [vS.n] * NC
            else:
                sq_mark = sq_pre
            for c in range(NC):
                for nt in range(4):
                    P.step("tensor",
                           lambda e, c=c, nt=nt: e.matmul(
                               row_reg[0:2, nt * NT:(nt + 1) * NT], lnT[:, 0:2],
                               xch[c][:, nt * NT:(nt + 1) * NT],
                               start=(c == 0), stop=False),
                           list(inb[c]) + [(sw, cw)] + list(row_extra), None)
            for c in range(NC):
                for nt in range(4):
                    last = (c == NC - 1) and (nt == 3)
                    P.step("tensor",
                           lambda e, c=c, nt=nt, last=last: e.matmul(
                               row_reg[0:2, nt * NT:(nt + 1) * NT], lnT[:, 2:4],
                               sq_dst(c)[:, nt * NT:(nt + 1) * NT],
                               start=False, stop=last),
                           [(vS, sq_mark[c])], tS if last else None)
            mmd = tS.n
            P.step("vector", lambda e: e.tensor_copy(ROWS[0:2, 0:T], row_reg[0:2, :]),
                   [(tS, mmd)], vS)
            rowd = vS.n
            for nt in range(4):
                P.step("tensor", lambda e, nt=nt: e.matmul(
                    t1_reg[:, nt * NT:(nt + 1) * NT], b2[:],
                    ROWS[0:2, nt * NT:(nt + 1) * NT], start=True, stop=True),
                    [(vS, rowd)] + list(t1_extra), tS if nt == 3 else None)
            mtd = tS.n
            for nt in range(4):
                P.step("tensor", lambda e, nt=nt: e.matmul(
                    t2_reg[:, nt * NT:(nt + 1) * NT], b2b[:],
                    ROWS[0:2, nt * NT:(nt + 1) * NT], start=True, stop=True),
                    [(vS, rowd)], tS if nt == 3 else None)
            qtd = tS.n
            P.step("scalar", lambda e: e.activation(e0, t1_reg[:], AF.Square),
                   [(tS, mtd)] + list(ebuf_guard), sS)
            sqd2 = sS.n
            P.step("vector", lambda e: e.tensor_sub(e1, t2_reg[:], e0),
                   [(tS, qtd), (sS, sqd2)] + list(ebuf_guard), vS)
            vd = vS.n
            P.step("scalar", lambda e: e.activation(e0, e1, AF.Ln, bias=epsc[:, 0:1]),
                   [(vS, vd)], sS)
            P.step("scalar", lambda e: e.activation(g(2), e0, AF.Exp, scale=-0.5),
                   list(g_guard), sS)
            sd = sS.n
            P.step("vector", lambda e: e.tensor_mul(g(3), t1_reg[:], g(2)),
                   [(sS, sd)] + list(g_guard), vS)
            return {"rows": rowd, "tiles": vS.n, "ps_used": tS.n}

        xcast = []
        for c in range(NC):
            stg = XS if c % 2 == 0 else YST
            s_ = sx0 if c % 2 == 0 else sx1
            wv = [(vS, xcast[c - 2])] if c > 1 else [(vS, cdone)]
            P.step("sync", lambda e, c=c, stg=stg: e.dma_start(out=stg[:], in_=x_in[:, c, :]),
                   wv, s_, 16)
            P.step("vector", lambda e, c=c, stg=stg: e.tensor_copy(Ach[c], stg[:]),
                   [(s_, s_.n)], vS)
            xcast.append(vS.n)
        Bsq = lambda c: B[:, c * T:(c + 1) * T]
        m1 = ln_stats_apply(Ach, Bsq, [[(vS, xcast[c])] for c in range(NC)])
        apl1 = []
        for c in range(NC):
            P.step("vector", lambda e, c=c: e.tensor_mul(Bsq(c), Ach[c], g(2)),
                   [(vS, m1["tiles"])], vS)
            P.step("vector", lambda e, c=c: e.tensor_sub(Ach[c], Bsq(c), g(3)), (), vS)
            apl1.append(vS.n)
        ln1d = vS.n
        dump("dbg_ln1", A[:], [(vS, ln1d)])

        qsq = B[:, 2 * T:3 * T]
        ksq = B[:, 3 * T:4 * T]
        zr = lambda p: B[0:65, p * T:(p + 1) * T]
        zrl = lambda p: B[64:65, p * T:(p + 1) * T]
        LF = B[:, 4 * T:8 * T].bitcast(F32)
        sE = lambda k: ps1[:, (k % 2) * NT:(k % 2 + 1) * NT]
        sO = lambda k: ps1[:, 1024 + (k % 2) * NT:1024 + (k % 2 + 1) * NT]
        zE = ps2[0:65, 0:NT]
        zO = ps2[0:65, NT:2 * NT]
        FA = ps2[:, 1024:2048]
        FAh = [ps2[:, 1024:1536], ps2[:, 1536:2048]]

        last_zevac = vS.n
        last_div = {0: vS.n, 1: vS.n}
        pv_slot = {}
        exp_bank = {0: sS.n, 1: sS.n}
        sub_free = [(vS, 0), (vS, 0)]
        fa_free = (vS, 0)
        bcol_n = {}
        recs_n = {}
        lin_n = [0]
        appd_n = {}
        rows_n = {}
        chain_out = {}

        def _sub(idx, sem, cnt):
            sub_free[idx] = (sem, cnt)

        sq_n = {}

        def pk_qkv_group(p, side, nt):
            sub = nt % 2
            reg = FAh[sub]
            wkey = ("q", p) if side == 0 else ("k", p)
            fraw = f(0) if side == 0 else f(1)
            sq = qsq if side == 0 else ksq
            bias = bqc[:, p:p + 1] if side == 0 else bqc[:, 8 + p:9 + p]

            def emit():
                aw = [(vS, ln1d)] if p == 0 else []
                for c in range(NC):
                    last = c == NC - 1
                    P.step("tensor", lambda e, c=c, last=last: e.matmul(
                        reg, wbv[side][:, c, :],
                        A[:, c * T + nt * NT: c * T + (nt + 1) * NT],
                        start=(c == 0), stop=last),
                        [(si, wld[wkey]), sub_free[sub]] + aw, tS if last else None)
                mmn = tS.n
                if side == 1 and nt == 3:
                    consume_wb(1, tS, mmn)
                if side == 0 and nt == 3:
                    consume_wb(0, tS, mmn)
                P.step("vector", lambda e: e.tensor_scalar_add(
                    fraw[:, nt * NT:(nt + 1) * NT], reg, bias), [(tS, mmn)], vS)
                P.step("vector", lambda e: e.tensor_mul(
                    sq[:, nt * NT:(nt + 1) * NT], fraw[:, nt * NT:(nt + 1) * NT],
                    fraw[:, nt * NT:(nt + 1) * NT]), (), vS)
                _sub(sub, vS, vS.n)
                sq_n[(side, nt)] = vS.n
            return emit

        def pk_stats(p, side):
            sq = qsq if side == 0 else ksq
            lh = lhq if side == 0 else lhk
            off = 0 if side == 0 else T

            def emit():
                bw = [(sb, vbtd)] if p == 0 else []
                for nt in range(4):
                    sub = nt % 2
                    P.step("tensor", lambda e, nt=nt, sub=sub: e.matmul(
                        FAh[sub][0:4, :], lh[:],
                        sq[:, nt * NT:(nt + 1) * NT], start=True, stop=True),
                        [sub_free[sub], (vS, sq_n[(side, nt)])], tS)
                    mmn = tS.n
                    P.step("vector", lambda e, nt=nt, sub=sub: e.tensor_copy(
                        ROWS[0:4, off + nt * NT: off + (nt + 1) * NT], FAh[sub][0:4, :]),
                        [(tS, mmn)] + bw, vS)
                    _sub(sub, vS, vS.n)
                rows_n[(p, side)] = vS.n
            return emit

        def pk_chain_half(p, side, hh):
            off = 0 if side == 0 else T
            sclg = g(0) if side == 0 else g(1)
            nrm = g(2) if side == 0 else g(3)
            wcol = wqc if side == 0 else wkc
            stg = XS if side == 0 else YST

            def emit():
                for i2 in range(2):
                    P.step("tensor", lambda e, i2=i2: e.matmul(
                        FAh[i2], i4r[:],
                        ROWS[0:4, off + hh * 1024 + i2 * NT: off + hh * 1024 + (i2 + 1) * NT],
                        start=True, stop=True),
                        [sub_free[i2], (vS, rows_n[(p, side)])], tS if i2 else None)
                mtn = tS.n
                P.step("scalar", lambda e: e.activation(stg[:, 0:1024], FA, AF.Ln,
                                                        bias=epsc[:, 1:2]), [(tS, mtn)], sS)
                P.step("scalar", lambda e: e.activation(stg[:, 1024:2048], stg[:, 0:1024],
                                                        AF.Exp, scale=-0.5), (), sS)
                lnn = sS.n
                for i2 in range(2):
                    P.step("tensor", lambda e, i2=i2: e.matmul(
                        FAh[i2], i4n[:],
                        ROWS[0:4, off + hh * 1024 + i2 * NT: off + hh * 1024 + (i2 + 1) * NT],
                        start=True, stop=True),
                        [(sS, lnn)], tS if i2 else None)
                ntn = tS.n
                P.step("vector", lambda e: e.tensor_mul(
                    sclg[:, hh * 1024:(hh + 1) * 1024], stg[:, 1024:2048], FA),
                    [(tS, ntn)], vS)
                gw = [(si, si.n)] if side == 1 else []
                P.step("vector", lambda e: e.tensor_mul(
                    nrm[:, hh * 1024:(hh + 1) * 1024],
                    sclg[:, hh * 1024:(hh + 1) * 1024], stg[:, 1024:2048]), gw, vS)
                P.step("vector", lambda e: e.tensor_scalar_mul(
                    sclg[:, hh * 1024:(hh + 1) * 1024], stg[:, 1024:2048], wcol[:, 0:1]),
                    (), vS)
                _sub(0, vS, vS.n)
                _sub(1, vS, vS.n)
                chain_out[(p, side, hh)] = vS.n
            return emit

        def pk_bias(p):
            par = p % 2

            def emit():
                w0 = [(vS, chain_out[(p, 0, 0)]), (vS, chain_out[(p, 0, 1)]),
                      (vS, chain_out[(p, 1, 0)]), (vS, chain_out[(p, 1, 1)])]
                P.step("vector", lambda e: e.reduce_max(CCOL[:, 0:1], g(2), axis=AX),
                       w0, vS)
                P.step("vector", lambda e: e.reduce_max(CCOL[:, 1:2], g(3), axis=AX),
                       (), vS)
                vmax = vS.n
                P.step("scalar", lambda e: e.activation(CCOL[:, 4:6], CCOL[:, 0:2], AF.Ln),
                       [(vS, vmax)], sS)
                lnn = sS.n
                P.step("vector", lambda e: e.reduce_sum(CCOL[:, 6:7], CCOL[:, 4:6],
                                                        axis=AX), [(sS, lnn)], vS)
                vsum = vS.n
                P.step("scalar", lambda e: e.activation(CCOL[:, 2:3], CCOL[:, 6:7],
                                                        AF.Exp, scale=0.5), [(vS, vsum)], sS)
                expn = sS.n
                pb = bcol_n.get(p - 2, (0, 0))[0]
                P.step("vector", lambda e: e.tensor_scalar(CCOL[:, 3:4], CCOL[:, 2:3],
                                                           -1.0, BIAS_SHIFT,
                                                           ALU.mult, ALU.add),
                       [(sS, expn), (sb, pb)], vS)
                biasd = vS.n
                P.step("sync", lambda e, par=par: e.dma_start(
                    out=bcole[:, par:par + 1],
                    in_=CCOL[0:1, 3:4].unsqueeze(1).to_broadcast([1, 128, 1])),
                    [(vS, biasd)], sb, 16)
                P.step("sync", lambda e, par=par: e.dma_start(
                    out=bcolo[:, par:par + 1],
                    in_=CCOL[64:65, 3:4].unsqueeze(1).to_broadcast([1, 128, 1])),
                    [(vS, biasd)], sb, 16)
                bcol_n[p] = (sb.n, par)
                if p == 0 and "dbg_bcol" in dbg_t:
                    dump("dbg_bcol", CCOL[:, 0:6], [(vS, biasd)])
            return emit

        def pk_divides(jp):
            def emit():
                P.step("vector", lambda e: e.memset(G[64:65, 2 * T:2 * T + 64], 1.0),
                       [(vS, recs_n[jp])], vS)
                onesd = vS.n
                for h2 in range(2):
                    for hh in range(2):
                        for i2 in range(2):
                            P.step("tensor", lambda e, hh=hh, i2=i2, h2=h2: e.matmul(
                                FAh[i2][0:64, :], G[64:65, 2 * T:2 * T + 64],
                                ROWS[64:65, h2 * T + hh * 1024 + i2 * NT:
                                     h2 * T + hh * 1024 + (i2 + 1) * NT],
                                start=True, stop=True),
                                [(vS, onesd), sub_free[i2]], tS if i2 else None)
                        linn = tS.n
                        lin_n[0] = linn
                        if h2 == 0:
                            P.step("vector", lambda e, jp=jp, hh=hh: e.tensor_mul(
                                zb[0:64, jp, hh * 1024:(hh + 1) * 1024],
                                zr(0)[0:64, hh * 1024:(hh + 1) * 1024], FA[0:64, :]),
                                [(tS, linn)], vS)
                            last_div[0] = vS.n
                        else:
                            P.step("vector", lambda e, jp=jp, hh=hh: e.tensor_mul(
                                g(3)[0:64, hh * 1024:(hh + 1) * 1024],
                                zr(1)[0:64, hh * 1024:(hh + 1) * 1024], FA[0:64, :]),
                                [(tS, linn), (si, si.n)], vS)
                            last_div[1] = vS.n
                        _sub(0, vS, vS.n)
                        _sub(1, vS, vS.n)
                P.step("gpsimd", lambda e, jp=jp: e.dma_start(
                    out=zb[64:128, jp, :], in_=g(3)[0:64, :]),
                    [(vS, last_div[1])], si, 16)
            return emit

        def emit_attention(j, packets):
            nonlocal last_zevac
            pki = [0]

            def pop(n=1):
                for _ in range(n):
                    if pki[0] < len(packets):
                        packets[pki[0]]()
                        pki[0] += 1

            par = bcol_n[j][1]
            bcE = bcole[:, par:par + 1]
            bcO = bcolo[:, par:par + 1]
            hE, hO = 2 * j, 2 * j + 1
            zfree = {0: (vS, last_zevac), 1: (vS, last_zevac)}
            expn = {}
            smm_kt = {}
            slast_n = 0
            defer_recips = (j + 1 < NP)
            deferred = []
            for qq in range(4):
                for kt in range(KT):
                    swl = [(sS, exp_bank[kt % 2]), (vS, appd_n[j]),
                           (sb, bcol_n[j][0])]
                    P.step("tensor", lambda e, kt=kt, qq=qq: e.matmul(
                        sE(kt), f(3)[0:64, kt * 128:(kt + 1) * 128],
                        f(2)[0:64, qq * NT:(qq + 1) * NT],
                        start=True, stop=True, tile_position=(0, 0)), swl, None)
                    P.step("tensor", lambda e, kt=kt, qq=qq: e.matmul(
                        sO(kt), f(3)[64:128, kt * 128:(kt + 1) * 128],
                        f(2)[64:128, qq * NT:(qq + 1) * NT],
                        start=True, stop=True, tile_position=(64, 0)), (), tS)
                    smm_kt[kt] = tS.n
                    slast_n = tS.n
                    ew = [(tS, smm_kt[kt])]
                    if (0, kt % 4) in pv_slot:
                        ew.append((tS, pv_slot[(0, kt % 4)]))
                    P.step("scalar", lambda e, kt=kt, bcE=bcE: e.activation(
                        ptE(kt), sE(kt), AF.Exp, bias=bcE), ew, sS)
                    ew2 = []
                    if (1, kt % 4) in pv_slot:
                        ew2.append((tS, pv_slot[(1, kt % 4)]))
                    P.step("scalar", lambda e, kt=kt, bcO=bcO: e.activation(
                        ptO(kt), sO(kt), AF.Exp, bias=bcO), ew2, sS)
                    exp_bank[kt % 2] = sS.n
                    expn[kt] = sS.n
                    if kt > 1:
                        kp = kt - 2
                        pwE = [(sS, expn[kp])]
                        if kp == 0:
                            pwE.append(zfree[0])
                            pwE.append(zfree[1])
                        P.step("tensor", lambda e, kp=kp, qq=qq, hE=hE: e.matmul(
                            zE, Vv[:, kp, hE, :], ptE(kp),
                            start=(kp == 0), stop=False), pwE, None)
                        P.step("tensor", lambda e, kp=kp, qq=qq, hO=hO: e.matmul(
                            zO, Vv[:, kp, hO, :], ptO(kp),
                            start=(kp == 0), stop=False), (), tS)
                        pv_slot[(0, kp % 4)] = tS.n
                        pv_slot[(1, kp % 4)] = tS.n
                    if kt >= 5 and kt % 2 == 1:
                        pop()
                for kp in (KT - 2, KT - 1):
                    P.step("tensor", lambda e, kp=kp, qq=qq, hE=hE: e.matmul(
                        zE, Vv[:, kp, hE, :], ptE(kp),
                        start=False, stop=(kp == KT - 1)), [(sS, expn[kp])], None)
                    P.step("tensor", lambda e, kp=kp, qq=qq, hO=hO: e.matmul(
                        zO, Vv[:, kp, hO, :], ptO(kp),
                        start=False, stop=(kp == KT - 1)), (), tS)
                    pv_slot[(0, kp % 4)] = tS.n
                    pv_slot[(1, kp % 4)] = tS.n
                pvall = tS.n
                P.step("vector", lambda e, qq=qq: e.tensor_copy(
                    zr(0)[:, qq * NT:(qq + 1) * NT], zE),
                    [(tS, pvall), (vS, last_div[0])], vS)
                zfree[0] = (vS, vS.n)
                P.step("vector", lambda e, qq=qq: e.tensor_copy(
                    zr(1)[:, qq * NT:(qq + 1) * NT], zO),
                    [(vS, last_div[1])], vS)
                zfree[1] = (vS, vS.n)
                last_zevac = vS.n

                P.step("vector", lambda e, qq=qq: e.tensor_copy(
                    LF[64:65, qq * NT:(qq + 1) * NT], ps2[64:65, 0:NT]), (), vS)
                P.step("vector", lambda e, qq=qq: e.tensor_copy(
                    LF[64:65, 2048 + qq * NT: 2048 + (qq + 1) * NT],
                    ps2[64:65, NT:2 * NT]), (), vS)
                zfree[0] = (vS, vS.n)
                zfree[1] = (vS, vS.n)
                for part, off in ((0, qq * NT), (1, 2048 + qq * NT)):
                    def _rc(e, off=off, part=part, qq=qq):
                        with nc.allow_low_precision(reason="softmax denom to bf16"):
                            return e.reciprocal(
                                ROWS[64:65, part * T + qq * NT: part * T + (qq + 1) * NT],
                                LF[64:65, off:off + NT])
                    rw = [(tS, lin_n[0])] if (qq == 0 and part == 0) else ()
                    if qq == 3 and defer_recips:
                        deferred.append(lambda _rc=_rc, rw=rw: P.step("vector", _rc, rw, vS))
                    else:
                        P.step("vector", _rc, rw, vS)
            if j == 0:
                dump("dbg_zraw", zr(0), [(vS, last_zevac)])
            if not defer_recips:
                recs_n[j] = vS.n
            pop(len(packets))
            return slast_n, deferred

        P.step("sync", lambda e: e.dma_start(out=XS[0:1, 0:1024], in_=b_vrow),
               [(vS, xcast[6])], sw, 16)
        P.step("vector", lambda e: e.tensor_copy(ROWS[0:1, 0:1024], XS[0:1, 0:1024]),
               [(sw, sw.n), (tS, m1["ps_used"])], vS)
        P.step("gpsimd", lambda e: e.dma_start(
            out=VBT[:], in_=ROWS[0:1, 0:1024].unsqueeze(1).to_broadcast([1, 128, 1024])),
            [(vS, vS.n)], sb, 16)
        vbtd = sb.n
        wvd = []
        for c in range(NC):
            load_wblock(("wv", c), w_qkv_v[:, c, :], c % 2)
            P.step("scalar", lambda e, c=c: e.activation(
                D[:, c * 1024:(c + 1) * 1024],
                WB[:, (c % 2) * 1024:((c % 2) + 1) * 1024], AF.Copy),
                [(si, wld[("wv", c)])], sS)
            wvd.append(sS.n)
            consume_wb(c % 2, sS, sS.n)
        load_wblock(("q", 0), w_qkv_qk[:, 0, :], 0)
        load_wblock(("k", 0), w_qkv_qk[:, 8, :], 1)
        psV = [ps1[:, 0:1024], ps1[:, 1024:2048]]
        vevac = [0, 0]
        pre_pk = [pk_qkv_group(0, 0, nt) for nt in range(4)] + [pk_stats(0, 0)] \
            + [pk_qkv_group(0, 1, nt) for nt in range(4)] + [pk_stats(0, 1)] \
            + [pk_chain_half(0, s_, h_) for s_ in range(2) for h_ in range(2)]
        ppi = [0]
        for tt in range(KT):
            reg = psV[tt % 2]
            for c in range(NC):
                for nt in range(2):
                    last = (c == NC - 1) and (nt == 1)
                    P.step("tensor", lambda e, tt=tt, c=c, nt=nt, reg=reg, last=last: e.matmul(
                        reg[:, nt * NT:(nt + 1) * NT],
                        A[:, c * T + tt * 128: c * T + (tt + 1) * 128],
                        D[:, c * 1024 + nt * NT: c * 1024 + (nt + 1) * NT],
                        start=(c == 0), stop=last),
                        [(sS, wvd[c]), (vS, vevac[tt % 2]), (vS, apl1[c])],
                        tS if last else None)
            mmd = tS.n
            P.step("vector", lambda e, tt=tt, reg=reg: e.tensor_add(
                Vv[:, tt, :, 0:64],
                reg[:].rearrange("p (h e) -> p h e", h=NH),
                VBT[:].rearrange("p (h e) -> p h e", h=NH)),
                [(tS, mmd), (sb, vbtd)], vS)
            vevac[tt % 2] = vS.n
            if tt >= 1 and ppi[0] < len(pre_pk):
                pre_pk[ppi[0]]()
                ppi[0] += 1
        while ppi[0] < len(pre_pk):
            pre_pk[ppi[0]]()
            ppi[0] += 1
        dump("dbg_vp", Vv[:, 0, :, :], [(vS, vS.n)])


        load_wblock(("q", 1), w_qkv_qk[:, 1, :], 0)
        load_wblock(("k", 1), w_qkv_qk[:, 9, :], 1)
        pk_bias(0)()
        if "dbg_qr" in dbg_t:
            dump("dbg_qr", f(0), [(vS, vS.n)])
            dump("dbg_lhq", lhq[:], [(vS, cdone)])
            dump("dbg_rows", ROWS[:], [(vS, rows_n[(0, 1)])])
        P.step("vector", lambda e: e.tensor_mul(f(2), f(0), g(0)),
               [(vS, chain_out[(0, 0, 1)]), (vS, chain_out[(0, 1, 1)])], vS)
        P.step("vector", lambda e: e.tensor_mul(f(3), f(1), g(1)), (), vS)
        appd_n[0] = vS.n
        if "dbg_q" in dbg_t:
            dump("dbg_q", f(2), [(vS, appd_n[0])])
            dump("dbg_sclq", g(0), [(vS, appd_n[0])])

        for j in range(NP):
            packets = []
            if j > 0:
                packets.append(pk_divides(j - 1))
            if j + 1 < NP:
                p = j + 1
                for side in range(2):
                    for nt in range(4):
                        packets.append(pk_qkv_group(p, side, nt))
                    packets.append(pk_stats(p, side))
                for side in range(2):
                    for hh in range(2):
                        packets.append(pk_chain_half(p, side, hh))
                packets.append(pk_bias(p))
            slast, deferred = emit_attention(j, packets)

            if j + 1 < NP:
                if j + 2 < NP:
                    load_wblock(("q", j + 2), w_qkv_qk[:, j + 2, :], 0)
                    load_wblock(("k", j + 2), w_qkv_qk[:, 10 + j, :], 1)
                else:
                    load_wblock(("o", 0), w_o[:, 0, :], 0)
                    load_wblock(("o", 1), w_o[:, 1, :], 1)
                P.step("vector", lambda e: e.tensor_mul(f(2), f(0), g(0)),
                       [(vS, chain_out[(j + 1, 0, 1)]), (vS, chain_out[(j + 1, 1, 1)]),
                        (tS, slast)], vS)
                P.step("vector", lambda e: e.tensor_mul(f(3), f(1), g(1)), (), vS)
                appd_n[j + 1] = vS.n
                for dr in deferred:
                    dr()
                recs_n[j] = vS.n

        pk_divides(NP - 1)()
        zdone_v = vS.n
        zdone_i = si.n
        pv_final = tS.n
        dump("dbg_zbuf", D[:], [(vS, zdone_v), (si, zdone_i)])

        psW = [ps1, ps2]
        woev = [sS.n, sS.n]
        woev_list = []
        wo_last_mm = [0]
        xdma_n = {}
        x1c = []
        sqm = []
        Csq = lambda c: C[:, c * T:(c + 1) * T]

        def emit_x1_add(c):
            stg = XS if c % 2 == 0 else YST
            s_ = sx0 if c % 2 == 0 else sx1
            P.step("vector", lambda e, c=c, stg=stg: e.tensor_add(
                Ach[c], stg[:], B[:, c * T:(c + 1) * T]),
                [(s_, xdma_n[c]), (sS, woev_list[c])], vS)
            x1c.append(vS.n)
            P.step("vector", lambda e, c=c: e.tensor_mul(Csq(c), Ach[c], Ach[c]),
                   (), vS)
            sqm.append(vS.n)

        for o in range(NC):
            reg = psW[o % 2]
            basew = [(si, wld[("o", o)]), (vS, zdone_v), (si, zdone_i),
                     (sS, woev[o % 2])]
            if o < 2:
                basew += [(sS, exp_bank[0]), (sS, exp_bank[1])]
            for c in range(NC):
                for nt in range(4):
                    last = (c == NC - 1) and (nt == 3)
                    P.step("tensor", lambda e, o=o, c=c, nt=nt, reg=reg, last=last: e.matmul(
                        reg[:, nt * NT:(nt + 1) * NT], wbv[o % 2][:, c, :],
                        zb[:, c, nt * NT:(nt + 1) * NT], start=(c == 0), stop=last),
                        basew, tS if last else None)
            mmd = tS.n
            wo_last_mm[0] = mmd
            consume_wb(o % 2, tS, mmd)
            if o >= 2:
                emit_x1_add(o - 2)
            stgx = XS if o % 2 == 0 else YST
            sx_ = sx0 if o % 2 == 0 else sx1
            if o == 0:
                wvx = [(vS, chain_out[(NP - 1, 0, 1)])]
            elif o == 1:
                wvx = [(vS, chain_out[(NP - 1, 1, 1)])]
            else:
                wvx = [(vS, x1c[o - 2])]
            P.step("sync", lambda e, o=o, stgx=stgx: e.dma_start(
                out=stgx[:], in_=x_in[:, o, :]), wvx, sx_, 16)
            xdma_n[o] = sx_.n
            if o + 2 < NC:
                load_wblock(("o", o + 2), w_o[:, o + 2, :], o % 2)
            elif o + 2 == NC:
                load_wblock(("f2", "r", 0), w_r[:, 0, :], 0)
            elif o + 2 == NC + 1:
                load_wblock(("f2", "r", 1), w_r[:, 1, :], 1)
            P.step("scalar", lambda e, o=o, reg=reg: e.activation(
                B[:, o * T:(o + 1) * T], reg[:], AF.Identity, bias=boc[:, o:o + 1]),
                [(tS, mmd)], sS)
            woev[o % 2] = sS.n
            woev_list.append(sS.n)
        aod = sS.n
        emit_x1_add(6)
        emit_x1_add(7)
        dump("dbg_ao", B[:], [(sS, aod)])

        m2 = ln_stats_apply(Ach, Csq,
                            [[(vS, x1c[c])] for c in range(NC)],
                            row_reg=ps1, t1_reg=ps2, t2_reg=ps1,
                            row_extra=[(sS, woev_list[6])],
                            t1_extra=[(sS, aod)],
                            ebuf_guard=[(tS, pv_final)],
                            g_guard=[(vS, zdone_v), (si, zdone_i)],
                            sq_pre=sqm)
        ln2tiles = m2["tiles"]

        ln2f = D[:].rearrange("p (c t) -> p c t", c=NC)
        xf = C[:, 0:16384].rearrange("p (c t) -> p c t", c=NC)
        xkf = A[:].rearrange("p (c t) -> p c t", c=NC)
        kaf = D[:].rearrange("p (c t) -> p c t", c=NC)

        def r_ap(o):
            if o < 4:
                return Fb[:, o * T:(o + 1) * T]
            if o < 6:
                return Ebuf[:, (o - 4) * T:(o - 3) * T]
            return G[:, (o - 4) * T:(o - 3) * T]

        psW2 = [ps1, ps2]
        XSB = XS[:].bitcast(BF16)
        apl = []
        for c in range(NC):
            aw = [(vS, ln2tiles)]
            P.step("vector", lambda e, c=c: e.tensor_mul(
                XSB[:, 0:T], Ach[c], g(2)), aw, vS)
            P.step("vector", lambda e, c=c: e.tensor_sub(
                ln2f[:, c, :], XSB[:, 0:T], g(3)), (), vS)
            apl.append(vS.n)
        if "dbg_ln2" in dbg_t:
            dump("dbg_ln2", D[:, 0:8192], [(vS, vS.n)])
        mixd = {}

        def emit_mix(xv, mu, om, tag):
            for c in range(NC):
                P.step("vector", lambda e, c=c, mu=mu: e.tensor_scalar_mul(
                    VBT2[c], ln2f[:, c, :], mu[:, c:c + 1]), [(vS, apl[c])] + mixd.get(
                        ("free", tag), []), vS)
                P.step("vector", lambda e, c=c, xv=xv, om=om: e.scalar_tensor_tensor(
                    xv[:, c, 1:T], ln2f[:, c, 0:T - 1], om[:, c:c + 1],
                    VBT2[c][:, 1:T], ALU.mult, ALU.add), (), vS)
                P.step("vector", lambda e, c=c, xv=xv, om=om: e.tensor_scalar_mul(
                    xv[:, c, 0:1], VBT2[c][:, 0:1], 1.0), (), vS)
            mixd[tag] = vS.n

        VBT2 = [XSB[:, 0:T] for _ in range(NC)]

        def emit_mat(tag, wsrc, rhs_v, out_cb, first_extra=()):
            for o in range(NC):
                slot = o % 2
                if ("f2", tag, o) not in wld:
                    load_wblock(("f2", tag, o), wsrc[:, o, :], slot)
                reg = psW2[o % 2]
                basew = [(si, wld[("f2", tag, o)]), (vS, mixd[tag]),
                         matev[o % 2][0], matev[o % 2][1]]
                if o < 2:
                    basew += list(first_extra)
                for c in range(NC):
                    for nt in range(4):
                        last = (c == NC - 1) and (nt == 3)
                        P.step("tensor", lambda e, c=c, nt=nt, reg=reg, slot=slot,
                               rhs_v=rhs_v, last=last: e.matmul(
                                   reg[:, nt * NT:(nt + 1) * NT], wbv[slot][:, c, :],
                                   rhs_v[:, c, nt * NT:(nt + 1) * NT],
                                   start=(c == 0), stop=last),
                               basew, tS if last else None)
                mmn = tS.n
                consume_wb(slot, tS, mmn)
                out_cb(o, reg, mmn)

        matev = [[(vS, ln2tiles), (sS, sS.n)], [(vS, ln2tiles), (sS, sS.n)]]
        rsig_n = {}

        emit_mix(xf, murc, omrc, "r")
        emit_mix(xkf, mukc, omkc, "k")
        rmm_last = [0]

        def r_out(o, reg, mmn):
            P.step("scalar", lambda e, o=o, reg=reg: e.activation(
                r_ap(o), reg[:], AF.Sigmoid), [(tS, mmn)], sS)
            rsig_n[o] = sS.n
            matev[o % 2] = [(vS, vS.n), (sS, sS.n)]
            rmm_last[0] = mmn
        emit_mat("r", w_r, xf, r_out)
        if "dbg_r" in dbg_t:
            dump("dbg_r", Fb[:, 0:8192], [(sS, sS.n)])

        def k_out(o, reg, mmn):
            P.step("vector", lambda e, o=o, reg=reg: e.tensor_scalar_max(
                B2T2[o], reg[:], 0.0), [(tS, mmn)], vS)
            P.step("vector", lambda e, o=o: e.tensor_mul(
                kaf[:, o, :], B2T2[o], B2T2[o]), (), vS)
            matev[o % 2] = [(vS, vS.n), (sS, sS.n)]
        YSTB = YST[:].bitcast(BF16)
        B2T2 = [YSTB[:, 0:T] for _ in range(NC)]
        emit_mat("k", w_k2, xkf, k_out, first_extra=[(vS, mixd["k"])])
        if "dbg_ka" in dbg_t:
            dump("dbg_ka", D[:, 0:8192], [(vS, vS.n)])
        mixd["v"] = vS.n
        ydma_slot = [0, 0]
        xsfree = [vS.n]

        def v_out(o, reg, mmn):
            P.step("gpsimd", lambda e, o=o: e.dma_start(
                out=XS[:], in_=x_in[:, o, :]), [(vS, xsfree[0])], sb, 16)
            xld = sb.n
            P.step("vector", lambda e, o=o, reg=reg: e.scalar_tensor_tensor(
                YST[:, 0:1024], reg[:, 0:1024], 1.0, r_ap(o)[:, 0:1024],
                ALU.mult, ALU.mult),
                [(tS, mmn), (so, ydma_slot[0]), (sS, rsig_n[o])], vS)
            P.step("vector", lambda e, o=o: e.tensor_add(
                YST[:, 0:1024], YST[:, 0:1024], B[:, o * T:o * T + 1024]), (), vS)
            P.step("vector", lambda e, o=o: e.tensor_add(
                YST[:, 0:1024], YST[:, 0:1024], XS[:, 0:1024]), [(sb, xld)], vS)
            f1 = vS.n
            P.step("scalar", lambda e, o=o: e.dma_start(
                out=y_out[:, o, 0:1024], in_=YST[:, 0:1024]), [(vS, f1)], so, 16)
            P.step("vector", lambda e, o=o, reg=reg: e.scalar_tensor_tensor(
                YST[:, 1024:2048], reg[:, 1024:2048], 1.0, r_ap(o)[:, 1024:2048],
                ALU.mult, ALU.mult), (), vS)
            P.step("vector", lambda e, o=o: e.tensor_add(
                YST[:, 1024:2048], YST[:, 1024:2048],
                B[:, o * T + 1024:(o + 1) * T]), (), vS)
            P.step("vector", lambda e, o=o: e.tensor_add(
                YST[:, 1024:2048], YST[:, 1024:2048], XS[:, 1024:2048]), (), vS)
            f2_ = vS.n
            xsfree[0] = vS.n
            P.step("scalar", lambda e, o=o: e.dma_start(
                out=y_out[:, o, 1024:2048], in_=YST[:, 1024:2048]), [(vS, f2_)], so, 16)
            ydma_slot[0] = so.n
            matev[o % 2] = [(vS, vS.n), (sS, sS.n)]
        emit_mat("v", w_v2, kaf, v_out)
        P.step("sync", lambda e, n=so.n: e.wait_ge(so.h, n), (), None)

        @block.sync
        def _(e):
            for fn in P.ops["sync"]:
                fn(e)

        @block.gpsimd
        def _(e):
            for fn in P.ops["gpsimd"]:
                fn(e)

        @block.scalar
        def _(e):
            for fn in P.ops["scalar"]:
                fn(e)

        @block.vector
        def _(e):
            for fn in P.ops["vector"]:
                fn(e)

        @block.tensor
        def _(e):
            for fn in P.ops["tensor"]:
                fn(e)

    return nc


def _prep_inputs(L0_attn_x, L0_attn_Wqkv, L0_attn_bqkv, L0_attn_Wo, L0_attn_bo,
                 L0_attn_wq, L0_attn_wk, L0_ffn_W_r, L0_ffn_W_k, L0_ffn_W_v,
                 L0_ffn_mu_r, L0_ffn_mu_k):
    f32 = np.float32
    bf16 = ml_dtypes.bfloat16
    x = np.asarray(L0_attn_x, f32)

    def fm(w):
        wt = np.ascontiguousarray(np.asarray(w, f32).T)
        return np.ascontiguousarray(wt.reshape(8, 128, -1).transpose(1, 0, 2))

    def fm_slab(w):
        wt = np.asarray(w, f32).T
        O = wt.shape[1]
        a = wt.reshape(8, 128, O // 128, 128).transpose(1, 2, 0, 3)
        return np.ascontiguousarray(a.reshape(128, O // 128, 1024))

    wqkv_full = np.asarray(L0_attn_Wqkv, f32)
    wqkv_qk = fm_slab(wqkv_full[0:2048, :])
    wqkv_v = fm(wqkv_full[2048:3072, :])
    wo = fm_slab(L0_attn_Wo)
    wr = fm_slab(L0_ffn_W_r)
    wk2 = fm_slab(L0_ffn_W_k)
    wv2 = fm_slab(L0_ffn_W_v)
    bqkv = np.asarray(L0_attn_bqkv, f32)
    b_qkv = np.ascontiguousarray(bqkv.reshape(24, 128).T)
    b_vrow = np.ascontiguousarray(bqkv[2048:3072].reshape(1, 1024))
    b_o = np.ascontiguousarray(np.asarray(L0_attn_bo, f32).reshape(8, 128).T)
    mu_r = np.ascontiguousarray(np.asarray(L0_ffn_mu_r, f32).reshape(8, 128).T)
    mu_k = np.ascontiguousarray(np.asarray(L0_ffn_mu_k, f32).reshape(8, 128).T)
    wq_col = np.ascontiguousarray(np.tile(np.asarray(L0_attn_wq, f32), 2).reshape(128, 1))
    wk_col = np.ascontiguousarray(np.tile(np.asarray(L0_attn_wk, f32), 2).reshape(128, 1))
    cst_ln = np.zeros((128, 4), bf16)
    cst_ln[:, 0] = bf16(1.0 / 1024.0)
    cst_ln[:, 3] = bf16(1.0 / 1024.0)
    cst_ind = np.zeros((128, 4), bf16)
    cst_ind[0:64, 0] = bf16(1.0 / 64.0)
    cst_ind[64:128, 1] = bf16(1.0 / 64.0)
    cst_b2 = np.zeros((2, 128), bf16)
    cst_b2[0, :] = bf16(1.0)
    cst_b2b = np.zeros((2, 128), bf16)
    cst_b2b[1, :] = bf16(1.0)
    cst_i4r = np.zeros((4, 128), bf16)
    cst_i4r[0, 0:64] = bf16(1.0)
    cst_i4r[1, 64:128] = bf16(1.0)
    cst_i4n = np.zeros((4, 128), bf16)
    cst_i4n[2, 0:64] = bf16(1.0)
    cst_i4n[3, 64:128] = bf16(1.0)

    shared = dict(w_qkv_qk=wqkv_qk, w_qkv_v=wqkv_v, w_o=wo, w_r=wr, w_k2=wk2,
                  w_v2=wv2, b_qkv=b_qkv,
                  b_vrow=b_vrow, b_o=b_o, mu_r_c=mu_r, mu_k_c=mu_k,
                  wq_col=wq_col, wk_col=wk_col, cst_ln=cst_ln, cst_ind=cst_ind,
                  cst_b2=cst_b2, cst_b2b=cst_b2b, cst_i4r=cst_i4r, cst_i4n=cst_i4n)
    in_maps = []
    for b in range(8):
        xb = np.ascontiguousarray(x[b].T.reshape(8, 128, T).transpose(1, 0, 2))
        m = dict(shared)
        m["x_in"] = xb
        in_maps.append(m)
    return in_maps


def kernel(**inputs):
    in_maps = _prep_inputs(**inputs)
    nc = build_program()
    res = run_bass_kernel_spmd(nc, in_maps, core_ids=list(range(8)))
    outs = []
    for b in range(8):
        yb = res.results[b]["y"]
        outs.append(yb.transpose(1, 0, 2).reshape(1024, T).T)
    return np.ascontiguousarray(np.stack(outs, axis=0), dtype=np.float32)
